# version 01
# baseline (speedup 1.0000x reference)
import math
import types
import numpy as np
import concourse.bass as bass
import concourse.mybir as mybir
from concourse.bass_utils import run_bass_kernel_spmd

F32 = mybir.dt.float32
BF16 = mybir.dt.bfloat16
U32 = mybir.dt.uint32
U8 = mybir.dt.uint8
AF = mybir.ActivationFunctionType
OP = mybir.AluOpType
AX = mybir.AxisListType

B, N, HID = 4, 2048, 2048
H, HKV, D = 16, 8, 128
NT = 16
SCALE = 1.0 / math.sqrt(D)
EPS = 1e-5

DEBUG_TILE_BARRIER = False
FULL_SERIAL = False
SER_MARKED = True
STATIC_PIECE = False
DELTA_CHUNKS = 16
DELTA_J = 8
DELTA_UPTO = 8

SBUF_BASE = 16512
SBUF_LIMIT = 212832


class Ctr:
    __slots__ = ("h", "n", "by")

    def __init__(self, h, by=1):
        self.h, self.n, self.by = h, 0, by


class Buf:
    __slots__ = ("name", "w", "rs")

    def __init__(self, name):
        self.name, self.w, self.rs = name, [], []


class Stream:
    def __init__(self, name, ctr, self_waits=False):
        self.name, self.ctr = name, ctr
        self.self_waits = self_waits
        self.items = []
        self.waited = {}
        self.sink = self.items

    dma_ctr = None

    def raw(self, fn, waits=(), inc=None):
        ws = []
        for t in waits:
            if t is None:
                continue
            c, v = t
            if c is self.ctr and not self.self_waits:
                continue
            if self.waited.get(id(c), 0) >= v:
                continue
            self.waited[id(c)] = v
            ws.append((c, v))
        tok = None
        if inc is not None:
            inc.n += inc.by
            tok = (inc, inc.n)
        self.sink.append((ws, fn, inc, inc.by if inc else 0))
        return tok


def _merge_toks(toks):
    best = {}
    for c, v in toks:
        if best.get(id(c), (None, -1))[1] < v:
            best[id(c)] = (c, v)
    return list(best.values())


class Prog:
    def __init__(self, nc):
        self.nc = nc
        self.ctrs = {}
        for nm in ("t", "v", "s", "sy", "gp", "gq"):
            by = 16 if nm in ("sy", "gq") else 1
            self.ctrs[nm] = Ctr(nc.alloc_semaphore(name=f"c_{nm}"), by)
        self.st = {nm: Stream(nm, self.ctrs[nm], self_waits=nm in ("v", "s"))
                   for nm in ("t", "v", "s", "sy", "gp")}
        self.st["gp"].dma_ctr = self.ctrs["gq"]
        self.nopfns = {}
        self.bufs = {}
        self.pid_ap = None

    def buf(self, name):
        b = self.bufs.get(name)
        if b is None:
            b = self.bufs[name] = Buf(name)
        return b

    def op(self, snm, fn, reads=(), writes=(), extra=(), ser=False, dma=False):
        if (FULL_SERIAL is True or (isinstance(FULL_SERIAL, str) and snm in FULL_SERIAL)
                or (ser and SER_MARKED)):
            self.barrier()
        s = self.st[snm]
        waits = list(extra) + list(self.buf("__barrier__").w)
        rb = [self.buf(x) for x in reads]
        wb = [self.buf(x) for x in writes]
        for b in rb:
            waits += b.w
        for b in wb:
            waits += b.rs
            waits += b.w
        ctr = s.dma_ctr if (dma and s.dma_ctr is not None) else s.ctr
        tok = s.raw(fn, waits=waits, inc=ctr)
        for b in rb:
            b.rs.append(tok)
        for b in wb:
            b.w = [tok]
            b.rs = []
        return tok

    def barrier(self):
        toks = [(c, c.n) for c in self.ctrs.values() if c.n > 0]
        for b in self.bufs.values():
            b.w = _merge_toks(b.w + toks)
            b.rs = []
        bb = self.buf("__barrier__")
        bb.w = _merge_toks(toks)

    def bwait(self):
        return list(self.buf("__barrier__").w)

    def begin_branch(self):
        self._snap_ctr = {k: c.n for k, c in self.ctrs.items()}
        self._snap_buf = {k: (list(b.w), list(b.rs)) for k, b in self.bufs.items()}
        self._snap_wait = {k: dict(s.waited) for k, s in self.st.items()}
        self._sideA = {k: [] for k in self.st}
        self._sideB = {k: [] for k in self.st}
        for k, s in self.st.items():
            s.sink = self._sideA[k]

    def mid_branch(self):
        self._a_ctr = {k: c.n for k, c in self.ctrs.items()}
        self._a_buf = {k: (list(b.w), list(b.rs)) for k, b in self.bufs.items()}
        for k, c in self.ctrs.items():
            c.n = self._snap_ctr[k]
        for k, b in self.bufs.items():
            w, r = self._snap_buf.get(k, ([], []))
            b.w, b.rs = list(w), list(r)
        for k, s in self.st.items():
            s.waited = dict(self._snap_wait[k])
            s.sink = self._sideB[k]

    def end_branch(self):
        for k in self.st:
            fn = self.nopfns[k]
            for side in (self._sideA[k], self._sideB[k]):
                if not side:
                    side.append(((), fn, None, 0))
        b_ctr = {k: c.n for k, c in self.ctrs.items()}
        b_buf = {k: (list(b.w), list(b.rs)) for k, b in self.bufs.items()}
        for k, c in self.ctrs.items():
            hi = max(self._a_ctr[k], b_ctr[k])
            if hi - self._a_ctr[k]:
                self._sideA[k].append(("pad", c, hi - self._a_ctr[k]))
            if hi - b_ctr[k]:
                self._sideB[k].append(("pad", c, hi - b_ctr[k]))
            c.n = hi
        for nm in set(self._a_buf) | set(b_buf):
            wa, ra = self._a_buf.get(nm, ([], []))
            wb, rb = b_buf.get(nm, ([], []))
            bb = self.buf(nm)
            bb.w = _merge_toks(wa + wb)
            bb.rs = _merge_toks(ra + rb)
        for k, s in self.st.items():
            s.waited = dict(self._snap_wait[k])
            s.sink = s.items
            s.items.append(("br", self._sideA[k], self._sideB[k]))


def _ensure_regs(K, eng, st):
    if st["preg"] is None:
        preg = eng.alloc_register(f"parity_{st['nm']}")
        roff = eng.alloc_register(f"roff_{st['nm']}")
        for c, v in st["pidw"]:
            eng.wait_ge(c.h, v)
        eng.reg_load(preg, K.pidsb[0:1, 0:1])
        eng.reg_mod(preg, preg, 2)
        st["preg"], st["roff"] = preg, roff


def _emit_items(K, eng, items, st):
    for it in items:
        if it and it[0] == "pad":
            eng.sem_inc(it[1].h, it[2])
        elif it and it[0] == "br":
            _ensure_regs(K, eng, st)
            with eng.If_eq(st["preg"], 0):
                _emit_items(K, eng, it[1], st)
            with eng.Else():
                _emit_items(K, eng, it[2], st)
        else:
            ws, fn, ctr, by = it
            for c, v in ws:
                eng.wait_ge(c.h, v)
            if getattr(fn, "_needs_regs", False):
                _ensure_regs(K, eng, st)
                inst = fn(eng, st["preg"], st["roff"])
            else:
                inst = fn(eng)
            if ctr is not None:
                inst.then_inc(ctr.h, by)


def _emit_engine(K, eng, nm):
    P = K.P
    items = P.st[nm].items
    st = {"preg": None, "roff": None, "nm": nm,
          "pidw": getattr(P, "pidw_new", list(P.bufs["pidsb"].w))}
    _emit_items(K, eng, items, st)


def make_K(debug=()):
    nc = bass.Bass("TRN2", target_bir_lowering=False)
    K = types.SimpleNamespace(nc=nc, dbg=set(debug))
    K.P = Prog(nc)

    K.x_d = nc.dram_tensor("x", [N, HID], F32, kind="ExternalInput").ap()
    K.xown_d = nc.dram_tensor("xown", [1024, HID], F32, kind="ExternalInput").ap()
    K.wq_d = nc.dram_tensor("wq", [HID, 2048], F32, kind="ExternalInput").ap()
    K.wk_d = nc.dram_tensor("wk", [HID, 1024], F32, kind="ExternalInput").ap()
    K.wv_d = nc.dram_tensor("wv", [HID, 1024], F32, kind="ExternalInput").ap()
    K.wo_d = nc.dram_tensor("wo", [2048, HID], F32, kind="ExternalInput").ap()
    K.wg_d = nc.dram_tensor("wg", [D, 1], F32, kind="ExternalInput").ap()
    K.bg_d = nc.dram_tensor("bg", [8, 1], F32, kind="ExternalInput").ap()
    K.gw_d = nc.dram_tensor("gatew", [HID, 8], F32, kind="ExternalInput").ap()
    K.gb_d = nc.dram_tensor("gateb", [1, 8], F32, kind="ExternalInput").ap()
    K.ew_d = nc.dram_tensor("expw", [8, HID, 2048], F32, kind="ExternalInput").ap()
    K.eb_d = nc.dram_tensor("expb", [8, 2048], F32, kind="ExternalInput").ap()
    K.fmask2_d = nc.dram_tensor("fmask2", [128, 4096], BF16, kind="ExternalInput").ap()
    K.pmask_d = nc.dram_tensor("pmask", [128, 256], U8, kind="ExternalInput").ap()
    K.out_d = nc.dram_tensor("out", [1024, HID], F32, kind="ExternalOutput").ap()
    def scratch(name, shape):
        kind = "ExternalOutput" if name in K.dbg else "Internal"
        return nc.dram_tensor(name, shape, BF16, kind=kind).ap()

    K.xfm_h = nc.dram_tensor("xfm_h", [NT, 128, HID], BF16,
                             kind="ExternalOutput").ap()
    K.kfm_h = scratch("kfm_h", [128, HKV * N])
    K.qfm_h = scratch("qfm_h", [128, H * 1024])

    def at(name, shape, dtype, off):
        assert off % 32 == 0, (name, off)
        per = int(np.prod(shape[1:])) * mybir.dt.size(dtype)
        assert off + per <= SBUF_LIMIT, (name, off, per)
        return nc.alloc_sbuf_tensor_at(name, shape, dtype, offset=SBUF_BASE + off).ap()

    o = [0]

    def nx(sz):
        r = o[0]
        o[0] += (sz + 31) // 32 * 32
        return r

    K.id_b = at("id_b", [128, 128], BF16, nx(256))
    K.id_f = at("id_f", [128, 128], F32, nx(512))
    K.ones1_b = at("ones1_b", [1, 128], BF16, nx(256))
    K.ones1_f = at("ones1_f", [1, 128], F32, nx(512))
    K.ones128_b = at("ones128_b", [128, 1], BF16, nx(32))
    K.mstr = at("mstr", [128, 128], F32, nx(512))
    K.mustr = at("mustr", [128, 128], F32, nx(512))
    K.muincs = at("muincs", [128, 128], F32, nx(512))
    K.cumT = at("cumT", [128, 128], F32, nx(512))
    K.lall = at("lall", [128, 128], F32, nx(512))
    K.slall = at("slall", [128, 128], F32, nx(512))
    K.lcb = at("lcb", [128, 128], F32, nx(512))
    K.ccb = at("ccb", [128, 128], F32, nx(512))
    K.sstate = at("sstate", [128, 8 * 128], BF16, nx(2048))
    K.pmasksb = at("pmasksb", [128, 256], U8, nx(256))
    K.cum = at("cum", [8, N], F32, nx(8192))
    K.pidsb = at("pidsb", [1, 1], U32, nx(32))
    K.bgsb = at("bgsb", [8, 1], F32, nx(32))
    K.wgsb = at("wgsb", [128, 1], BF16, nx(32))
    K.ln_a = at("ln_a", [128, 1], F32, nx(32))
    K.ln_b = at("ln_b", [128, 1], F32, nx(32))
    K.ln_c = at("ln_c", [128, 1], F32, nx(32))
    K.ln_d = at("ln_d", [128, 1], F32, nx(32))
    K.ln_e = at("ln_e", [128, 1], F32, nx(32))
    K.dn4 = at("dn4", [128, 8], F32, nx(64))
    K.dn4r = at("dn4r", [128, 8], F32, nx(64))
    K.gt1 = at("gt1", [128, 8], F32, nx(64))
    K.gt2 = at("gt2", [128, 8], F32, nx(64))
    K.gm1 = at("gm1", [128, 1], F32, nx(32))
    K.gm2 = at("gm2", [128, 1], F32, nx(32))
    K.cpool = at("cpool", [128, 2], F32, nx(64))
    K.nopsc = at("nopsc", [1, 8], F32, nx(32))
    nc.const_aps.register(F32, EPS, K.cpool[:, 0:1])
    nc.const_aps.register(F32, 1.0, K.cpool[:, 1:2])

    PEND = (o[0] + 1023) // 1024 * 1024
    R1 = PEND
    R2 = R1 + 32768
    R3 = R2 + 32768
    R4 = R3 + 32768
    R5 = R4 + 32768
    R6 = R5 + 32768
    assert R6 + 22528 <= SBUF_LIMIT, (PEND, R6)
    K.R6_avail = SBUF_LIMIT - R6

    K.ks_tm = at("ks_tm", [128, NT * 1024], BF16, R1)
    K.qfm_bk = at("qfm_bk", [128, H * 1024], BF16, R1)
    K.x1a = at("x1a", [128, 4 * 2048], F32, R1)
    K.v_tm = at("v_tm", [128, NT * 1024], BF16, R2)
    K.xh2_fm = at("xh2_fm", [128, 16 * 1024], BF16, R2)
    K.woslab = [at("woslab0", [128, 16 * 512], BF16, R2),
                at("woslab1", [128, 16 * 512], BF16, R2 + 16384)]
    K.xcb = at("xcb", [128, 2048], BF16, R3)
    K.expst = at("expst", [128, 512], BF16, R3 + 4096)
    K.kfm_st = at("kfm_st", [128, 1024], BF16, R3 + 5120)
    K.t2b = at("t2b", [8, N], F32, R3 + 7168)
    K.gsb = at("gsb", [8, N], F32, R3 + 15360)
    K.fla = at("fla", [128, 8 * 2048], BF16, R3)
    K.weslab = [at("weslab0", [128, 16 * 512], BF16, R3),
                at("weslab1", [128, 16 * 512], BF16, R3 + 16384)]
    K.wkv = at("wkv", [128, 16 * 2048], BF16, R4)
    K.qs_fm = at("qs_fm", [128, H * 1024], BF16, R4)
    K.base_fm = at("base_fm", [128, H * 1024], BF16, R4)
    K.xpiece = at("xpiece", [128, 16 * 512], BF16, R5)
    K.wqslab = [at("wqslab0", [128, 16 * 512], BF16, R5 + 16384),
                at("wqslab1", [128, 16 * 512], BF16, R6)]
    K.kfm_bk = at("kfm_bk", [128, HKV * N], BF16, R5)
    K.x1b = at("x1b", [128, 4 * 2048], F32, R5)
    ds = [R5]

    def dn(sz):
        r = ds[0]
        ds[0] += (sz + 31) // 32 * 32
        return r

    K.ksfm_c = at("ksfm_c", [128, 128], BF16, dn(256))
    K.a_sb = at("a_sb", [128, 128], BF16, dn(256))
    K.at_sb = at("at_sb", [128, 128], BF16, dn(256))
    K.ap_sb = [at("ap_sb0", [128, 128], BF16, dn(256)),
               at("ap_sb1", [128, 128], BF16, dn(256))]
    K.apt_sb = [at("apt_sb0", [128, 128], BF16, dn(256)),
                at("apt_sb1", [128, 128], BF16, dn(256))]
    K.z_sb = [at("z_sb0", [128, 128], BF16, dn(256)),
              at("z_sb1", [128, 128], BF16, dn(256))]
    K.lk_sb = at("lk_sb", [128, 128], BF16, dn(256))
    K.kp_sb = at("kp_sb", [128, 128], BF16, dn(256))
    K.nmt_sb = at("nmt_sb", [128, 128], BF16, dn(256))
    K.uc_sb = at("uc_sb", [128, 128], BF16, dn(256))
    K.oa_sb = at("oa_sb", [128, 128], BF16, dn(256))
    K.bt_sb = at("bt_sb", [128, 128], BF16, dn(256))
    K.e1 = at("e1", [128, 128], F32, dn(512))
    K.e2 = at("e2", [128, 128], F32, dn(512))
    K.e1m = at("e1m", [128, 128], F32, dn(512))
    K.e2m = at("e2m", [128, 128], F32, dn(512))
    K.ebt = at("ebt", [128, 128], F32, dn(512))
    K.dn1 = at("dn1", [128, 128], F32, dn(512))
    K.kdcol = at("kdcol", [128, 1], F32, dn(32))
    K.kpcol = at("kpcol", [128, 1], F32, dn(32))
    K.cumrow = at("cumrow", [1, 1024], F32, dn(4096))
    K.dtmp = at("dtmp", [128, 128], BF16, dn(256))
    assert ds[0] <= R5 + 32768

    K.xt = [at("xt0", [128, 2048], F32, R6),
            at("xt1", [128, 2048], F32, R6 + 8192)]
    K.sqsc = at("sqsc", [128, 2048], F32, R6 + 16384)
    K.kraw_st = at("kraw_st", [128, 1024], F32, R6 + 24576)
    K.gstage = at("gstage", [1, 1024], F32, R6 + 24576)
    K.t1b = at("t1b", [8, N], F32, R6)
    K.rmask = at("rmask", [8, N], F32, R6 + 8192)
    K.pt_all = at("pt_all", [128, 16 * 512], BF16, R6)
    K.rb_sb = at("rb_sb", [128, 512], BF16, R6 + 16384)
    K.rl_sb = at("rl_sb", [1, 512], BF16, R6 + 17408)
    K.fmask2sb = at("fmask2sb", [128, 4096], BF16, R6 + 18432)
    K.pt_tmp = at("pt_tmp", [128, 512], BF16, R6 + 26624)
    K.ln2sc = at("ln2sc", [128, 2048], F32, R6)
    K.berow = [at("berow0", [1, 2048], BF16, R6 + 8192),
               at("berow1", [1, 2048], BF16, R6 + 12288)]
    K.wcols = at("wcols", [128, 64], F32, R6 + 16384)
    K.gatewsb = at("gatewsb", [128, 128], BF16, R6 + 16672)
    K.gatebsb = at("gatebsb", [1, 8], BF16, R6 + 16960)
    K.ln2st = at("ln2st", [128, 2048], BF16, R6 + 20480)
    K.ffnu = at("ffnu", [128, 512], BF16, R6 + 24576)
    K.qse = at("qse", [128, 512], BF16, R6 + 17408)
    K.qse2 = at("qse2", [128, 512], BF16, R6 + 18432)
    K.rde = at("rde", [1, 512], BF16, R6 + 19456)

    K.P.nopfns = {
        "t": lambda e: e.matmul(K.pb[0][0:1, 0:1], K.ones1_b[0:1, 0:1],
                                K.ones1_b[0:1, 0:1], start=True, stop=True),
        "v": lambda e: e.memset(K.nopsc[0:1, 0:1], 0.0),
        "s": lambda e: e.memzero(K.nopsc[0:1, 1:2]),
        "gp": lambda e: e.memset(K.nopsc[0:1, 2:3], 0.0),
        "sy": lambda e: e.nop(),
    }
    K.pb = [nc.alloc_psum_tensor(f"pb{i}", [128, 512], F32).ap() for i in range(7)]
    K.pb7b = nc.alloc_psum_tensor("pb7b", [128, 1024], BF16).ap()
    K.pbn = [f"pb{i}" for i in range(7)] + ["pb7b"]
    return K


def ph_consts(K):
    P = K.P

    def g(fn, **kw):
        P.op("gp", fn, **kw)

    def mask_init(ap, nm, fill_val, cmp, base, pattern, chm):
        g(lambda e: e.memset(ap[:, :], fill_val), writes=(nm,))
        g(lambda e: e.affine_select(out=ap[:, :], in_=ap[:, :], compare_op=cmp,
                                    fill=0.0, base=base, pattern=pattern,
                                    channel_multiplier=chm), writes=(nm,))

    g(lambda e: e.memset(K.id_b[:, :], 0.0), writes=("id_b",))
    g(lambda e: e.affine_select(out=K.id_b[:, :], in_=K.id_b[:, :], compare_op=OP.not_equal,
                                fill=1.0, base=0, pattern=[[-1, 128]], channel_multiplier=1),
      writes=("id_b",))
    g(lambda e: e.memset(K.id_f[:, :], 0.0), writes=("id_f",))
    g(lambda e: e.affine_select(out=K.id_f[:, :], in_=K.id_f[:, :], compare_op=OP.not_equal,
                                fill=1.0, base=0, pattern=[[-1, 128]], channel_multiplier=1),
      writes=("id_f",))
    g(lambda e: e.memset(K.ones1_b[:, :], 1.0), writes=("ones1_b",))
    g(lambda e: e.memset(K.ones1_f[:, :], 1.0), writes=("ones1_f",))
    g(lambda e: e.memset(K.ones128_b[:, :], 1.0), writes=("ones128_b",))
    mask_init(K.mstr, "mstr", 1.0, OP.is_gt, 0, [[-1, 128]], 1)
    mask_init(K.mustr, "mustr", 1.0, OP.is_gt, 0, [[1, 128]], -1)
    mask_init(K.muincs, "muincs", SCALE, OP.is_ge, 0, [[1, 128]], -1)
    g(lambda e: e.memset(K.sstate[:, :], 0.0), writes=("sstate",))
    g(lambda e: e.memset(K.cpool[:, 0:1], EPS), writes=("cpool",))
    g(lambda e: e.memset(K.cpool[:, 1:2], 1.0), writes=("cpool",))

    P.pid_tok_old = [P.op("sy", lambda e: e.dma_start(
        out=K.pidsb[:, :], in_=K.nc.partition_id_tensor.ap()[0:1, 0:1]),
        writes=("pidsb",))]
    P.op("sy", lambda e: e.dma_start(out=K.bgsb[:, :], in_=K.bg_d[:, :]), writes=("bgsb",))
    P.op("gp", lambda e: e.dma_start(out=K.wgsb[:, :], in_=K.wg_d[:, :]),
         writes=("wgsb",), dma=True)
    P.op("gp", lambda e: e.dma_start(
        out=K.wkv.rearrange("p (c n) -> p c n", n=2048)[:, :, 0:1024],
        in_=K.wk_d.rearrange("(c p) n -> p c n", p=128)), writes=("wkv",), dma=True)
    P.op("gp", lambda e: e.dma_start(
        out=K.wkv.rearrange("p (c n) -> p c n", n=2048)[:, :, 1024:2048],
        in_=K.wv_d.rearrange("(c p) n -> p c n", p=128)), writes=("wkv",), dma=True)


def ph_b1(K):
    P = K.P
    pb = K.pb
    for t in range(NT):
        if DEBUG_TILE_BARRIER:
            P.barrier()
        xb = K.xt[t % 2]
        xbuf = f"xt{t % 2}"
        P.op("sy", lambda e, t=t, xb=xb: e.dma_start(out=xb[:, :],
                                                     in_=K.x_d[t * 128:(t + 1) * 128, :]),
             writes=(xbuf,))
        P.op("v", lambda e, xb=xb: e.reduce_sum(K.ln_a[:, :], xb[:, :], axis=AX.X),
             reads=(xbuf,), writes=("ln_a",))
        P.op("s", lambda e: e.activation(K.ln_b[:, :], K.ln_a[:, :], AF.Copy, scale=1.0 / HID),
             reads=("ln_a",), writes=("ln_b",))
        P.op("v", lambda e, xb=xb: e.tensor_scalar(xb[:, :], xb[:, :], K.ln_b[:, :], None,
                                                   OP.subtract),
             reads=("ln_b",), writes=(xbuf,))
        P.op("s", lambda e, xb=xb: e.activation(K.sqsc[:, :], xb[:, :], AF.Square),
             reads=(xbuf,), writes=("sqsc",))
        P.op("v", lambda e: e.reduce_sum(K.ln_c[:, :], K.sqsc[:, :], axis=AX.X),
             reads=("sqsc",), writes=("ln_c",))
        P.op("s", lambda e: e.activation(K.ln_d[:, :], K.ln_c[:, :], AF.Sqrt,
                                         scale=1.0 / HID, bias=EPS),
             reads=("ln_c", "cpool"), writes=("ln_d",))
        P.op("v", lambda e: e.reciprocal(K.ln_e[:, :], K.ln_d[:, :]),
             reads=("ln_d",), writes=("ln_e",))
        P.op("v", lambda e, xb=xb: e.tensor_scalar(xb[:, :], xb[:, :], K.ln_e[:, :],
                                                   None, OP.mult),
             reads=("ln_e",), writes=(xbuf,))
        for grp in range(4):
            bank = grp % 2
            for kk in range(4):
                fc = grp * 4 + kk
                P.op("t", lambda e, xb=xb, fc=fc, bank=bank, kk=kk: e.transpose(
                    pb[bank][:, kk * 128:(kk + 1) * 128],
                    xb[:, fc * 128:(fc + 1) * 128], K.id_f[:, :]),
                    reads=(xbuf, "id_f"), writes=(K.pbn[bank],))
            P.op("s", lambda e, grp=grp, bank=bank: e.copy(
                K.xcb[:, grp * 512:(grp + 1) * 512], pb[bank][:, :]),
                reads=(K.pbn[bank],), writes=("xcb",))
        P.op("sy", lambda e, t=t: e.dma_start(out=K.xfm_h[t], in_=K.xcb[:, :]),
             reads=("xcb",))
        for bi, co in ((2, 0), (3, 512), (4, 1024), (5, 1536)):
            for fc in range(16):
                st, sp = fc == 0, fc == 15
                P.op("t", lambda e, fc=fc, st=st, sp=sp, bi=bi, co=co: e.matmul(
                    pb[bi][:, :], K.xcb[:, fc * 128:(fc + 1) * 128],
                    K.wkv[:, fc * 2048 + co:fc * 2048 + co + 512], start=st, stop=sp),
                    reads=("xcb", "wkv"), writes=(K.pbn[bi],))
        for half in range(2):
            kb, vb = K.pbn[2 + half], K.pbn[4 + half]
            P.op("s", lambda e, half=half: e.copy(
                K.kraw_st[:, half * 512:(half + 1) * 512], pb[2 + half][:, :]),
                reads=(kb,), writes=("gkovl",))
            P.op("s", lambda e, half=half: e.activation(K.expst[:, :], pb[2 + half][:, :], AF.Exp),
                reads=(kb,), writes=("expst",))
            P.op("v", lambda e, half=half: e.reduce_sum(
                K.dn4[:, half * 4:(half + 1) * 4],
                K.expst.rearrange("p (s d) -> p s d", d=128), axis=AX.X),
                reads=("expst",), writes=("dn4",))
            P.op("v", lambda e, half=half: e.reciprocal(
                K.dn4r[:, half * 4:(half + 1) * 4], K.dn4[:, half * 4:(half + 1) * 4]),
                reads=("dn4",), writes=("dn4r",))
            for s4 in range(4):
                j = half * 4 + s4
                P.op("v", lambda e, t=t, j=j, s4=s4: e.tensor_scalar(
                    K.ks_tm[:, t * 1024 + j * 128:t * 1024 + (j + 1) * 128],
                    K.expst[:, s4 * 128:(s4 + 1) * 128], K.dn4r[:, j:j + 1], None, OP.mult),
                    reads=("expst", "dn4r"), writes=("ks_tm",))
            P.op("s", lambda e, t=t, half=half: e.copy(
                K.v_tm[:, t * 1024 + half * 512:t * 1024 + (half + 1) * 512],
                pb[4 + half][:, :]),
                reads=(vb,), writes=("v_tm",))
        for grp in range(2):
            for kk in range(4):
                j = grp * 4 + kk
                P.op("t", lambda e, j=j, kk=kk: e.transpose(
                    pb[6][:, kk * 128:(kk + 1) * 128],
                    K.kraw_st[:, j * 128:(j + 1) * 128], K.id_f[:, :]),
                    reads=("gkovl", "id_f"), writes=(K.pbn[6],))
            P.op("s", lambda e, grp=grp: e.copy(
                K.kfm_st[:, grp * 512:(grp + 1) * 512], pb[6][:, :]),
                reads=(K.pbn[6],), writes=("kfm_st",))
        P.op("sy", lambda e, t=t: e.dma_start(
            out=K.kfm_h.rearrange("p (j n) -> p j n", n=N)[:, :, t * 128:(t + 1) * 128],
            in_=K.kfm_st.rearrange("p (j n) -> p j n", n=128)),
            reads=("kfm_st",))
        for j in range(8):
            bi = 2 if j < 4 else 3
            P.op("t", lambda e, j=j, bi=bi: e.matmul(
                K.pb[bi][0:1, (j % 4) * 128:((j % 4) + 1) * 128], K.wgsb[:, :],
                K.kfm_st[:, j * 128:(j + 1) * 128], start=True, stop=True),
                reads=("kfm_st", "wgsb"), writes=(K.pbn[bi],))
        P.op("s", lambda e: e.copy(K.gstage[:, 0:512], pb[2][0:1, 0:512]),
             reads=(K.pbn[2], "kfm_st"), writes=("gkovl",))
        P.op("s", lambda e: e.copy(K.gstage[:, 512:1024], pb[3][0:1, 0:512]),
             reads=(K.pbn[3],), writes=("gkovl",))
        P.op("sy", lambda e, t=t: e.dma_start(
            out=K.gsb[0:8, t * 128:(t + 1) * 128],
            in_=K.gstage[0:1, 0:1024]),
            reads=("gkovl",), writes=("gsb",))


def ph_prologue(K):
    P = K.P
    if "v_early" in K.dbg:
        d = K.nc.dram_tensor("dbg_v_early", [128, NT * 1024], BF16,
                             kind="ExternalOutput").ap()
        P.op("sy", lambda e, d=d: e.dma_start(out=d, in_=K.v_tm[:, :]), reads=("v_tm",))
    pb = K.pb
    P.barrier()
    P.op("gp", lambda e: e.memset(K.rmask[:, :], 1.0), writes=("rmask",))
    P.op("gp", lambda e: e.affine_select(out=K.rmask[:, :], in_=K.rmask[:, :],
                                         compare_op=OP.is_gt, fill=0.0, base=0,
                                         pattern=[[0, 16], [1, 128]], channel_multiplier=0),
         writes=("rmask",))
    if "rmask_early" in K.dbg:
        dearly = K.nc.dram_tensor("dbg_rmask_early", [8, N], F32,
                                  kind="ExternalOutput").ap()
        P.op("sy", lambda e, d=dearly: e.dma_start(out=d, in_=K.rmask[:, :]),
             reads=("rmask",))
    P.op("v", lambda e: e.tensor_scalar(K.gsb[:, :], K.gsb[:, :], K.bgsb[:, :], None, OP.add),
         reads=("bgsb",), writes=("gsb",))
    P.op("s", lambda e: e.activation(K.t1b[:, :], K.gsb[:, :], AF.Abs),
         reads=("gsb",), writes=("t1b",))
    P.op("s", lambda e: e.activation(K.t1b[:, :], K.t1b[:, :], AF.Exp, scale=-1.0),
         reads=("t1b",), writes=("t1b",))
    P.op("s", lambda e: e.activation(K.t2b[:, :], K.t1b[:, :], AF.Ln, bias=1.0),
         reads=("t1b", "cpool"), writes=("t2b",))
    P.op("v", lambda e: e.tensor_scalar(K.t1b[:, :], K.gsb[:, :], 0.0, None, OP.min),
         reads=("gsb",), writes=("t1b",))
    P.op("v", lambda e: e.tensor_sub(K.t1b[:, :], K.t1b[:, :], K.t2b[:, :]),
         reads=("t1b", "t2b"), writes=("t1b",))
    P.op("s", lambda e: e.activation(K.gsb[:, :], K.t1b[:, :], AF.Copy, scale=1.0 / 16.0),
         reads=("t1b",), writes=("gsb",))
    P.op("v", lambda e: e.tensor_tensor_scan(K.cum[:, :], K.rmask[:, :], K.gsb[:, :], 0.0,
                                             OP.mult, OP.add),
         reads=("rmask", "gsb"), writes=("cum",))
    for c in range(16):
        P.op("t", lambda e, c=c: e.transpose(pb[0][0:128, 0:8],
                                             K.cum[0:8, c * 128:(c + 1) * 128],
                                             K.id_f[0:8, 0:8]),
             reads=("cum", "id_f"), writes=(K.pbn[0],))
        P.op("s", lambda e, c=c: e.copy(K.cumT[:, c * 8:(c + 1) * 8], pb[0][0:128, 0:8]),
             reads=(K.pbn[0],), writes=("cumT",))
    P.op("s", lambda e: e.activation(K.lall[:, :], K.cumT[:, :], AF.Exp),
         reads=("cumT",), writes=("lall",))
    P.op("v", lambda e: e.tensor_scalar(K.slall[:, :], K.lall[:, :], SCALE, None, OP.mult),
         reads=("lall",), writes=("slall",))
    P.op("sy", lambda e: e.dma_start(out=K.cumrow[0:1, 0:128], in_=K.cumT[127:128, :]),
         reads=("cumT",), writes=("cumrow",))
    P.op("t", lambda e: e.matmul(pb[1][:, 0:128], K.ones1_f[:, :], K.cumrow[0:1, 0:128],
                                 start=True, stop=True),
         reads=("ones1_f", "cumrow"), writes=(K.pbn[1],))
    P.op("s", lambda e: e.copy(K.ccb[:, :], pb[1][:, 0:128]), reads=(K.pbn[1],), writes=("ccb",))
    P.op("s", lambda e: e.activation(K.lcb[:, :], pb[1][:, 0:128], AF.Exp),
         reads=(K.pbn[1],), writes=("lcb",))


def dbg_dump(K, names):
    P = K.P
    for nm in names:
        if not hasattr(K, nm):
            continue
        ap = getattr(K, nm)
        d = K.nc.dram_tensor("dbg_" + nm, list(ap.shape), ap.dtype,
                             kind="ExternalOutput").ap()
        P.op("sy", lambda e, d=d, ap=ap: e.dma_start(out=d, in_=ap), reads=(nm,))


def _collect_waited(items, used):
    for it in items:
        if it and it[0] == "pad":
            continue
        if it and it[0] == "br":
            _collect_waited(it[1], used)
            _collect_waited(it[2], used)
            continue
        ws, fn, ctr, by = it
        for c, v in ws:
            used.setdefault(id(c), set()).add(v)


def _number_incs(items, used, newv, cnt, keep_all=False):
    out = []
    for it in items:
        if it and it[0] == "pad":
            _, c, amt = it
            st = cnt.setdefault(id(c), [0, 0])
            u = used.get(id(c), ())
            kept = sum(1 for v in range(st[0] + 1, st[0] + amt + 1) if v in u)
            for v in range(st[0] + 1, st[0] + amt + 1):
                st_new = st[1] + sum(1 for q in range(st[0] + 1, v + 1) if q in u)
                newv.setdefault(id(c), {})[v] = st_new
            st[0] += amt
            st[1] += kept
            if kept:
                out.append(("pad", c, kept))
            continue
        if it and it[0] == "br":
            snap = {k: [vv[0], vv[1]] for k, vv in cnt.items()}
            a = _number_incs(it[1], used, newv, cnt, keep_all)
            enda = {k: [vv[0], vv[1]] for k, vv in cnt.items()}
            for k in list(cnt.keys()):
                cnt[k] = [snap.get(k, [0, 0])[0], snap.get(k, [0, 0])[1]]
            b = _number_incs(it[2], used, newv, cnt, keep_all)
            for k in set(cnt) | set(enda):
                ca = enda.get(k, [0, 0])
                cb = cnt.get(k, [0, 0])
                assert ca == cb or k not in enda or k not in cnt, (ca, cb)
                cnt[k] = [max(ca[0], cb[0]), max(ca[1], cb[1])]
            out.append(("br", a, b))
            continue
        ws, fn, ctr, by = it
        keep = False
        if ctr is not None:
            st = cnt.setdefault(id(ctr), [0, 0])
            st[0] += by
            if keep_all or st[0] in used.get(id(ctr), ()):
                st[1] += by
                keep = True
            newv.setdefault(id(ctr), {})[st[0]] = st[1]
        out.append((ws, fn, ctr if keep else None, by))
    return out


def _rewrite_waits(items, newv):
    out = []
    for it in items:
        if it and it[0] == "pad":
            out.append(it)
        elif it and it[0] == "br":
            out.append(("br", _rewrite_waits(it[1], newv),
                        _rewrite_waits(it[2], newv)))
        else:
            ws, fn, ctr, by = it
            out.append(([(c, newv[id(c)][v]) for c, v in ws], fn, ctr, by))
    return out


def _prune_incs(P, force_used=()):
    used = {}
    for st in P.st.values():
        _collect_waited(st.items, used)
    for c, v in force_used:
        used.setdefault(id(c), set()).add(v)
    newv = {}
    cnt = {}
    for snm in P.st:
        P.st[snm].items = _number_incs(P.st[snm].items, used, newv, cnt,
                                       keep_all=snm in ("sy", "gp"))
    for snm in P.st:
        P.st[snm].items = _rewrite_waits(P.st[snm].items, newv)
    return newv


def finish(K):
    P = K.P
    fin = (P.ctrs["sy"], P.ctrs["sy"].n)
    pidw_old = list(P.pid_tok_old)
    newv = _prune_incs(P, force_used=[fin] + pidw_old)
    K._final_wait = (P.ctrs["sy"].h, newv[id(P.ctrs["sy"])][fin[1]])
    P.pidw_new = [(c, newv[id(c)][v]) for c, v in pidw_old]
    with K.nc.Block() as block:

        @block.tensor
        def _(eng):
            _emit_engine(K, eng, "t")

        @block.vector
        def _(eng):
            _emit_engine(K, eng, "v")

        @block.scalar
        def _(eng):
            _emit_engine(K, eng, "s")

        @block.gpsimd
        def _(eng):
            _emit_engine(K, eng, "gp")

        @block.sync
        def _(eng):
            _emit_engine(K, eng, "sy")
            eng.wait_ge(*K._final_wait)
    return K.nc


def shard_inputs(inputs):
    x = np.ascontiguousarray(inputs["L0_attn_hidden_states"], dtype=np.float32)
    maps = []
    import ml_dtypes
    tril = np.triu(np.ones((128, 128), np.float32))
    for core in range(8):
        b, h = core // 2, core % 2
        rows = np.concatenate([np.arange((2 * u + h) * 128, (2 * u + h) * 128 + 128)
                               for u in range(8)])
        own = np.ascontiguousarray(x[b, rows])
        fm = np.zeros((128, 8, 4, 128), np.float32)
        for r in range(8):
            for v in range(4):
                tv = 2 * v + h
                if r < tv:
                    fm[:, r, v, :] = 1.0
                elif r == tv:
                    fm[:, r, v, :] = tril
        pm = np.zeros((128, 256), np.uint8)
        pm[:, 0:128] = 1 if h == 0 else 0
        pm[:, 128:256] = 0 if h == 0 else 1
        maps.append({
            "fmask2": np.ascontiguousarray(
                fm.reshape(128, 4096).astype(ml_dtypes.bfloat16)),
            "pmask": pm,
            "x": np.ascontiguousarray(x[b]),
            "xown": own,
            "wq": np.ascontiguousarray(inputs["L0_attn_Wq"], np.float32),
            "wk": np.ascontiguousarray(inputs["L0_attn_Wk"], np.float32),
            "wv": np.ascontiguousarray(inputs["L0_attn_Wv"], np.float32),
            "wo": np.ascontiguousarray(inputs["L0_attn_Wo"], np.float32),
            "wg": np.ascontiguousarray(inputs["L0_attn_Wg"], np.float32),
            "bg": np.full((8, 1), np.float32(np.asarray(inputs["L0_attn_bg"]).reshape(-1)[0])),
            "gatew": np.ascontiguousarray(inputs["L0_ffn_gate_w"], np.float32),
            "gateb": np.ascontiguousarray(inputs["L0_ffn_gate_b"], np.float32).reshape(1, 8),
            "expw": np.ascontiguousarray(inputs["L0_ffn_expert_w"], np.float32),
            "expb": np.ascontiguousarray(inputs["L0_ffn_expert_b"], np.float32),
        })
    return maps


def assemble_out(results):
    out = np.empty((B, N, HID), np.float32)
    for core in range(8):
        b, h = core // 2, core % 2
        r = results[core]["out"]
        for u in range(8):
            out[b, (2 * u + h) * 128:(2 * u + h) * 128 + 128] = r[u * 128:(u + 1) * 128]
    return out


def ph_b2(K):
    P = K.P
    pb = K.pb
    P.barrier()
    for p in range(2):
        if STATIC_PIECE:
            P.op("gp", lambda e, p=p: e.dma_start(
                out=K.xpiece.rearrange("q (c t n) -> q c t n", t=4, n=128),
                in_=bass.AP(K.xfm_h.tensor, 8 * p * 262144,
                            [[2048, 128], [128, 16], [524288, 4], [1, 128]])),
                writes=("xpiece",), dma=True)
        else:
            def piece_load(e, preg, roff, p=p):
                e.reg_mov(roff, preg)
                e.reg_mul(roff, roff, 262144)
                e.reg_add(roff, roff, 8 * p * 262144)
                return e.dma_start(
                    out=K.xpiece.rearrange("q (c t n) -> q c t n", t=4, n=128),
                    in_=bass.AP(K.xfm_h.tensor, roff,
                                [[2048, 128], [128, 16], [524288, 4], [1, 128]]))
            piece_load._needs_regs = True
            P.op("gp", piece_load, writes=("xpiece",), dma=True)
        for fg in range(4):
            slab = K.wqslab[fg % 2]
            sbuf = f"wqslab{fg % 2}"
            P.op("gp", lambda e, slab=slab, fg=fg: e.dma_start(
                out=slab.rearrange("q (c n) -> q c n", n=512),
                in_=K.wq_d.rearrange("(c q) n -> q c n", q=128)[:, :, fg * 512:(fg + 1) * 512]),
                writes=(sbuf,), dma=True)
            for hp in range(4):
                h = fg * 4 + hp
                bank = hp % 2
                for fc in range(16):
                    P.op("t", lambda e, slab=slab, hp=hp, fc=fc, bank=bank: e.matmul(
                        pb[bank][:, :], slab[:, fc * 512 + hp * 128:fc * 512 + (hp + 1) * 128],
                        K.xpiece[:, fc * 512:(fc + 1) * 512],
                        start=fc == 0, stop=fc == 15),
                        reads=(sbuf, "xpiece"), writes=(K.pbn[bank],))
                P.op("s", lambda e, bank=bank: e.copy(K.qse[:, :], pb[bank][:, :]),
                     reads=(K.pbn[bank],), writes=("qse",))
                P.op("sy", lambda e, h=h, p=p: e.dma_start(
                    out=K.qfm_h[:, h * 1024 + p * 512:h * 1024 + (p + 1) * 512],
                    in_=K.qse[:, :]), reads=("qse",))
                P.op("s", lambda e, bank=bank: e.activation(K.qse2[:, :], pb[bank][:, :],
                                                            AF.Exp),
                     reads=(K.pbn[bank],), writes=("qse2",))
                P.op("t", lambda e: e.matmul(pb[2][0:1, :], K.ones128_b[:, :], K.qse2[:, :],
                                             start=True, stop=True),
                     reads=("ones128_b", "qse2"), writes=(K.pbn[2],))
                def _recip(e):
                    with K.nc.allow_low_precision(reason="softmax denom bf16"):
                        return e.reciprocal(K.rde[0:1, :], pb[2][0:1, :])
                P.op("v", _recip, reads=(K.pbn[2],), writes=("rde",))
                P.op("t", lambda e: e.matmul(pb[3][:, :], K.ones1_b[:, :], K.rde[0:1, :],
                                             start=True, stop=True),
                     reads=("ones1_b", "rde"), writes=(K.pbn[3],))
                P.op("v", lambda e, h=h, p=p: e.tensor_tensor(
                    K.qs_fm[:, h * 1024 + p * 512:h * 1024 + (p + 1) * 512],
                    K.qse2[:, :], pb[3][:, :], OP.mult),
                    reads=("qse2", K.pbn[3]), writes=("qs_fm",))


def ph_delta(K):
    P = K.P
    pb = K.pb
    if DELTA_UPTO >= -3:
        P.barrier()
    if DELTA_UPTO >= -2:
        P.op("sy", lambda e: e.dma_start(out=K.pmasksb[:, :], in_=K.pmask_d[:, :]),
             writes=("pmasksb",))
    ksl = lambda c, j: K.ks_tm[:, c * 1024 + j * 128:c * 1024 + (j + 1) * 128]
    for c in range(DELTA_CHUNKS):
        if DELTA_UPTO >= -1:
            P.op("sy", lambda e, c=c: e.dma_start(
                out=K.cumrow.rearrange("o (j n) -> o j n", n=128),
                in_=K.cum[0:8, c * 128:(c + 1) * 128]),
                reads=("cum",), writes=("cumrow",))
        for j in range(DELTA_J):
            col = slice(c * 8 + j, c * 8 + j + 1)
            cumTc = K.cumT[:, col]
            if DELTA_UPTO < 1:
                continue
            P.op("t", lambda e, c=c, j=j: e.transpose(
                K.pb7b[:, 0:128], ksl(c, j), K.id_b[:, :]),
                reads=("ks_tm", "id_b"), writes=("pb7b",))
            P.op("s", lambda e: e.copy(K.ksfm_c[:, :], K.pb7b[:, 0:128]),
                 reads=("pb7b",), writes=("ksfm_c",))
            if DELTA_UPTO < 2:
                continue
            P.op("t", lambda e, j=j: e.matmul(
                pb[1][:, 0:128], K.ones1_f[:, :], K.cumrow[0:1, j * 128:(j + 1) * 128],
                start=True, stop=True),
                reads=("ones1_f", "cumrow"), writes=(K.pbn[1],))
            P.op("v", lambda e, cumTc=cumTc: e.tensor_scalar(
                K.dn1[:, :], pb[1][:, 0:128], cumTc, 0.0, OP.subtract, OP.max),
                reads=(K.pbn[1], "cumT"), writes=("dn1",))
            P.op("s", lambda e: e.activation(K.e1[:, :], K.dn1[:, :], AF.Exp, scale=-1.0),
                 reads=("dn1",), writes=("e1",))
            P.op("v", lambda e, cumTc=cumTc: e.tensor_scalar(
                K.dn1[:, :], pb[1][:, 0:128], cumTc, 0.0, OP.subtract, OP.min),
                reads=(K.pbn[1], "cumT", "e1"), writes=("dn1",))
            P.op("s", lambda e: e.activation(K.e2[:, :], K.dn1[:, :], AF.Exp),
                 reads=("dn1",), writes=("e2",))
            P.op("v", lambda e: e.tensor_tensor(K.e1m[:, :], K.e1[:, :], K.mstr[:, :],
                                                OP.mult),
                 reads=("e1", "mstr"), writes=("e1m",))
            P.op("v", lambda e: e.tensor_tensor(K.e2m[:, :], K.e2[:, :], K.mustr[:, :],
                                                OP.mult),
                 reads=("e2", "mustr"), writes=("e2m",))
            P.op("v", lambda e: e.tensor_tensor(K.ebt[:, :], K.e2[:, :], K.muincs[:, :],
                                                OP.mult),
                 reads=("e2", "muincs"), writes=("ebt",))
            if DELTA_UPTO < 3:
                continue
            P.op("t", lambda e: e.matmul(pb[0][:, 0:128], K.ksfm_c[:, :], K.ksfm_c[:, :],
                                         start=True, stop=True),
                 reads=("ksfm_c",), writes=(K.pbn[0],))
            P.op("v", lambda e: e.tensor_tensor(K.a_sb[:, :], pb[0][:, 0:128],
                                                K.e1m[:, :], OP.mult),
                 reads=(K.pbn[0], "e1m"), writes=("a_sb",))
            P.op("v", lambda e: e.tensor_tensor(K.at_sb[:, :], pb[0][:, 0:128],
                                                K.e2m[:, :], OP.mult),
                 reads=(K.pbn[0], "e2m"), writes=("at_sb",))
            if DELTA_UPTO < 4:
                continue
            P.op("v", lambda e: e.scalar_tensor_tensor(
                K.z_sb[0][:, :], K.at_sb[:, :], -1.0, K.id_b[:, :], OP.mult, OP.add),
                reads=("at_sb", "id_b"), writes=("z_sb0",))
            cur, curt = "a_sb", "at_sb"
            curap, curtap = K.a_sb, K.at_sb
            for lvl in range(6):
                zi, zo = lvl % 2, (lvl + 1) % 2
                P.op("t", lambda e, curap=curap, curtap=curtap: e.matmul(
                    pb[2][:, 0:128], curtap[:, :], curap[:, :], start=True, stop=True),
                    reads=(cur, curt), writes=(K.pbn[2],))
                if lvl < 5:
                    P.op("t", lambda e, curap=curap, curtap=curtap: e.matmul(
                        pb[3][:, 0:128], curap[:, :], curtap[:, :], start=True, stop=True),
                        reads=(cur, curt), writes=(K.pbn[3],))
                nap, napt = K.ap_sb[lvl % 2], K.apt_sb[lvl % 2]
                napn, naptn = f"ap_sb{lvl % 2}", f"apt_sb{lvl % 2}"
                P.op("s", lambda e, nap=nap: e.copy(nap[:, :], pb[2][:, 0:128]),
                     reads=(K.pbn[2],), writes=(napn,))
                if lvl < 5:
                    P.op("s", lambda e, napt=napt: e.copy(napt[:, :], pb[3][:, 0:128]),
                         reads=(K.pbn[3],), writes=(naptn,))
                P.op("t", lambda e, nap=nap, zi=zi: e.matmul(
                    pb[4][:, 0:128], nap[:, :], K.z_sb[zi][:, :], start=True, stop=False),
                    reads=(napn, f"z_sb{zi}"), writes=(K.pbn[4],))
                P.op("t", lambda e, zi=zi: e.matmul(
                    pb[4][:, 0:128], K.id_b[:, :], K.z_sb[zi][:, :], start=False, stop=True),
                    reads=("id_b", f"z_sb{zi}"), writes=(K.pbn[4],))
                P.op("s", lambda e, zo=zo: e.copy(K.z_sb[zo][:, :], pb[4][:, 0:128]),
                     reads=(K.pbn[4],), writes=(f"z_sb{zo}",))
                cur, curt = napn, naptn
                curap, curtap = nap, napt
            zf, zfn = K.z_sb[0], "z_sb0"
            if DELTA_UPTO < 5:
                continue
            lallc = K.lall[:, col]
            P.op("v", lambda e, c=c, j=j, lallc=lallc: e.tensor_scalar(
                K.lk_sb[:, :], ksl(c, j), lallc, None, OP.mult),
                reads=("ks_tm", "lall"), writes=("lk_sb",))
            P.op("v", lambda e, cumTc=cumTc, c=c, j=j: e.tensor_scalar(
                K.kdcol[:, :], cumTc, K.ccb[:, c * 8 + j:c * 8 + j + 1], -1.0,
                OP.subtract, OP.mult),
                reads=("cumT", "ccb"), writes=("kdcol",))
            P.op("s", lambda e: e.activation(K.kpcol[:, :], K.kdcol[:, :], AF.Exp),
                 reads=("kdcol",), writes=("kpcol",))
            P.op("v", lambda e, c=c, j=j: e.tensor_scalar(
                K.kp_sb[:, :], ksl(c, j), K.kpcol[:, :], None, OP.mult),
                reads=("ks_tm", "kpcol"), writes=("kp_sb",))
            P.op("t", lambda e, zf=zf: e.matmul(pb[5][:, 0:128], K.lk_sb[:, :], zf[:, :],
                                                start=True, stop=True),
                 reads=("lk_sb", zfn), writes=(K.pbn[5],))
            P.op("s", lambda e: e.activation(K.nmt_sb[:, :], pb[5][:, 0:128], AF.Copy,
                                             scale=-1.0),
                 reads=(K.pbn[5],), writes=("nmt_sb",))
            if DELTA_UPTO < 6:
                continue
            sstate_j = K.sstate[:, j * 128:(j + 1) * 128]
            P.op("t", lambda e, zf=zf, c=c, j=j: e.matmul(
                pb[6][:, 0:128], zf[:, :],
                K.v_tm[:, c * 1024 + j * 128:c * 1024 + (j + 1) * 128],
                start=True, stop=False),
                reads=(zfn, "v_tm"), writes=(K.pbn[6],))
            P.op("t", lambda e, sstate_j=sstate_j: e.matmul(
                pb[6][:, 0:128], K.nmt_sb[:, :], sstate_j, start=False, stop=True),
                reads=("nmt_sb", "sstate"), writes=(K.pbn[6],))
            P.op("v", lambda e: e.tensor_copy(K.uc_sb[:, :], pb[6][:, 0:128]),
                 reads=(K.pbn[6],), writes=("uc_sb",))
            if DELTA_UPTO < 7:
                continue
            u = c // 2
            slallc = K.slall[:, col]
            pmsl = K.pmasksb[:, (c % 2) * 128:(c % 2) * 128 + 128]
            for hh in (2 * j, 2 * j + 1):
                qsl = K.qs_fm[:, hh * 1024 + u * 128:hh * 1024 + (u + 1) * 128]
                P.op("t", lambda e, qsl=qsl: e.matmul(
                    pb[0][:, 0:128], K.ksfm_c[:, :], qsl, start=True, stop=True),
                    reads=("ksfm_c", "qs_fm"), writes=(K.pbn[0],))
                P.op("v", lambda e: e.tensor_tensor(K.bt_sb[:, :], pb[0][:, 0:128],
                                                    K.ebt[:, :], OP.mult),
                     reads=(K.pbn[0], "ebt"), writes=("bt_sb",))
                P.op("t", lambda e, qsl=qsl, sstate_j=sstate_j: e.matmul(
                    pb[2][:, 0:128], qsl, sstate_j, start=True, stop=True),
                    reads=("qs_fm", "sstate"), writes=(K.pbn[2],))
                P.op("s", lambda e, slallc=slallc: e.activation(
                    K.oa_sb[:, :], pb[2][:, 0:128], AF.Copy, scale=slallc),
                    reads=(K.pbn[2], "slall"), writes=("oa_sb",))
                P.op("t", lambda e: e.matmul(pb[3][:, 0:128], K.bt_sb[:, :],
                                             K.uc_sb[:, :], start=True, stop=True),
                     reads=("bt_sb", "uc_sb"), writes=(K.pbn[3],))
                P.op("v", lambda e: e.tensor_tensor(K.dtmp[:, :], K.oa_sb[:, :],
                                                    pb[3][:, 0:128], OP.add),
                     reads=("oa_sb", K.pbn[3]), writes=("dtmp",))
                P.op("v", lambda e, hh=hh, u=u, pmsl=pmsl: e.copy_predicated(
                    K.fla[:, u * 2048 + hh * 128:u * 2048 + (hh + 1) * 128],
                    pmsl, K.dtmp[:, :]),
                    reads=("dtmp", "pmasksb"), writes=("fla",))
            if DELTA_UPTO < 8:
                continue
            lcbc = K.lcb[:, col]
            P.op("t", lambda e: e.matmul(pb[4][:, 0:128], K.kp_sb[:, :], K.uc_sb[:, :],
                                         start=True, stop=True),
                 reads=("kp_sb", "uc_sb"), writes=(K.pbn[4],))
            P.op("v", lambda e, sstate_j=sstate_j, lcbc=lcbc: e.scalar_tensor_tensor(
                sstate_j, sstate_j, lcbc, pb[4][:, 0:128], OP.mult, OP.add),
                reads=("lcb", K.pbn[4]), writes=("sstate",))


def ph_flash(K):
    P = K.P
    pb = K.pb
    P.barrier()
    P.op("sy", lambda e: e.dma_start(out=K.kfm_bk[:, :], in_=K.kfm_h[:, :]),
         writes=("kfm_bk",))
    P.op("sy", lambda e: e.dma_start(out=K.qfm_bk[:, :], in_=K.qfm_h[:, :]),
         writes=("qfm_bk",))
    P.op("sy", lambda e: e.dma_start(out=K.fmask2sb[:, :], in_=K.fmask2_d[:, :]),
         writes=("fmask2sb",))
    for h in range(H):
        j = h // 2
        for p in range(2):
            kext = 8 * p + 8
            qsl = K.qfm_bk[:, h * 1024 + p * 512:h * 1024 + (p + 1) * 512]
            for k in range(kext):
                bank = k % 2
                P.op("t", lambda e, j=j, k=k, bank=bank, qsl=qsl: e.matmul(
                    pb[bank][:, :], K.kfm_bk[:, j * 2048 + k * 128:j * 2048 + (k + 1) * 128],
                    qsl, start=True, stop=True),
                    reads=("kfm_bk", "qfm_bk"), writes=(K.pbn[bank],))
                r = k - 8 * p
                if r >= 0:
                    P.op("s", lambda e, bank=bank: e.activation(
                        K.pt_tmp[:, :], pb[bank][:, :], AF.Exp, scale=SCALE),
                        reads=(K.pbn[bank],), writes=("pt_tmp",))
                    P.op("v", lambda e, k=k, r=r: e.tensor_tensor(
                        K.pt_all[:, k * 512:(k + 1) * 512], K.pt_tmp[:, :],
                        K.fmask2sb[:, r * 512:(r + 1) * 512], OP.mult),
                        reads=("fmask2sb", "pt_tmp"), writes=("pt_all",))
                else:
                    P.op("s", lambda e, k=k, bank=bank: e.activation(
                        K.pt_all[:, k * 512:(k + 1) * 512], pb[bank][:, :], AF.Exp,
                        scale=SCALE),
                        reads=(K.pbn[bank],), writes=("pt_all",))
            for k in range(kext):
                P.op("t", lambda e, j=j, k=k, kext=kext: e.matmul(
                    pb[2][:, :], K.v_tm[:, k * 1024 + j * 128:k * 1024 + (j + 1) * 128],
                    K.pt_all[:, k * 512:(k + 1) * 512],
                    start=k == 0, stop=k == kext - 1),
                    reads=("v_tm", "pt_all"), writes=(K.pbn[2],))
            for k in range(kext):
                P.op("t", lambda e, k=k, kext=kext: e.matmul(
                    pb[3][0:1, :], K.ones128_b[:, :],
                    K.pt_all[:, k * 512:(k + 1) * 512],
                    start=k == 0, stop=k == kext - 1),
                    reads=("ones128_b", "pt_all"), writes=(K.pbn[3],))
            def _recipl(e):
                with K.nc.allow_low_precision(reason="softmax denom bf16"):
                    return e.reciprocal(K.rl_sb[0:1, :], pb[3][0:1, :])
            P.op("v", _recipl, reads=(K.pbn[3],), writes=("rl_sb",))
            P.op("t", lambda e: e.matmul(pb[4][:, :], K.ones1_b[:, :], K.rl_sb[0:1, :],
                                         start=True, stop=True),
                 reads=("ones1_b", "rl_sb"), writes=(K.pbn[4],))
            P.op("s", lambda e: e.copy(K.rb_sb[:, :], pb[4][:, :]),
                 reads=(K.pbn[4],), writes=("rb_sb",))
            P.op("v", lambda e, h=h, p=p: e.tensor_tensor(
                K.base_fm[:, h * 1024 + p * 512:h * 1024 + (p + 1) * 512],
                pb[2][:, :], K.rb_sb[:, :], OP.mult),
                reads=(K.pbn[2], "rb_sb"), writes=("base_fm",))


def x1sl(K, tto, c0=0, c1=2048):
    buf = K.x1a if tto < 4 else K.x1b
    nm = "x1a" if tto < 4 else "x1b"
    return buf[:, (tto % 4) * 2048 + c0:(tto % 4) * 2048 + c1], nm


def ph_wo(K):
    P = K.P
    pb = K.pb
    P.barrier()
    for tto in range(8):
        xs, nm = x1sl(K, tto)
        P.op("sy", lambda e, tto=tto, xs=xs: e.dma_start(
            out=xs, in_=K.xown_d[tto * 128:(tto + 1) * 128, :]), writes=(nm,))
    for fg in range(4):
        slab = K.woslab[fg % 2]
        snm = f"woslab{fg % 2}"
        P.op("gp", lambda e, slab=slab, fg=fg: e.dma_start(
            out=slab.rearrange("q (c n) -> q c n", n=512),
            in_=K.wo_d.rearrange("(c q) n -> q c n", q=128)[:, :, fg * 512:(fg + 1) * 512]),
            writes=(snm,), dma=True)
        for tto in range(8):
            bank = tto % 2
            for fc in range(16):
                P.op("t", lambda e, slab=slab, fc=fc, tto=tto, bank=bank: e.matmul(
                    pb[bank][:, :], K.base_fm[:, fc * 1024 + tto * 128:fc * 1024 + (tto + 1) * 128],
                    slab[:, fc * 512:(fc + 1) * 512], start=fc == 0, stop=fc == 15),
                    reads=("base_fm", snm), writes=(K.pbn[bank],))
            xs, nm = x1sl(K, tto, fg * 512, (fg + 1) * 512)
            P.op("v", lambda e, xs=xs, bank=bank: e.scalar_tensor_tensor(
                xs, pb[bank][:, :], 0.99, xs, OP.mult, OP.add),
                reads=(K.pbn[bank],), writes=(nm,))
            P.op("v", lambda e, xs=xs, tto=tto, fg=fg: e.scalar_tensor_tensor(
                xs, K.fla[:, tto * 2048 + fg * 512:tto * 2048 + (fg + 1) * 512],
                0.01, xs, OP.mult, OP.add),
                reads=("fla",), writes=(nm,))


def ph_ffn(K):
    P = K.P
    pb = K.pb
    P.barrier()
    P.op("gp", lambda e: e.dma_start(
        out=K.gatewsb.rearrange("q (c n) -> q c n", n=8),
        in_=K.gw_d.rearrange("(c q) n -> q c n", q=128)), writes=("gatewsb",), dma=True)
    P.op("gp", lambda e: e.dma_start(out=K.gatebsb[:, :], in_=K.gb_d[:, :]),
         writes=("gatebsb",), dma=True)
    for tto in range(8):
        xs, nm = x1sl(K, tto)
        P.op("v", lambda e, xs=xs: e.reduce_sum(K.ln_a[:, :], xs, axis=AX.X),
             reads=(nm,), writes=("ln_a",))
        P.op("s", lambda e: e.activation(K.ln_b[:, :], K.ln_a[:, :], AF.Copy,
                                         scale=1.0 / HID),
             reads=("ln_a",), writes=("ln_b",))
        P.op("v", lambda e, xs=xs: e.tensor_scalar(K.ln2sc[:, :], xs, K.ln_b[:, :],
                                                   None, OP.subtract),
             reads=(nm, "ln_b"), writes=("ln2sc",))
        P.op("s", lambda e: e.activation(K.ln2st[:, :], K.ln2sc[:, :], AF.Square),
             reads=("ln2sc",), writes=("ln2st",))
        P.op("v", lambda e: e.reduce_sum(K.ln_c[:, :], K.ln2st[:, :], axis=AX.X),
             reads=("ln2st",), writes=("ln_c",))
        P.op("s", lambda e: e.activation(K.ln_d[:, :], K.ln_c[:, :], AF.Sqrt,
                                         scale=1.0 / HID, bias=EPS),
             reads=("ln_c", "cpool"), writes=("ln_d",))
        P.op("v", lambda e: e.reciprocal(K.ln_e[:, :], K.ln_d[:, :]),
             reads=("ln_d",), writes=("ln_e",))
        P.op("v", lambda e: e.tensor_scalar(K.ln2sc[:, :], K.ln2sc[:, :], K.ln_e[:, :],
                                            None, OP.mult),
             reads=("ln_e",), writes=("ln2sc",))
        for grp in range(4):
            bank = grp % 2
            for kk in range(4):
                fc = grp * 4 + kk
                P.op("t", lambda e, fc=fc, bank=bank, kk=kk: e.transpose(
                    pb[bank][:, kk * 128:(kk + 1) * 128],
                    K.ln2sc[:, fc * 128:(fc + 1) * 128], K.id_f[:, :]),
                    reads=("ln2sc", "id_f"), writes=(K.pbn[bank],))
            for kk in range(4):
                fc = grp * 4 + kk
                P.op("s", lambda e, fc=fc, kk=kk, bank=bank, tto=tto: e.copy(
                    K.xh2_fm[:, fc * 1024 + tto * 128:fc * 1024 + (tto + 1) * 128],
                    pb[bank][:, kk * 128:(kk + 1) * 128]),
                    reads=(K.pbn[bank],), writes=("xh2_fm",))
        P.op("t", lambda e: e.matmul(pb[5][:, 0:8], K.ones1_b[:, :],
                                     K.gatebsb[0:1, :], start=True, stop=False),
             reads=("ones1_b", "gatebsb"), writes=(K.pbn[5],))
        for fc in range(16):
            P.op("t", lambda e, fc=fc, tto=tto: e.matmul(
                pb[5][:, 0:8], K.xh2_fm[:, fc * 1024 + tto * 128:fc * 1024 + (tto + 1) * 128],
                K.gatewsb[:, fc * 8:(fc + 1) * 8], start=False, stop=fc == 15),
                reads=("xh2_fm", "gatewsb"), writes=(K.pbn[5],))
        P.op("v", lambda e: e.reduce_max(K.gm1[:, :], pb[5][:, 0:8], axis=AX.X),
             reads=(K.pbn[5],), writes=("gm1",))
        P.op("v", lambda e: e.tensor_scalar(K.gt1[:, :], pb[5][:, 0:8], K.gm1[:, :],
                                            None, OP.is_ge),
             reads=(K.pbn[5], "gm1"), writes=("gt1",))
        P.op("v", lambda e: e.scalar_tensor_tensor(K.gt2[:, :], K.gt1[:, :], -1e30,
                                                   pb[5][:, 0:8], OP.mult, OP.add),
             reads=("gt1", K.pbn[5]), writes=("gt2",))
        P.op("v", lambda e: e.reduce_max(K.gm2[:, :], K.gt2[:, :], axis=AX.X),
             reads=("gt2",), writes=("gm2",))
        P.op("v", lambda e, tto=tto: e.tensor_scalar(
            K.wcols[:, tto * 8:(tto + 1) * 8], pb[5][:, 0:8], K.gm2[:, :], 0.5,
            OP.is_ge, OP.mult),
            reads=(K.pbn[5], "gm2"), writes=("wcols",))
    for e_i in range(8):
        brow = K.berow[e_i % 2]
        bnm = f"berow{e_i % 2}"
        P.op("gp", lambda e, brow=brow, e_i=e_i: e.dma_start(
            out=brow[0:1, :], in_=K.eb_d[e_i:e_i + 1, :]), writes=(bnm,), dma=True)
        for fg in range(4):
            si = (e_i * 4 + fg) % 2
            slab = K.weslab[si]
            snm = f"weslab{si}"
            P.op("gp", lambda e, slab=slab, e_i=e_i, fg=fg: e.dma_start(
                out=slab.rearrange("q (c n) -> q c n", n=512),
                in_=K.ew_d.rearrange("e (c q) n -> e q c n", q=128)[
                    e_i, :, :, fg * 512:(fg + 1) * 512]),
                writes=(snm,), dma=True)
            for tto in range(8):
                bank = 2 + tto % 2
                P.op("t", lambda e, brow=brow, fg=fg, bank=bank: e.matmul(
                    pb[bank][:, :], K.ones1_b[:, :],
                    brow[0:1, fg * 512:(fg + 1) * 512], start=True, stop=False),
                    reads=("ones1_b", bnm), writes=(K.pbn[bank],))
                for fc in range(16):
                    P.op("t", lambda e, slab=slab, fc=fc, tto=tto, bank=bank: e.matmul(
                        pb[bank][:, :],
                        K.xh2_fm[:, fc * 1024 + tto * 128:fc * 1024 + (tto + 1) * 128],
                        slab[:, fc * 512:(fc + 1) * 512], start=False, stop=fc == 15),
                        reads=("xh2_fm", snm), writes=(K.pbn[bank],))
                P.op("s", lambda e, bank=bank, tto=tto, e_i=e_i: e.activation(
                    K.ffnu[:, :], pb[bank][:, :], AF.Relu,
                    scale=K.wcols[:, tto * 8 + e_i:tto * 8 + e_i + 1]),
                    reads=(K.pbn[bank], "wcols"), writes=("ffnu",))
                xs, nm = x1sl(K, tto, fg * 512, (fg + 1) * 512)
                P.op("v", lambda e, xs=xs: e.tensor_tensor(xs, xs, K.ffnu[:, :], OP.add),
                     reads=("ffnu",), writes=(nm,))
    for tto in range(8):
        xs, nm = x1sl(K, tto)
        P.op("sy", lambda e, tto=tto, xs=xs: e.dma_start(
            out=K.out_d[tto * 128:(tto + 1) * 128, :], in_=xs), reads=(nm,))


def build_full(debug=(), upto="all"):
    K = make_K(debug)
    ph_consts(K)
    ph_b1(K)
    ph_prologue(K)
    if upto != "b1":
        ph_b2(K)
        if upto != "b2":
            ph_delta(K)
            if upto != "delta":
                ph_flash(K)
                if upto != "flash":
                    ph_wo(K)
                    ph_ffn(K)
    if K.dbg:
        dbg_dump(K, [n for n in K.dbg if not n.endswith("_h")])
    return finish(K)


def kernel(**inputs):
    nc = build_full()
    maps = shard_inputs(inputs)
    res = run_bass_kernel_spmd(nc, maps, core_ids=list(range(8)))
    return assemble_out(res.results)


# version 02
# speedup vs baseline: 1.0262x; 1.0262x over previous
import math
import types
import numpy as np
import concourse.bass as bass
import concourse.mybir as mybir
from concourse.bass_utils import run_bass_kernel_spmd

F32 = mybir.dt.float32
BF16 = mybir.dt.bfloat16
U32 = mybir.dt.uint32
U8 = mybir.dt.uint8
AF = mybir.ActivationFunctionType
OP = mybir.AluOpType
AX = mybir.AxisListType

B, N, HID = 4, 2048, 2048
H, HKV, D = 16, 8, 128
NT = 16
SCALE = 1.0 / math.sqrt(D)
EPS = 1e-5

DEBUG_TILE_BARRIER = False
FULL_SERIAL = False
SER_MARKED = True
STATIC_PIECE = False
DELTA_CHUNKS = 16
DELTA_J = 8
DELTA_UPTO = 8

SBUF_BASE = 16512
SBUF_LIMIT = 212832


class Ctr:
    __slots__ = ("h", "n", "by")

    def __init__(self, h, by=1):
        self.h, self.n, self.by = h, 0, by


SCALARISH = {"ln_a", "ln_b", "ln_c", "ln_d", "ln_e", "dn4", "dn4r", "gt1",
             "gt2", "gm1", "gm2", "kdcol", "kpcol", "rl_sb", "rde", "wcols",
             "bgsb"}


class Buf:
    __slots__ = ("name", "w", "rs", "scalar")

    def __init__(self, name):
        self.name, self.w, self.rs = name, [], []
        self.scalar = name in SCALARISH


class Stream:
    def __init__(self, name, ctr, self_waits=False):
        self.name, self.ctr = name, ctr
        self.self_waits = self_waits
        self.items = []
        self.waited = {}
        self.sink = self.items

    dma_ctr = None

    def raw(self, fn, waits=(), inc=None, self_ok=False):
        ws = []
        for t in waits:
            if t is None:
                continue
            c, v = t
            if c is self.ctr and not (self.self_waits and self_ok):
                continue
            if self.waited.get(id(c), 0) >= v:
                continue
            self.waited[id(c)] = v
            ws.append((c, v))
        tok = None
        if inc is not None:
            inc.n += inc.by
            tok = (inc, inc.n)
        self.sink.append((ws, fn, inc, inc.by if inc else 0))
        return tok


def _merge_toks(toks):
    best = {}
    for c, v in toks:
        if best.get(id(c), (None, -1))[1] < v:
            best[id(c)] = (c, v)
    return list(best.values())


class Prog:
    def __init__(self, nc):
        self.nc = nc
        self.ctrs = {}
        for nm in ("t", "v", "s", "sy", "gp", "gq"):
            by = 16 if nm in ("sy", "gq") else 1
            self.ctrs[nm] = Ctr(nc.alloc_semaphore(name=f"c_{nm}"), by)
        self.st = {nm: Stream(nm, self.ctrs[nm], self_waits=nm in ("v", "s"))
                   for nm in ("t", "v", "s", "sy", "gp")}
        self.st["gp"].dma_ctr = self.ctrs["gq"]
        self.nopfns = {}
        self.bufs = {}
        self.pid_ap = None

    def buf(self, name):
        b = self.bufs.get(name)
        if b is None:
            b = self.bufs[name] = Buf(name)
        return b

    def op(self, snm, fn, reads=(), writes=(), extra=(), ser=False, dma=False):
        if (FULL_SERIAL is True or (isinstance(FULL_SERIAL, str) and snm in FULL_SERIAL)
                or (ser and SER_MARKED)):
            self.barrier()
        s = self.st[snm]
        waits = list(extra) + list(self.buf("__barrier__").w)
        rb = [self.buf(x) for x in reads]
        wb = [self.buf(x) for x in writes]
        for b in rb:
            waits += b.w
        for b in wb:
            waits += b.rs
            waits += b.w
        ctr = s.dma_ctr if (dma and s.dma_ctr is not None) else s.ctr
        self_ok = any(b.scalar for b in rb)
        tok = s.raw(fn, waits=waits, inc=ctr, self_ok=self_ok)
        for b in rb:
            b.rs.append(tok)
        for b in wb:
            b.w = [tok]
            b.rs = []
        return tok

    def barrier(self):
        toks = [(c, c.n) for c in self.ctrs.values() if c.n > 0]
        for b in self.bufs.values():
            b.w = _merge_toks(b.w + toks)
            b.rs = []
        bb = self.buf("__barrier__")
        bb.w = _merge_toks(toks)

    def bwait(self):
        return list(self.buf("__barrier__").w)

    def begin_branch(self):
        self._snap_ctr = {k: c.n for k, c in self.ctrs.items()}
        self._snap_buf = {k: (list(b.w), list(b.rs)) for k, b in self.bufs.items()}
        self._snap_wait = {k: dict(s.waited) for k, s in self.st.items()}
        self._sideA = {k: [] for k in self.st}
        self._sideB = {k: [] for k in self.st}
        for k, s in self.st.items():
            s.sink = self._sideA[k]

    def mid_branch(self):
        self._a_ctr = {k: c.n for k, c in self.ctrs.items()}
        self._a_buf = {k: (list(b.w), list(b.rs)) for k, b in self.bufs.items()}
        for k, c in self.ctrs.items():
            c.n = self._snap_ctr[k]
        for k, b in self.bufs.items():
            w, r = self._snap_buf.get(k, ([], []))
            b.w, b.rs = list(w), list(r)
        for k, s in self.st.items():
            s.waited = dict(self._snap_wait[k])
            s.sink = self._sideB[k]

    def end_branch(self):
        for k in self.st:
            fn = self.nopfns[k]
            for side in (self._sideA[k], self._sideB[k]):
                if not side:
                    side.append(((), fn, None, 0))
        b_ctr = {k: c.n for k, c in self.ctrs.items()}
        b_buf = {k: (list(b.w), list(b.rs)) for k, b in self.bufs.items()}
        for k, c in self.ctrs.items():
            hi = max(self._a_ctr[k], b_ctr[k])
            if hi - self._a_ctr[k]:
                self._sideA[k].append(("pad", c, hi - self._a_ctr[k]))
            if hi - b_ctr[k]:
                self._sideB[k].append(("pad", c, hi - b_ctr[k]))
            c.n = hi
        for nm in set(self._a_buf) | set(b_buf):
            wa, ra = self._a_buf.get(nm, ([], []))
            wb, rb = b_buf.get(nm, ([], []))
            bb = self.buf(nm)
            bb.w = _merge_toks(wa + wb)
            bb.rs = _merge_toks(ra + rb)
        for k, s in self.st.items():
            s.waited = dict(self._snap_wait[k])
            s.sink = s.items
            s.items.append(("br", self._sideA[k], self._sideB[k]))


def _ensure_regs(K, eng, st):
    if st["preg"] is None:
        preg = eng.alloc_register(f"parity_{st['nm']}")
        roff = eng.alloc_register(f"roff_{st['nm']}")
        for c, v in st["pidw"]:
            eng.wait_ge(c.h, v)
        eng.reg_load(preg, K.pidsb[0:1, 0:1])
        eng.reg_mod(preg, preg, 2)
        st["preg"], st["roff"] = preg, roff


def _emit_items(K, eng, items, st):
    for it in items:
        if it and it[0] == "pad":
            eng.sem_inc(it[1].h, it[2])
        elif it and it[0] == "br":
            _ensure_regs(K, eng, st)
            with eng.If_eq(st["preg"], 0):
                _emit_items(K, eng, it[1], st)
            with eng.Else():
                _emit_items(K, eng, it[2], st)
        else:
            ws, fn, ctr, by = it
            for c, v in ws:
                eng.wait_ge(c.h, v)
            if getattr(fn, "_needs_regs", False):
                _ensure_regs(K, eng, st)
                inst = fn(eng, st["preg"], st["roff"])
            else:
                inst = fn(eng)
            if ctr is not None:
                inst.then_inc(ctr.h, by)


def _emit_engine(K, eng, nm):
    P = K.P
    items = P.st[nm].items
    st = {"preg": None, "roff": None, "nm": nm,
          "pidw": getattr(P, "pidw_new", list(P.bufs["pidsb"].w))}
    _emit_items(K, eng, items, st)


def make_K(debug=()):
    nc = bass.Bass("TRN2", target_bir_lowering=False)
    K = types.SimpleNamespace(nc=nc, dbg=set(debug))
    K.P = Prog(nc)

    K.x_d = nc.dram_tensor("x", [N, HID], F32, kind="ExternalInput").ap()
    K.xown_d = nc.dram_tensor("xown", [1024, HID], F32, kind="ExternalInput").ap()
    K.wq_d = nc.dram_tensor("wq", [HID, 2048], F32, kind="ExternalInput").ap()
    K.wk_d = nc.dram_tensor("wk", [HID, 1024], F32, kind="ExternalInput").ap()
    K.wv_d = nc.dram_tensor("wv", [HID, 1024], F32, kind="ExternalInput").ap()
    K.wo_d = nc.dram_tensor("wo", [2048, HID], F32, kind="ExternalInput").ap()
    K.wg_d = nc.dram_tensor("wg", [D, 1], F32, kind="ExternalInput").ap()
    K.bg_d = nc.dram_tensor("bg", [8, 1], F32, kind="ExternalInput").ap()
    K.gw_d = nc.dram_tensor("gatew", [HID, 8], F32, kind="ExternalInput").ap()
    K.gb_d = nc.dram_tensor("gateb", [1, 8], F32, kind="ExternalInput").ap()
    K.ew_d = nc.dram_tensor("expw", [8, HID, 2048], F32, kind="ExternalInput").ap()
    K.eb_d = nc.dram_tensor("expb", [8, 2048], F32, kind="ExternalInput").ap()
    K.fmask2_d = nc.dram_tensor("fmask2", [128, 4096], BF16, kind="ExternalInput").ap()
    K.pmask_d = nc.dram_tensor("pmask", [128, 256], U8, kind="ExternalInput").ap()
    K.out_d = nc.dram_tensor("out", [1024, HID], F32, kind="ExternalOutput").ap()
    def scratch(name, shape):
        kind = "ExternalOutput" if name in K.dbg else "Internal"
        return nc.dram_tensor(name, shape, BF16, kind=kind).ap()

    K.xfm_h = nc.dram_tensor("xfm_h", [NT, 128, HID], BF16,
                             kind="ExternalOutput").ap()
    K.kfm_h = scratch("kfm_h", [128, HKV * N])
    K.qfm_h = scratch("qfm_h", [128, H * 1024])

    def at(name, shape, dtype, off):
        assert off % 32 == 0, (name, off)
        per = int(np.prod(shape[1:])) * mybir.dt.size(dtype)
        assert off + per <= SBUF_LIMIT, (name, off, per)
        return nc.alloc_sbuf_tensor_at(name, shape, dtype, offset=SBUF_BASE + off).ap()

    o = [0]

    def nx(sz):
        r = o[0]
        o[0] += (sz + 31) // 32 * 32
        return r

    K.id_b = at("id_b", [128, 128], BF16, nx(256))
    K.id_f = at("id_f", [128, 128], F32, nx(512))
    K.ones1_b = at("ones1_b", [1, 128], BF16, nx(256))
    K.ones1_f = at("ones1_f", [1, 128], F32, nx(512))
    K.ones128_b = at("ones128_b", [128, 1], BF16, nx(32))
    K.mstr = at("mstr", [128, 128], F32, nx(512))
    K.mustr = at("mustr", [128, 128], F32, nx(512))
    K.muincs = at("muincs", [128, 128], F32, nx(512))
    K.cumT = at("cumT", [128, 128], F32, nx(512))
    K.lall = at("lall", [128, 128], F32, nx(512))
    K.slall = at("slall", [128, 128], F32, nx(512))
    K.lcb = at("lcb", [128, 128], F32, nx(512))
    K.ccb = at("ccb", [128, 128], F32, nx(512))
    K.sstate = at("sstate", [128, 8 * 128], BF16, nx(2048))
    K.pmasksb = at("pmasksb", [128, 256], U8, nx(256))
    K.cum = at("cum", [8, N], F32, nx(8192))
    K.pidsb = at("pidsb", [1, 1], U32, nx(32))
    K.bgsb = at("bgsb", [8, 1], F32, nx(32))
    K.wgsb = at("wgsb", [128, 1], BF16, nx(32))
    K.ln_a = at("ln_a", [128, 1], F32, nx(32))
    K.ln_b = at("ln_b", [128, 1], F32, nx(32))
    K.ln_c = at("ln_c", [128, 1], F32, nx(32))
    K.ln_d = at("ln_d", [128, 1], F32, nx(32))
    K.ln_e = at("ln_e", [128, 1], F32, nx(32))
    K.dn4 = at("dn4", [128, 8], F32, nx(64))
    K.dn4r = at("dn4r", [128, 8], F32, nx(64))
    K.gt1 = at("gt1", [128, 8], F32, nx(64))
    K.gt2 = at("gt2", [128, 8], F32, nx(64))
    K.gm1 = at("gm1", [128, 1], F32, nx(32))
    K.gm2 = at("gm2", [128, 1], F32, nx(32))
    K.cpool = at("cpool", [128, 2], F32, nx(64))
    K.nopsc = at("nopsc", [1, 8], F32, nx(32))
    nc.const_aps.register(F32, EPS, K.cpool[:, 0:1])
    nc.const_aps.register(F32, 1.0, K.cpool[:, 1:2])

    PEND = (o[0] + 1023) // 1024 * 1024
    R1 = PEND
    R2 = R1 + 32768
    R3 = R2 + 32768
    R4 = R3 + 32768
    R5 = R4 + 32768
    R6 = R5 + 32768
    assert R6 + 22528 <= SBUF_LIMIT, (PEND, R6)
    K.R6_avail = SBUF_LIMIT - R6

    K.ks_tm = at("ks_tm", [128, NT * 1024], BF16, R1)
    K.qfm_bk = at("qfm_bk", [128, H * 1024], BF16, R1)
    K.x1a = at("x1a", [128, 4 * 2048], F32, R1)
    K.v_tm = at("v_tm", [128, NT * 1024], BF16, R2)
    K.xh2_fm = at("xh2_fm", [128, 16 * 1024], BF16, R2)
    K.woslab = [at("woslab0", [128, 16 * 512], BF16, R2),
                at("woslab1", [128, 16 * 512], BF16, R2 + 16384)]
    K.xcb = at("xcb", [128, 2048], BF16, R3)
    K.expst = at("expst", [128, 512], BF16, R3 + 4096)
    K.kfm_st = at("kfm_st", [128, 1024], BF16, R3 + 5120)
    K.t2b = at("t2b", [8, N], F32, R3 + 7168)
    K.gsb = at("gsb", [8, N], F32, R3 + 15360)
    K.fla = at("fla", [128, 8 * 2048], BF16, R3)
    K.weslab = [at("weslab0", [128, 16 * 512], BF16, R3),
                at("weslab1", [128, 16 * 512], BF16, R3 + 16384)]
    K.wkv = at("wkv", [128, 16 * 2048], BF16, R4)
    K.qs_fm = at("qs_fm", [128, H * 1024], BF16, R4)
    K.base_fm = at("base_fm", [128, H * 1024], BF16, R4)
    K.xpiece = at("xpiece", [128, 16 * 512], BF16, R5)
    K.wqslab = [at("wqslab0", [128, 16 * 512], BF16, R5 + 16384),
                at("wqslab1", [128, 16 * 512], BF16, R6)]
    K.kfm_bk = at("kfm_bk", [128, HKV * N], BF16, R5)
    K.x1b = at("x1b", [128, 4 * 2048], F32, R5)
    ds = [R5]

    def dn(sz):
        r = ds[0]
        ds[0] += (sz + 31) // 32 * 32
        return r

    K.ksfm_c = at("ksfm_c", [128, 128], BF16, dn(256))
    K.a_sb = at("a_sb", [128, 128], BF16, dn(256))
    K.at_sb = at("at_sb", [128, 128], BF16, dn(256))
    K.ap_sb = [at("ap_sb0", [128, 128], BF16, dn(256)),
               at("ap_sb1", [128, 128], BF16, dn(256))]
    K.apt_sb = [at("apt_sb0", [128, 128], BF16, dn(256)),
                at("apt_sb1", [128, 128], BF16, dn(256))]
    K.z_sb = [at("z_sb0", [128, 128], BF16, dn(256)),
              at("z_sb1", [128, 128], BF16, dn(256))]
    K.lk_sb = at("lk_sb", [128, 128], BF16, dn(256))
    K.kp_sb = at("kp_sb", [128, 128], BF16, dn(256))
    K.nmt_sb = at("nmt_sb", [128, 128], BF16, dn(256))
    K.uc_sb = at("uc_sb", [128, 128], BF16, dn(256))
    K.oa_sb = at("oa_sb", [128, 128], BF16, dn(256))
    K.bt_sb = at("bt_sb", [128, 128], BF16, dn(256))
    K.e1 = at("e1", [128, 128], F32, dn(512))
    K.e2 = at("e2", [128, 128], F32, dn(512))
    K.e1m = at("e1m", [128, 128], F32, dn(512))
    K.e2m = at("e2m", [128, 128], F32, dn(512))
    K.ebt = at("ebt", [128, 128], F32, dn(512))
    K.dn1 = at("dn1", [128, 128], F32, dn(512))
    K.kdcol = at("kdcol", [128, 1], F32, dn(32))
    K.kpcol = at("kpcol", [128, 1], F32, dn(32))
    K.cumrow = at("cumrow", [1, 1024], F32, dn(4096))
    K.dtmp = at("dtmp", [128, 128], BF16, dn(256))
    assert ds[0] <= R5 + 32768

    K.xt = [at("xt0", [128, 2048], F32, R6),
            at("xt1", [128, 2048], F32, R6 + 8192)]
    K.sqsc = at("sqsc", [128, 2048], F32, R6 + 16384)
    K.kraw_st = at("kraw_st", [128, 1024], F32, R6 + 24576)
    K.gstage = at("gstage", [1, 1024], F32, R6 + 24576)
    K.t1b = at("t1b", [8, N], F32, R6)
    K.rmask = at("rmask", [8, N], F32, R6 + 8192)
    K.pt_all = at("pt_all", [128, 16 * 512], BF16, R6)
    K.rb_sb = at("rb_sb", [128, 512], BF16, R6 + 16384)
    K.rl_sb = at("rl_sb", [1, 512], BF16, R6 + 17408)
    K.fmask2sb = at("fmask2sb", [128, 4096], BF16, R6 + 18432)
    K.pt_tmp = at("pt_tmp", [128, 512], BF16, R6 + 26624)
    K.ln2sc = at("ln2sc", [128, 2048], F32, R6)
    K.berow = [at("berow0", [1, 2048], BF16, R6 + 8192),
               at("berow1", [1, 2048], BF16, R6 + 12288)]
    K.wcols = at("wcols", [128, 64], F32, R6 + 16384)
    K.gatewsb = at("gatewsb", [128, 128], BF16, R6 + 16672)
    K.gatebsb = at("gatebsb", [1, 8], BF16, R6 + 16960)
    K.ln2st = at("ln2st", [128, 2048], BF16, R6 + 20480)
    K.ffnu = at("ffnu", [128, 512], BF16, R6 + 24576)
    K.qse = at("qse", [128, 512], BF16, R6 + 17408)
    K.qse2 = at("qse2", [128, 512], BF16, R6 + 18432)
    K.rde = at("rde", [1, 512], BF16, R6 + 19456)

    K.P.nopfns = {
        "t": lambda e: e.matmul(K.pb[0][0:1, 0:1], K.ones1_b[0:1, 0:1],
                                K.ones1_b[0:1, 0:1], start=True, stop=True),
        "v": lambda e: e.memset(K.nopsc[0:1, 0:1], 0.0),
        "s": lambda e: e.memzero(K.nopsc[0:1, 1:2]),
        "gp": lambda e: e.memset(K.nopsc[0:1, 2:3], 0.0),
        "sy": lambda e: e.nop(),
    }
    K.pb = [nc.alloc_psum_tensor(f"pb{i}", [128, 512], F32).ap() for i in range(7)]
    K.pb7b = nc.alloc_psum_tensor("pb7b", [128, 1024], BF16).ap()
    K.pbn = [f"pb{i}" for i in range(7)] + ["pb7b"]
    return K


def ph_consts(K):
    P = K.P

    def g(fn, **kw):
        P.op("gp", fn, **kw)

    def mask_init(ap, nm, fill_val, cmp, base, pattern, chm):
        g(lambda e: e.memset(ap[:, :], fill_val), writes=(nm,))
        g(lambda e: e.affine_select(out=ap[:, :], in_=ap[:, :], compare_op=cmp,
                                    fill=0.0, base=base, pattern=pattern,
                                    channel_multiplier=chm), writes=(nm,))

    g(lambda e: e.memset(K.id_b[:, :], 0.0), writes=("id_b",))
    g(lambda e: e.affine_select(out=K.id_b[:, :], in_=K.id_b[:, :], compare_op=OP.not_equal,
                                fill=1.0, base=0, pattern=[[-1, 128]], channel_multiplier=1),
      writes=("id_b",))
    g(lambda e: e.memset(K.id_f[:, :], 0.0), writes=("id_f",))
    g(lambda e: e.affine_select(out=K.id_f[:, :], in_=K.id_f[:, :], compare_op=OP.not_equal,
                                fill=1.0, base=0, pattern=[[-1, 128]], channel_multiplier=1),
      writes=("id_f",))
    g(lambda e: e.memset(K.ones1_b[:, :], 1.0), writes=("ones1_b",))
    g(lambda e: e.memset(K.ones1_f[:, :], 1.0), writes=("ones1_f",))
    g(lambda e: e.memset(K.ones128_b[:, :], 1.0), writes=("ones128_b",))
    mask_init(K.mstr, "mstr", 1.0, OP.is_gt, 0, [[-1, 128]], 1)
    mask_init(K.mustr, "mustr", 1.0, OP.is_gt, 0, [[1, 128]], -1)
    mask_init(K.muincs, "muincs", SCALE, OP.is_ge, 0, [[1, 128]], -1)
    g(lambda e: e.memset(K.sstate[:, :], 0.0), writes=("sstate",))
    g(lambda e: e.memset(K.cpool[:, 0:1], EPS), writes=("cpool",))
    g(lambda e: e.memset(K.cpool[:, 1:2], 1.0), writes=("cpool",))

    P.pid_tok_old = [P.op("sy", lambda e: e.dma_start(
        out=K.pidsb[:, :], in_=K.nc.partition_id_tensor.ap()[0:1, 0:1]),
        writes=("pidsb",))]
    P.op("sy", lambda e: e.dma_start(out=K.bgsb[:, :], in_=K.bg_d[:, :]), writes=("bgsb",))
    P.op("gp", lambda e: e.dma_start(out=K.wgsb[:, :], in_=K.wg_d[:, :]),
         writes=("wgsb",), dma=True)
    P.op("gp", lambda e: e.dma_start(
        out=K.wkv.rearrange("p (c n) -> p c n", n=2048)[:, :, 0:1024],
        in_=K.wk_d.rearrange("(c p) n -> p c n", p=128)), writes=("wkv",), dma=True)
    P.op("gp", lambda e: e.dma_start(
        out=K.wkv.rearrange("p (c n) -> p c n", n=2048)[:, :, 1024:2048],
        in_=K.wv_d.rearrange("(c p) n -> p c n", p=128)), writes=("wkv",), dma=True)


def ph_b1(K):
    P = K.P
    pb = K.pb
    for t in range(NT):
        if DEBUG_TILE_BARRIER:
            P.barrier()
        xb = K.xt[t % 2]
        xbuf = f"xt{t % 2}"
        P.op("sy", lambda e, t=t, xb=xb: e.dma_start(out=xb[:, :],
                                                     in_=K.x_d[t * 128:(t + 1) * 128, :]),
             writes=(xbuf,))
        P.op("v", lambda e, xb=xb: e.reduce_sum(K.ln_a[:, :], xb[:, :], axis=AX.X),
             reads=(xbuf,), writes=("ln_a",))
        P.op("s", lambda e: e.activation(K.ln_b[:, :], K.ln_a[:, :], AF.Copy, scale=1.0 / HID),
             reads=("ln_a",), writes=("ln_b",))
        P.op("v", lambda e, xb=xb: e.tensor_scalar(xb[:, :], xb[:, :], K.ln_b[:, :], None,
                                                   OP.subtract),
             reads=("ln_b",), writes=(xbuf,))
        P.op("s", lambda e, xb=xb: e.activation(K.sqsc[:, :], xb[:, :], AF.Square),
             reads=(xbuf,), writes=("sqsc",))
        P.op("v", lambda e: e.reduce_sum(K.ln_c[:, :], K.sqsc[:, :], axis=AX.X),
             reads=("sqsc",), writes=("ln_c",))
        P.op("s", lambda e: e.activation(K.ln_d[:, :], K.ln_c[:, :], AF.Sqrt,
                                         scale=1.0 / HID, bias=EPS),
             reads=("ln_c", "cpool"), writes=("ln_d",))
        P.op("v", lambda e: e.reciprocal(K.ln_e[:, :], K.ln_d[:, :]),
             reads=("ln_d",), writes=("ln_e",))
        P.op("v", lambda e, xb=xb: e.tensor_scalar(xb[:, :], xb[:, :], K.ln_e[:, :],
                                                   None, OP.mult),
             reads=("ln_e",), writes=(xbuf,))
        for grp in range(4):
            bank = grp % 2
            for kk in range(4):
                fc = grp * 4 + kk
                P.op("t", lambda e, xb=xb, fc=fc, bank=bank, kk=kk: e.transpose(
                    pb[bank][:, kk * 128:(kk + 1) * 128],
                    xb[:, fc * 128:(fc + 1) * 128], K.id_f[:, :]),
                    reads=(xbuf, "id_f"), writes=(K.pbn[bank],))
            P.op("s", lambda e, grp=grp, bank=bank: e.copy(
                K.xcb[:, grp * 512:(grp + 1) * 512], pb[bank][:, :]),
                reads=(K.pbn[bank],), writes=("xcb",))
        P.op("sy", lambda e, t=t: e.dma_start(out=K.xfm_h[t], in_=K.xcb[:, :]),
             reads=("xcb",))
        for bi, co in ((2, 0), (3, 512), (4, 1024), (5, 1536)):
            for fc in range(16):
                st, sp = fc == 0, fc == 15
                P.op("t", lambda e, fc=fc, st=st, sp=sp, bi=bi, co=co: e.matmul(
                    pb[bi][:, :], K.xcb[:, fc * 128:(fc + 1) * 128],
                    K.wkv[:, fc * 2048 + co:fc * 2048 + co + 512], start=st, stop=sp),
                    reads=("xcb", "wkv"), writes=(K.pbn[bi],))
        for half in range(2):
            kb, vb = K.pbn[2 + half], K.pbn[4 + half]
            P.op("s", lambda e, half=half: e.copy(
                K.kraw_st[:, half * 512:(half + 1) * 512], pb[2 + half][:, :]),
                reads=(kb,), writes=("gkovl",))
            P.op("s", lambda e, half=half: e.activation(K.expst[:, :], pb[2 + half][:, :], AF.Exp),
                reads=(kb,), writes=("expst",))
            P.op("v", lambda e, half=half: e.reduce_sum(
                K.dn4[:, half * 4:(half + 1) * 4],
                K.expst.rearrange("p (s d) -> p s d", d=128), axis=AX.X),
                reads=("expst",), writes=("dn4",))
            P.op("v", lambda e, half=half: e.reciprocal(
                K.dn4r[:, half * 4:(half + 1) * 4], K.dn4[:, half * 4:(half + 1) * 4]),
                reads=("dn4",), writes=("dn4r",))
            for s4 in range(4):
                j = half * 4 + s4
                P.op("v", lambda e, t=t, j=j, s4=s4: e.tensor_scalar(
                    K.ks_tm[:, t * 1024 + j * 128:t * 1024 + (j + 1) * 128],
                    K.expst[:, s4 * 128:(s4 + 1) * 128], K.dn4r[:, j:j + 1], None, OP.mult),
                    reads=("expst", "dn4r"), writes=("ks_tm",))
            P.op("s", lambda e, t=t, half=half: e.copy(
                K.v_tm[:, t * 1024 + half * 512:t * 1024 + (half + 1) * 512],
                pb[4 + half][:, :]),
                reads=(vb,), writes=("v_tm",))
        for grp in range(2):
            for kk in range(4):
                j = grp * 4 + kk
                P.op("t", lambda e, j=j, kk=kk: e.transpose(
                    pb[6][:, kk * 128:(kk + 1) * 128],
                    K.kraw_st[:, j * 128:(j + 1) * 128], K.id_f[:, :]),
                    reads=("gkovl", "id_f"), writes=(K.pbn[6],))
            P.op("s", lambda e, grp=grp: e.copy(
                K.kfm_st[:, grp * 512:(grp + 1) * 512], pb[6][:, :]),
                reads=(K.pbn[6],), writes=("kfm_st",))
        P.op("sy", lambda e, t=t: e.dma_start(
            out=K.kfm_h.rearrange("p (j n) -> p j n", n=N)[:, :, t * 128:(t + 1) * 128],
            in_=K.kfm_st.rearrange("p (j n) -> p j n", n=128)),
            reads=("kfm_st",))
        for j in range(8):
            bi = 2 if j < 4 else 3
            P.op("t", lambda e, j=j, bi=bi: e.matmul(
                K.pb[bi][0:1, (j % 4) * 128:((j % 4) + 1) * 128], K.wgsb[:, :],
                K.kfm_st[:, j * 128:(j + 1) * 128], start=True, stop=True),
                reads=("kfm_st", "wgsb"), writes=(K.pbn[bi],))
        P.op("s", lambda e: e.copy(K.gstage[:, 0:512], pb[2][0:1, 0:512]),
             reads=(K.pbn[2], "kfm_st"), writes=("gkovl",))
        P.op("s", lambda e: e.copy(K.gstage[:, 512:1024], pb[3][0:1, 0:512]),
             reads=(K.pbn[3],), writes=("gkovl",))
        P.op("sy", lambda e, t=t: e.dma_start(
            out=K.gsb[0:8, t * 128:(t + 1) * 128],
            in_=K.gstage[0:1, 0:1024]),
            reads=("gkovl",), writes=("gsb",))


def ph_prologue(K):
    P = K.P
    if "v_early" in K.dbg:
        d = K.nc.dram_tensor("dbg_v_early", [128, NT * 1024], BF16,
                             kind="ExternalOutput").ap()
        P.op("sy", lambda e, d=d: e.dma_start(out=d, in_=K.v_tm[:, :]), reads=("v_tm",))
    pb = K.pb
    P.barrier()
    P.op("gp", lambda e: e.memset(K.rmask[:, :], 1.0), writes=("rmask",))
    P.op("gp", lambda e: e.affine_select(out=K.rmask[:, :], in_=K.rmask[:, :],
                                         compare_op=OP.is_gt, fill=0.0, base=0,
                                         pattern=[[0, 16], [1, 128]], channel_multiplier=0),
         writes=("rmask",))
    if "rmask_early" in K.dbg:
        dearly = K.nc.dram_tensor("dbg_rmask_early", [8, N], F32,
                                  kind="ExternalOutput").ap()
        P.op("sy", lambda e, d=dearly: e.dma_start(out=d, in_=K.rmask[:, :]),
             reads=("rmask",))
    P.op("v", lambda e: e.tensor_scalar(K.gsb[:, :], K.gsb[:, :], K.bgsb[:, :], None, OP.add),
         reads=("bgsb",), writes=("gsb",))
    P.op("s", lambda e: e.activation(K.t1b[:, :], K.gsb[:, :], AF.Abs),
         reads=("gsb",), writes=("t1b",))
    P.op("s", lambda e: e.activation(K.t1b[:, :], K.t1b[:, :], AF.Exp, scale=-1.0),
         reads=("t1b",), writes=("t1b",))
    P.op("s", lambda e: e.activation(K.t2b[:, :], K.t1b[:, :], AF.Ln, bias=1.0),
         reads=("t1b", "cpool"), writes=("t2b",))
    P.op("v", lambda e: e.tensor_scalar(K.t1b[:, :], K.gsb[:, :], 0.0, None, OP.min),
         reads=("gsb",), writes=("t1b",))
    P.op("v", lambda e: e.tensor_sub(K.t1b[:, :], K.t1b[:, :], K.t2b[:, :]),
         reads=("t1b", "t2b"), writes=("t1b",))
    P.op("s", lambda e: e.activation(K.gsb[:, :], K.t1b[:, :], AF.Copy, scale=1.0 / 16.0),
         reads=("t1b",), writes=("gsb",))
    P.op("v", lambda e: e.tensor_tensor_scan(K.cum[:, :], K.rmask[:, :], K.gsb[:, :], 0.0,
                                             OP.mult, OP.add),
         reads=("rmask", "gsb"), writes=("cum",))
    for c in range(16):
        P.op("t", lambda e, c=c: e.transpose(pb[0][0:128, 0:8],
                                             K.cum[0:8, c * 128:(c + 1) * 128],
                                             K.id_f[0:8, 0:8]),
             reads=("cum", "id_f"), writes=(K.pbn[0],))
        P.op("s", lambda e, c=c: e.copy(K.cumT[:, c * 8:(c + 1) * 8], pb[0][0:128, 0:8]),
             reads=(K.pbn[0],), writes=("cumT",))
    P.op("s", lambda e: e.activation(K.lall[:, :], K.cumT[:, :], AF.Exp),
         reads=("cumT",), writes=("lall",))
    P.op("v", lambda e: e.tensor_scalar(K.slall[:, :], K.lall[:, :], SCALE, None, OP.mult),
         reads=("lall",), writes=("slall",))
    P.op("sy", lambda e: e.dma_start(out=K.cumrow[0:1, 0:128], in_=K.cumT[127:128, :]),
         reads=("cumT",), writes=("cumrow",))
    P.op("t", lambda e: e.matmul(pb[1][:, 0:128], K.ones1_f[:, :], K.cumrow[0:1, 0:128],
                                 start=True, stop=True),
         reads=("ones1_f", "cumrow"), writes=(K.pbn[1],))
    P.op("s", lambda e: e.copy(K.ccb[:, :], pb[1][:, 0:128]), reads=(K.pbn[1],), writes=("ccb",))
    P.op("s", lambda e: e.activation(K.lcb[:, :], pb[1][:, 0:128], AF.Exp),
         reads=(K.pbn[1],), writes=("lcb",))


def dbg_dump(K, names):
    P = K.P
    for nm in names:
        if not hasattr(K, nm):
            continue
        ap = getattr(K, nm)
        d = K.nc.dram_tensor("dbg_" + nm, list(ap.shape), ap.dtype,
                             kind="ExternalOutput").ap()
        P.op("sy", lambda e, d=d, ap=ap: e.dma_start(out=d, in_=ap), reads=(nm,))


def _collect_waited(items, used):
    for it in items:
        if it and it[0] == "pad":
            continue
        if it and it[0] == "br":
            _collect_waited(it[1], used)
            _collect_waited(it[2], used)
            continue
        ws, fn, ctr, by = it
        for c, v in ws:
            used.setdefault(id(c), set()).add(v)


def _number_incs(items, used, newv, cnt, keep_all=False):
    out = []
    for it in items:
        if it and it[0] == "pad":
            _, c, amt = it
            st = cnt.setdefault(id(c), [0, 0])
            u = used.get(id(c), ())
            kept = sum(1 for v in range(st[0] + 1, st[0] + amt + 1) if v in u)
            for v in range(st[0] + 1, st[0] + amt + 1):
                st_new = st[1] + sum(1 for q in range(st[0] + 1, v + 1) if q in u)
                newv.setdefault(id(c), {})[v] = st_new
            st[0] += amt
            st[1] += kept
            if kept:
                out.append(("pad", c, kept))
            continue
        if it and it[0] == "br":
            snap = {k: [vv[0], vv[1]] for k, vv in cnt.items()}
            a = _number_incs(it[1], used, newv, cnt, keep_all)
            enda = {k: [vv[0], vv[1]] for k, vv in cnt.items()}
            for k in list(cnt.keys()):
                cnt[k] = [snap.get(k, [0, 0])[0], snap.get(k, [0, 0])[1]]
            b = _number_incs(it[2], used, newv, cnt, keep_all)
            for k in set(cnt) | set(enda):
                ca = enda.get(k, [0, 0])
                cb = cnt.get(k, [0, 0])
                assert ca == cb or k not in enda or k not in cnt, (ca, cb)
                cnt[k] = [max(ca[0], cb[0]), max(ca[1], cb[1])]
            out.append(("br", a, b))
            continue
        ws, fn, ctr, by = it
        keep = False
        if ctr is not None:
            st = cnt.setdefault(id(ctr), [0, 0])
            st[0] += by
            if keep_all or st[0] in used.get(id(ctr), ()):
                st[1] += by
                keep = True
            newv.setdefault(id(ctr), {})[st[0]] = st[1]
        out.append((ws, fn, ctr if keep else None, by))
    return out


def _rewrite_waits(items, newv):
    out = []
    for it in items:
        if it and it[0] == "pad":
            out.append(it)
        elif it and it[0] == "br":
            out.append(("br", _rewrite_waits(it[1], newv),
                        _rewrite_waits(it[2], newv)))
        else:
            ws, fn, ctr, by = it
            out.append(([(c, newv[id(c)][v]) for c, v in ws], fn, ctr, by))
    return out


def _prune_incs(P, force_used=()):
    used = {}
    for st in P.st.values():
        _collect_waited(st.items, used)
    for c, v in force_used:
        used.setdefault(id(c), set()).add(v)
    newv = {}
    cnt = {}
    for snm in P.st:
        P.st[snm].items = _number_incs(P.st[snm].items, used, newv, cnt,
                                       keep_all=snm in ("sy", "gp"))
    for snm in P.st:
        P.st[snm].items = _rewrite_waits(P.st[snm].items, newv)
    return newv


def finish(K):
    P = K.P
    fin = (P.ctrs["sy"], P.ctrs["sy"].n)
    pidw_old = list(P.pid_tok_old)
    newv = _prune_incs(P, force_used=[fin] + pidw_old)
    K._final_wait = (P.ctrs["sy"].h, newv[id(P.ctrs["sy"])][fin[1]])
    P.pidw_new = [(c, newv[id(c)][v]) for c, v in pidw_old]
    with K.nc.Block() as block:

        @block.tensor
        def _(eng):
            _emit_engine(K, eng, "t")

        @block.vector
        def _(eng):
            _emit_engine(K, eng, "v")

        @block.scalar
        def _(eng):
            _emit_engine(K, eng, "s")

        @block.gpsimd
        def _(eng):
            _emit_engine(K, eng, "gp")

        @block.sync
        def _(eng):
            _emit_engine(K, eng, "sy")
            eng.wait_ge(*K._final_wait)
    return K.nc


def shard_inputs(inputs):
    x = np.ascontiguousarray(inputs["L0_attn_hidden_states"], dtype=np.float32)
    maps = []
    import ml_dtypes
    tril = np.triu(np.ones((128, 128), np.float32))
    for core in range(8):
        b, h = core // 2, core % 2
        rows = np.concatenate([np.arange((2 * u + h) * 128, (2 * u + h) * 128 + 128)
                               for u in range(8)])
        own = np.ascontiguousarray(x[b, rows])
        fm = np.zeros((128, 8, 4, 128), np.float32)
        for r in range(8):
            for v in range(4):
                tv = 2 * v + h
                if r < tv:
                    fm[:, r, v, :] = 1.0
                elif r == tv:
                    fm[:, r, v, :] = tril
        pm = np.zeros((128, 256), np.uint8)
        pm[:, 0:128] = 1 if h == 0 else 0
        pm[:, 128:256] = 0 if h == 0 else 1
        maps.append({
            "fmask2": np.ascontiguousarray(
                fm.reshape(128, 4096).astype(ml_dtypes.bfloat16)),
            "pmask": pm,
            "x": np.ascontiguousarray(x[b]),
            "xown": own,
            "wq": np.ascontiguousarray(inputs["L0_attn_Wq"], np.float32),
            "wk": np.ascontiguousarray(inputs["L0_attn_Wk"], np.float32),
            "wv": np.ascontiguousarray(inputs["L0_attn_Wv"], np.float32),
            "wo": np.ascontiguousarray(inputs["L0_attn_Wo"], np.float32),
            "wg": np.ascontiguousarray(inputs["L0_attn_Wg"], np.float32),
            "bg": np.full((8, 1), np.float32(np.asarray(inputs["L0_attn_bg"]).reshape(-1)[0])),
            "gatew": np.ascontiguousarray(inputs["L0_ffn_gate_w"], np.float32),
            "gateb": np.ascontiguousarray(inputs["L0_ffn_gate_b"], np.float32).reshape(1, 8),
            "expw": np.ascontiguousarray(inputs["L0_ffn_expert_w"], np.float32),
            "expb": np.ascontiguousarray(inputs["L0_ffn_expert_b"], np.float32),
        })
    return maps


def assemble_out(results):
    out = np.empty((B, N, HID), np.float32)
    for core in range(8):
        b, h = core // 2, core % 2
        r = results[core]["out"]
        for u in range(8):
            out[b, (2 * u + h) * 128:(2 * u + h) * 128 + 128] = r[u * 128:(u + 1) * 128]
    return out


def ph_b2(K):
    P = K.P
    pb = K.pb
    P.barrier()
    for p in range(2):
        if STATIC_PIECE:
            P.op("gp", lambda e, p=p: e.dma_start(
                out=K.xpiece.rearrange("q (c t n) -> q c t n", t=4, n=128),
                in_=bass.AP(K.xfm_h.tensor, 8 * p * 262144,
                            [[2048, 128], [128, 16], [524288, 4], [1, 128]])),
                writes=("xpiece",), dma=True)
        else:
            def piece_load(e, preg, roff, p=p):
                e.reg_mov(roff, preg)
                e.reg_mul(roff, roff, 262144)
                e.reg_add(roff, roff, 8 * p * 262144)
                return e.dma_start(
                    out=K.xpiece.rearrange("q (c t n) -> q c t n", t=4, n=128),
                    in_=bass.AP(K.xfm_h.tensor, roff,
                                [[2048, 128], [128, 16], [524288, 4], [1, 128]]))
            piece_load._needs_regs = True
            P.op("gp", piece_load, writes=("xpiece",), dma=True)
        for fg in range(4):
            slab = K.wqslab[fg % 2]
            sbuf = f"wqslab{fg % 2}"
            P.op("gp", lambda e, slab=slab, fg=fg: e.dma_start(
                out=slab.rearrange("q (c n) -> q c n", n=512),
                in_=K.wq_d.rearrange("(c q) n -> q c n", q=128)[:, :, fg * 512:(fg + 1) * 512]),
                writes=(sbuf,), dma=True)
            for hp in range(4):
                h = fg * 4 + hp
                bank = hp % 2
                for fc in range(16):
                    P.op("t", lambda e, slab=slab, hp=hp, fc=fc, bank=bank: e.matmul(
                        pb[bank][:, :], slab[:, fc * 512 + hp * 128:fc * 512 + (hp + 1) * 128],
                        K.xpiece[:, fc * 512:(fc + 1) * 512],
                        start=fc == 0, stop=fc == 15),
                        reads=(sbuf, "xpiece"), writes=(K.pbn[bank],))
                P.op("s", lambda e, bank=bank: e.copy(K.qse[:, :], pb[bank][:, :]),
                     reads=(K.pbn[bank],), writes=("qse",))
                P.op("sy", lambda e, h=h, p=p: e.dma_start(
                    out=K.qfm_h[:, h * 1024 + p * 512:h * 1024 + (p + 1) * 512],
                    in_=K.qse[:, :]), reads=("qse",))
                P.op("s", lambda e, bank=bank: e.activation(K.qse2[:, :], pb[bank][:, :],
                                                            AF.Exp),
                     reads=(K.pbn[bank],), writes=("qse2",))
                P.op("t", lambda e: e.matmul(pb[2][0:1, :], K.ones128_b[:, :], K.qse2[:, :],
                                             start=True, stop=True),
                     reads=("ones128_b", "qse2"), writes=(K.pbn[2],))
                def _recip(e):
                    with K.nc.allow_low_precision(reason="softmax denom bf16"):
                        return e.reciprocal(K.rde[0:1, :], pb[2][0:1, :])
                P.op("v", _recip, reads=(K.pbn[2],), writes=("rde",))
                P.op("t", lambda e: e.matmul(pb[3][:, :], K.ones1_b[:, :], K.rde[0:1, :],
                                             start=True, stop=True),
                     reads=("ones1_b", "rde"), writes=(K.pbn[3],))
                P.op("v", lambda e, h=h, p=p: e.tensor_tensor(
                    K.qs_fm[:, h * 1024 + p * 512:h * 1024 + (p + 1) * 512],
                    K.qse2[:, :], pb[3][:, :], OP.mult),
                    reads=("qse2", K.pbn[3]), writes=("qs_fm",))


def ph_delta(K):
    P = K.P
    pb = K.pb
    if DELTA_UPTO >= -3:
        P.barrier()
    if DELTA_UPTO >= -2:
        P.op("sy", lambda e: e.dma_start(out=K.pmasksb[:, :], in_=K.pmask_d[:, :]),
             writes=("pmasksb",))
    ksl = lambda c, j: K.ks_tm[:, c * 1024 + j * 128:c * 1024 + (j + 1) * 128]
    for c in range(DELTA_CHUNKS):
        if DELTA_UPTO >= -1:
            P.op("sy", lambda e, c=c: e.dma_start(
                out=K.cumrow.rearrange("o (j n) -> o j n", n=128),
                in_=K.cum[0:8, c * 128:(c + 1) * 128]),
                reads=("cum",), writes=("cumrow",))
        for j in range(DELTA_J):
            col = slice(c * 8 + j, c * 8 + j + 1)
            cumTc = K.cumT[:, col]
            if DELTA_UPTO < 1:
                continue
            P.op("t", lambda e, c=c, j=j: e.transpose(
                K.pb7b[:, 0:128], ksl(c, j), K.id_b[:, :]),
                reads=("ks_tm", "id_b"), writes=("pb7b",))
            P.op("s", lambda e: e.copy(K.ksfm_c[:, :], K.pb7b[:, 0:128]),
                 reads=("pb7b",), writes=("ksfm_c",))
            if DELTA_UPTO < 2:
                continue
            P.op("t", lambda e, j=j: e.matmul(
                pb[1][:, 0:128], K.ones1_f[:, :], K.cumrow[0:1, j * 128:(j + 1) * 128],
                start=True, stop=True),
                reads=("ones1_f", "cumrow"), writes=(K.pbn[1],))
            P.op("v", lambda e, cumTc=cumTc: e.tensor_scalar(
                K.dn1[:, :], pb[1][:, 0:128], cumTc, 0.0, OP.subtract, OP.max),
                reads=(K.pbn[1], "cumT"), writes=("dn1",))
            P.op("s", lambda e: e.activation(K.e1[:, :], K.dn1[:, :], AF.Exp, scale=-1.0),
                 reads=("dn1",), writes=("e1",))
            P.op("v", lambda e, cumTc=cumTc: e.tensor_scalar(
                K.dn1[:, :], pb[1][:, 0:128], cumTc, 0.0, OP.subtract, OP.min),
                reads=(K.pbn[1], "cumT", "e1"), writes=("dn1",))
            P.op("s", lambda e: e.activation(K.e2[:, :], K.dn1[:, :], AF.Exp),
                 reads=("dn1",), writes=("e2",))
            P.op("v", lambda e: e.tensor_tensor(K.e1m[:, :], K.e1[:, :], K.mstr[:, :],
                                                OP.mult),
                 reads=("e1", "mstr"), writes=("e1m",))
            P.op("v", lambda e: e.tensor_tensor(K.e2m[:, :], K.e2[:, :], K.mustr[:, :],
                                                OP.mult),
                 reads=("e2", "mustr"), writes=("e2m",))
            P.op("v", lambda e: e.tensor_tensor(K.ebt[:, :], K.e2[:, :], K.muincs[:, :],
                                                OP.mult),
                 reads=("e2", "muincs"), writes=("ebt",))
            if DELTA_UPTO < 3:
                continue
            P.op("t", lambda e: e.matmul(pb[0][:, 0:128], K.ksfm_c[:, :], K.ksfm_c[:, :],
                                         start=True, stop=True),
                 reads=("ksfm_c",), writes=(K.pbn[0],))
            P.op("v", lambda e: e.tensor_tensor(K.a_sb[:, :], pb[0][:, 0:128],
                                                K.e1m[:, :], OP.mult),
                 reads=(K.pbn[0], "e1m"), writes=("a_sb",))
            P.op("v", lambda e: e.tensor_tensor(K.at_sb[:, :], pb[0][:, 0:128],
                                                K.e2m[:, :], OP.mult),
                 reads=(K.pbn[0], "e2m"), writes=("at_sb",))
            if DELTA_UPTO < 4:
                continue
            P.op("v", lambda e: e.scalar_tensor_tensor(
                K.z_sb[0][:, :], K.at_sb[:, :], -1.0, K.id_b[:, :], OP.mult, OP.add),
                reads=("at_sb", "id_b"), writes=("z_sb0",))
            cur, curt = "a_sb", "at_sb"
            curap, curtap = K.a_sb, K.at_sb
            for lvl in range(6):
                zi, zo = lvl % 2, (lvl + 1) % 2
                P.op("t", lambda e, curap=curap, curtap=curtap: e.matmul(
                    pb[2][:, 0:128], curtap[:, :], curap[:, :], start=True, stop=True),
                    reads=(cur, curt), writes=(K.pbn[2],))
                if lvl < 5:
                    P.op("t", lambda e, curap=curap, curtap=curtap: e.matmul(
                        pb[3][:, 0:128], curap[:, :], curtap[:, :], start=True, stop=True),
                        reads=(cur, curt), writes=(K.pbn[3],))
                nap, napt = K.ap_sb[lvl % 2], K.apt_sb[lvl % 2]
                napn, naptn = f"ap_sb{lvl % 2}", f"apt_sb{lvl % 2}"
                P.op("s", lambda e, nap=nap: e.copy(nap[:, :], pb[2][:, 0:128]),
                     reads=(K.pbn[2],), writes=(napn,))
                if lvl < 5:
                    P.op("s", lambda e, napt=napt: e.copy(napt[:, :], pb[3][:, 0:128]),
                         reads=(K.pbn[3],), writes=(naptn,))
                P.op("t", lambda e, nap=nap, zi=zi: e.matmul(
                    pb[4][:, 0:128], nap[:, :], K.z_sb[zi][:, :], start=True, stop=False),
                    reads=(napn, f"z_sb{zi}"), writes=(K.pbn[4],))
                P.op("t", lambda e, zi=zi: e.matmul(
                    pb[4][:, 0:128], K.id_b[:, :], K.z_sb[zi][:, :], start=False, stop=True),
                    reads=("id_b", f"z_sb{zi}"), writes=(K.pbn[4],))
                P.op("s", lambda e, zo=zo: e.copy(K.z_sb[zo][:, :], pb[4][:, 0:128]),
                     reads=(K.pbn[4],), writes=(f"z_sb{zo}",))
                cur, curt = napn, naptn
                curap, curtap = nap, napt
            zf, zfn = K.z_sb[0], "z_sb0"
            if DELTA_UPTO < 5:
                continue
            lallc = K.lall[:, col]
            P.op("v", lambda e, c=c, j=j, lallc=lallc: e.tensor_scalar(
                K.lk_sb[:, :], ksl(c, j), lallc, None, OP.mult),
                reads=("ks_tm", "lall"), writes=("lk_sb",))
            P.op("v", lambda e, cumTc=cumTc, c=c, j=j: e.tensor_scalar(
                K.kdcol[:, :], cumTc, K.ccb[:, c * 8 + j:c * 8 + j + 1], -1.0,
                OP.subtract, OP.mult),
                reads=("cumT", "ccb"), writes=("kdcol",))
            P.op("s", lambda e: e.activation(K.kpcol[:, :], K.kdcol[:, :], AF.Exp),
                 reads=("kdcol",), writes=("kpcol",))
            P.op("v", lambda e, c=c, j=j: e.tensor_scalar(
                K.kp_sb[:, :], ksl(c, j), K.kpcol[:, :], None, OP.mult),
                reads=("ks_tm", "kpcol"), writes=("kp_sb",))
            P.op("t", lambda e, zf=zf: e.matmul(pb[5][:, 0:128], K.lk_sb[:, :], zf[:, :],
                                                start=True, stop=True),
                 reads=("lk_sb", zfn), writes=(K.pbn[5],))
            P.op("s", lambda e: e.activation(K.nmt_sb[:, :], pb[5][:, 0:128], AF.Copy,
                                             scale=-1.0),
                 reads=(K.pbn[5],), writes=("nmt_sb",))
            if DELTA_UPTO < 6:
                continue
            sstate_j = K.sstate[:, j * 128:(j + 1) * 128]
            P.op("t", lambda e, zf=zf, c=c, j=j: e.matmul(
                pb[6][:, 0:128], zf[:, :],
                K.v_tm[:, c * 1024 + j * 128:c * 1024 + (j + 1) * 128],
                start=True, stop=False),
                reads=(zfn, "v_tm"), writes=(K.pbn[6],))
            P.op("t", lambda e, sstate_j=sstate_j: e.matmul(
                pb[6][:, 0:128], K.nmt_sb[:, :], sstate_j, start=False, stop=True),
                reads=("nmt_sb", "sstate"), writes=(K.pbn[6],))
            P.op("v", lambda e: e.tensor_copy(K.uc_sb[:, :], pb[6][:, 0:128]),
                 reads=(K.pbn[6],), writes=("uc_sb",))
            if DELTA_UPTO < 7:
                continue
            u = c // 2
            slallc = K.slall[:, col]
            pmsl = K.pmasksb[:, (c % 2) * 128:(c % 2) * 128 + 128]
            for hh in (2 * j, 2 * j + 1):
                qsl = K.qs_fm[:, hh * 1024 + u * 128:hh * 1024 + (u + 1) * 128]
                P.op("t", lambda e, qsl=qsl: e.matmul(
                    pb[0][:, 0:128], K.ksfm_c[:, :], qsl, start=True, stop=True),
                    reads=("ksfm_c", "qs_fm"), writes=(K.pbn[0],))
                P.op("v", lambda e: e.tensor_tensor(K.bt_sb[:, :], pb[0][:, 0:128],
                                                    K.ebt[:, :], OP.mult),
                     reads=(K.pbn[0], "ebt"), writes=("bt_sb",))
                P.op("t", lambda e, qsl=qsl, sstate_j=sstate_j: e.matmul(
                    pb[2][:, 0:128], qsl, sstate_j, start=True, stop=True),
                    reads=("qs_fm", "sstate"), writes=(K.pbn[2],))
                P.op("s", lambda e, slallc=slallc: e.activation(
                    K.oa_sb[:, :], pb[2][:, 0:128], AF.Copy, scale=slallc),
                    reads=(K.pbn[2], "slall"), writes=("oa_sb",))
                P.op("t", lambda e: e.matmul(pb[3][:, 0:128], K.bt_sb[:, :],
                                             K.uc_sb[:, :], start=True, stop=True),
                     reads=("bt_sb", "uc_sb"), writes=(K.pbn[3],))
                P.op("v", lambda e: e.tensor_tensor(K.dtmp[:, :], K.oa_sb[:, :],
                                                    pb[3][:, 0:128], OP.add),
                     reads=("oa_sb", K.pbn[3]), writes=("dtmp",))
                P.op("v", lambda e, hh=hh, u=u, pmsl=pmsl: e.copy_predicated(
                    K.fla[:, u * 2048 + hh * 128:u * 2048 + (hh + 1) * 128],
                    pmsl, K.dtmp[:, :]),
                    reads=("dtmp", "pmasksb"), writes=("fla",))
            if DELTA_UPTO < 8:
                continue
            lcbc = K.lcb[:, col]
            P.op("t", lambda e: e.matmul(pb[4][:, 0:128], K.kp_sb[:, :], K.uc_sb[:, :],
                                         start=True, stop=True),
                 reads=("kp_sb", "uc_sb"), writes=(K.pbn[4],))
            P.op("v", lambda e, sstate_j=sstate_j, lcbc=lcbc: e.scalar_tensor_tensor(
                sstate_j, sstate_j, lcbc, pb[4][:, 0:128], OP.mult, OP.add),
                reads=("lcb", K.pbn[4]), writes=("sstate",))


def ph_flash(K):
    P = K.P
    pb = K.pb
    P.barrier()
    P.op("sy", lambda e: e.dma_start(out=K.kfm_bk[:, :], in_=K.kfm_h[:, :]),
         writes=("kfm_bk",))
    P.op("sy", lambda e: e.dma_start(out=K.qfm_bk[:, :], in_=K.qfm_h[:, :]),
         writes=("qfm_bk",))
    P.op("sy", lambda e: e.dma_start(out=K.fmask2sb[:, :], in_=K.fmask2_d[:, :]),
         writes=("fmask2sb",))
    for h in range(H):
        j = h // 2
        for p in range(2):
            kext = 8 * p + 8
            qsl = K.qfm_bk[:, h * 1024 + p * 512:h * 1024 + (p + 1) * 512]
            for k in range(kext):
                bank = k % 2
                P.op("t", lambda e, j=j, k=k, bank=bank, qsl=qsl: e.matmul(
                    pb[bank][:, :], K.kfm_bk[:, j * 2048 + k * 128:j * 2048 + (k + 1) * 128],
                    qsl, start=True, stop=True),
                    reads=("kfm_bk", "qfm_bk"), writes=(K.pbn[bank],))
                r = k - 8 * p
                if r >= 0:
                    P.op("s", lambda e, bank=bank: e.activation(
                        K.pt_tmp[:, :], pb[bank][:, :], AF.Exp, scale=SCALE),
                        reads=(K.pbn[bank],), writes=("pt_tmp",))
                    P.op("v", lambda e, k=k, r=r: e.tensor_tensor(
                        K.pt_all[:, k * 512:(k + 1) * 512], K.pt_tmp[:, :],
                        K.fmask2sb[:, r * 512:(r + 1) * 512], OP.mult),
                        reads=("fmask2sb", "pt_tmp"), writes=("pt_all",))
                else:
                    P.op("s", lambda e, k=k, bank=bank: e.activation(
                        K.pt_all[:, k * 512:(k + 1) * 512], pb[bank][:, :], AF.Exp,
                        scale=SCALE),
                        reads=(K.pbn[bank],), writes=("pt_all",))
            for k in range(kext):
                P.op("t", lambda e, j=j, k=k, kext=kext: e.matmul(
                    pb[2][:, :], K.v_tm[:, k * 1024 + j * 128:k * 1024 + (j + 1) * 128],
                    K.pt_all[:, k * 512:(k + 1) * 512],
                    start=k == 0, stop=k == kext - 1),
                    reads=("v_tm", "pt_all"), writes=(K.pbn[2],))
            for k in range(kext):
                P.op("t", lambda e, k=k, kext=kext: e.matmul(
                    pb[3][0:1, :], K.ones128_b[:, :],
                    K.pt_all[:, k * 512:(k + 1) * 512],
                    start=k == 0, stop=k == kext - 1),
                    reads=("ones128_b", "pt_all"), writes=(K.pbn[3],))
            def _recipl(e):
                with K.nc.allow_low_precision(reason="softmax denom bf16"):
                    return e.reciprocal(K.rl_sb[0:1, :], pb[3][0:1, :])
            P.op("v", _recipl, reads=(K.pbn[3],), writes=("rl_sb",))
            P.op("t", lambda e: e.matmul(pb[4][:, :], K.ones1_b[:, :], K.rl_sb[0:1, :],
                                         start=True, stop=True),
                 reads=("ones1_b", "rl_sb"), writes=(K.pbn[4],))
            P.op("s", lambda e: e.copy(K.rb_sb[:, :], pb[4][:, :]),
                 reads=(K.pbn[4],), writes=("rb_sb",))
            P.op("v", lambda e, h=h, p=p: e.tensor_tensor(
                K.base_fm[:, h * 1024 + p * 512:h * 1024 + (p + 1) * 512],
                pb[2][:, :], K.rb_sb[:, :], OP.mult),
                reads=(K.pbn[2], "rb_sb"), writes=("base_fm",))


def x1sl(K, tto, c0=0, c1=2048):
    buf = K.x1a if tto < 4 else K.x1b
    nm = "x1a" if tto < 4 else "x1b"
    return buf[:, (tto % 4) * 2048 + c0:(tto % 4) * 2048 + c1], nm


def ph_wo(K):
    P = K.P
    pb = K.pb
    P.barrier()
    for tto in range(8):
        xs, nm = x1sl(K, tto)
        P.op("sy", lambda e, tto=tto, xs=xs: e.dma_start(
            out=xs, in_=K.xown_d[tto * 128:(tto + 1) * 128, :]), writes=(nm,))
    for fg in range(4):
        slab = K.woslab[fg % 2]
        snm = f"woslab{fg % 2}"
        P.op("gp", lambda e, slab=slab, fg=fg: e.dma_start(
            out=slab.rearrange("q (c n) -> q c n", n=512),
            in_=K.wo_d.rearrange("(c q) n -> q c n", q=128)[:, :, fg * 512:(fg + 1) * 512]),
            writes=(snm,), dma=True)
        for tto in range(8):
            bank = tto % 2
            for fc in range(16):
                P.op("t", lambda e, slab=slab, fc=fc, tto=tto, bank=bank: e.matmul(
                    pb[bank][:, :], K.base_fm[:, fc * 1024 + tto * 128:fc * 1024 + (tto + 1) * 128],
                    slab[:, fc * 512:(fc + 1) * 512], start=fc == 0, stop=fc == 15),
                    reads=("base_fm", snm), writes=(K.pbn[bank],))
            xs, nm = x1sl(K, tto, fg * 512, (fg + 1) * 512)
            P.op("v", lambda e, xs=xs, bank=bank: e.scalar_tensor_tensor(
                xs, pb[bank][:, :], 0.99, xs, OP.mult, OP.add),
                reads=(K.pbn[bank],), writes=(nm,))
            P.op("v", lambda e, xs=xs, tto=tto, fg=fg: e.scalar_tensor_tensor(
                xs, K.fla[:, tto * 2048 + fg * 512:tto * 2048 + (fg + 1) * 512],
                0.01, xs, OP.mult, OP.add),
                reads=("fla",), writes=(nm,))


def ph_ffn(K):
    P = K.P
    pb = K.pb
    P.barrier()
    P.op("gp", lambda e: e.dma_start(
        out=K.gatewsb.rearrange("q (c n) -> q c n", n=8),
        in_=K.gw_d.rearrange("(c q) n -> q c n", q=128)), writes=("gatewsb",), dma=True)
    P.op("gp", lambda e: e.dma_start(out=K.gatebsb[:, :], in_=K.gb_d[:, :]),
         writes=("gatebsb",), dma=True)
    for tto in range(8):
        xs, nm = x1sl(K, tto)
        P.op("v", lambda e, xs=xs: e.reduce_sum(K.ln_a[:, :], xs, axis=AX.X),
             reads=(nm,), writes=("ln_a",))
        P.op("s", lambda e: e.activation(K.ln_b[:, :], K.ln_a[:, :], AF.Copy,
                                         scale=1.0 / HID),
             reads=("ln_a",), writes=("ln_b",))
        P.op("v", lambda e, xs=xs: e.tensor_scalar(K.ln2sc[:, :], xs, K.ln_b[:, :],
                                                   None, OP.subtract),
             reads=(nm, "ln_b"), writes=("ln2sc",))
        P.op("s", lambda e: e.activation(K.ln2st[:, :], K.ln2sc[:, :], AF.Square),
             reads=("ln2sc",), writes=("ln2st",))
        P.op("v", lambda e: e.reduce_sum(K.ln_c[:, :], K.ln2st[:, :], axis=AX.X),
             reads=("ln2st",), writes=("ln_c",))
        P.op("s", lambda e: e.activation(K.ln_d[:, :], K.ln_c[:, :], AF.Sqrt,
                                         scale=1.0 / HID, bias=EPS),
             reads=("ln_c", "cpool"), writes=("ln_d",))
        P.op("v", lambda e: e.reciprocal(K.ln_e[:, :], K.ln_d[:, :]),
             reads=("ln_d",), writes=("ln_e",))
        P.op("v", lambda e: e.tensor_scalar(K.ln2sc[:, :], K.ln2sc[:, :], K.ln_e[:, :],
                                            None, OP.mult),
             reads=("ln_e",), writes=("ln2sc",))
        for grp in range(4):
            bank = grp % 2
            for kk in range(4):
                fc = grp * 4 + kk
                P.op("t", lambda e, fc=fc, bank=bank, kk=kk: e.transpose(
                    pb[bank][:, kk * 128:(kk + 1) * 128],
                    K.ln2sc[:, fc * 128:(fc + 1) * 128], K.id_f[:, :]),
                    reads=("ln2sc", "id_f"), writes=(K.pbn[bank],))
            for kk in range(4):
                fc = grp * 4 + kk
                P.op("s", lambda e, fc=fc, kk=kk, bank=bank, tto=tto: e.copy(
                    K.xh2_fm[:, fc * 1024 + tto * 128:fc * 1024 + (tto + 1) * 128],
                    pb[bank][:, kk * 128:(kk + 1) * 128]),
                    reads=(K.pbn[bank],), writes=("xh2_fm",))
        P.op("t", lambda e: e.matmul(pb[5][:, 0:8], K.ones1_b[:, :],
                                     K.gatebsb[0:1, :], start=True, stop=False),
             reads=("ones1_b", "gatebsb"), writes=(K.pbn[5],))
        for fc in range(16):
            P.op("t", lambda e, fc=fc, tto=tto: e.matmul(
                pb[5][:, 0:8], K.xh2_fm[:, fc * 1024 + tto * 128:fc * 1024 + (tto + 1) * 128],
                K.gatewsb[:, fc * 8:(fc + 1) * 8], start=False, stop=fc == 15),
                reads=("xh2_fm", "gatewsb"), writes=(K.pbn[5],))
        P.op("v", lambda e: e.reduce_max(K.gm1[:, :], pb[5][:, 0:8], axis=AX.X),
             reads=(K.pbn[5],), writes=("gm1",))
        P.op("v", lambda e: e.tensor_scalar(K.gt1[:, :], pb[5][:, 0:8], K.gm1[:, :],
                                            None, OP.is_ge),
             reads=(K.pbn[5], "gm1"), writes=("gt1",))
        P.op("v", lambda e: e.scalar_tensor_tensor(K.gt2[:, :], K.gt1[:, :], -1e30,
                                                   pb[5][:, 0:8], OP.mult, OP.add),
             reads=("gt1", K.pbn[5]), writes=("gt2",))
        P.op("v", lambda e: e.reduce_max(K.gm2[:, :], K.gt2[:, :], axis=AX.X),
             reads=("gt2",), writes=("gm2",))
        P.op("v", lambda e, tto=tto: e.tensor_scalar(
            K.wcols[:, tto * 8:(tto + 1) * 8], pb[5][:, 0:8], K.gm2[:, :], 0.5,
            OP.is_ge, OP.mult),
            reads=(K.pbn[5], "gm2"), writes=("wcols",))
    for e_i in range(8):
        brow = K.berow[e_i % 2]
        bnm = f"berow{e_i % 2}"
        P.op("gp", lambda e, brow=brow, e_i=e_i: e.dma_start(
            out=brow[0:1, :], in_=K.eb_d[e_i:e_i + 1, :]), writes=(bnm,), dma=True)
        for fg in range(4):
            si = (e_i * 4 + fg) % 2
            slab = K.weslab[si]
            snm = f"weslab{si}"
            P.op("gp", lambda e, slab=slab, e_i=e_i, fg=fg: e.dma_start(
                out=slab.rearrange("q (c n) -> q c n", n=512),
                in_=K.ew_d.rearrange("e (c q) n -> e q c n", q=128)[
                    e_i, :, :, fg * 512:(fg + 1) * 512]),
                writes=(snm,), dma=True)
            for tto in range(8):
                bank = 2 + tto % 2
                P.op("t", lambda e, brow=brow, fg=fg, bank=bank: e.matmul(
                    pb[bank][:, :], K.ones1_b[:, :],
                    brow[0:1, fg * 512:(fg + 1) * 512], start=True, stop=False),
                    reads=("ones1_b", bnm), writes=(K.pbn[bank],))
                for fc in range(16):
                    P.op("t", lambda e, slab=slab, fc=fc, tto=tto, bank=bank: e.matmul(
                        pb[bank][:, :],
                        K.xh2_fm[:, fc * 1024 + tto * 128:fc * 1024 + (tto + 1) * 128],
                        slab[:, fc * 512:(fc + 1) * 512], start=False, stop=fc == 15),
                        reads=("xh2_fm", snm), writes=(K.pbn[bank],))
                P.op("s", lambda e, bank=bank, tto=tto, e_i=e_i: e.activation(
                    K.ffnu[:, :], pb[bank][:, :], AF.Relu,
                    scale=K.wcols[:, tto * 8 + e_i:tto * 8 + e_i + 1]),
                    reads=(K.pbn[bank], "wcols"), writes=("ffnu",))
                xs, nm = x1sl(K, tto, fg * 512, (fg + 1) * 512)
                P.op("v", lambda e, xs=xs: e.tensor_tensor(xs, xs, K.ffnu[:, :], OP.add),
                     reads=("ffnu",), writes=(nm,))
    for tto in range(8):
        xs, nm = x1sl(K, tto)
        P.op("sy", lambda e, tto=tto, xs=xs: e.dma_start(
            out=K.out_d[tto * 128:(tto + 1) * 128, :], in_=xs), reads=(nm,))


def build_full(debug=(), upto="all"):
    K = make_K(debug)
    ph_consts(K)
    ph_b1(K)
    ph_prologue(K)
    if upto != "b1":
        ph_b2(K)
        if upto != "b2":
            ph_delta(K)
            if upto != "delta":
                ph_flash(K)
                if upto != "flash":
                    ph_wo(K)
                    ph_ffn(K)
    if K.dbg:
        dbg_dump(K, [n for n in K.dbg if not n.endswith("_h")])
    return finish(K)


def kernel(**inputs):
    nc = build_full()
    maps = shard_inputs(inputs)
    res = run_bass_kernel_spmd(nc, maps, core_ids=list(range(8)))
    return assemble_out(res.results)


# version 04
# speedup vs baseline: 1.1763x; 1.1462x over previous
import math
import types
import numpy as np
import concourse.bass as bass
import concourse.mybir as mybir
from concourse.bass_utils import run_bass_kernel_spmd

F32 = mybir.dt.float32
BF16 = mybir.dt.bfloat16
U32 = mybir.dt.uint32
U8 = mybir.dt.uint8
AF = mybir.ActivationFunctionType
OP = mybir.AluOpType
AX = mybir.AxisListType

B, N, HID = 4, 2048, 2048
H, HKV, D = 16, 8, 128
NT = 16
SCALE = 1.0 / math.sqrt(D)
EPS = 1e-5

DEBUG_TILE_BARRIER = False
FULL_SERIAL = False
SER_MARKED = True
STATIC_PIECE = False
DELTA_CHUNKS = 16
DELTA_J = 8
DELTA_UPTO = 8

SBUF_BASE = 16512
SBUF_LIMIT = 212832


class Ctr:
    __slots__ = ("h", "n", "by")

    def __init__(self, h, by=1):
        self.h, self.n, self.by = h, 0, by


SCALARISH = {"ln_a", "ln_b", "ln_c", "ln_d", "ln_e", "dn4", "dn4r", "gt1",
             "gt2", "gm1", "gm2", "kdcol", "kpcol", "rl_sb", "rde", "wcols",
             "bgsb"}


class Buf:
    __slots__ = ("name", "w", "rs", "scalar")

    def __init__(self, name):
        self.name, self.w, self.rs = name, [], []
        self.scalar = name in SCALARISH


class Stream:
    def __init__(self, name, ctr, self_waits=False):
        self.name, self.ctr = name, ctr
        self.self_waits = self_waits
        self.items = []
        self.waited = {}
        self.sink = self.items

    dma_ctr = None

    def raw(self, fn, waits=(), inc=None, self_ok=False):
        ws = []
        for t in waits:
            if t is None:
                continue
            c, v = t
            if c is self.ctr and not (self.self_waits and self_ok):
                continue
            if self.waited.get(id(c), 0) >= v:
                continue
            self.waited[id(c)] = v
            ws.append((c, v))
        tok = None
        if inc is not None:
            inc.n += inc.by
            tok = (inc, inc.n)
        self.sink.append((ws, fn, inc, inc.by if inc else 0))
        return tok


def _merge_toks(toks):
    best = {}
    for c, v in toks:
        if best.get(id(c), (None, -1))[1] < v:
            best[id(c)] = (c, v)
    return list(best.values())


class Prog:
    def __init__(self, nc):
        self.nc = nc
        self.ctrs = {}
        for nm in ("t", "v", "s", "sy", "gp", "gq"):
            by = 16 if nm in ("sy", "gq") else 1
            self.ctrs[nm] = Ctr(nc.alloc_semaphore(name=f"c_{nm}"), by)
        self.st = {nm: Stream(nm, self.ctrs[nm], self_waits=nm in ("v", "s"))
                   for nm in ("t", "v", "s", "sy", "gp")}
        self.st["gp"].dma_ctr = self.ctrs["gq"]
        self.nopfns = {}
        self.bufs = {}
        self.pid_ap = None

    def buf(self, name):
        b = self.bufs.get(name)
        if b is None:
            b = self.bufs[name] = Buf(name)
        return b

    def op(self, snm, fn, reads=(), writes=(), extra=(), ser=False, dma=False):
        if (FULL_SERIAL is True or (isinstance(FULL_SERIAL, str) and snm in FULL_SERIAL)
                or (ser and SER_MARKED)):
            self.barrier()
        s = self.st[snm]
        waits = list(extra) + list(self.buf("__barrier__").w)
        rb = [self.buf(x) for x in reads]
        wb = [self.buf(x) for x in writes]
        for b in rb:
            waits += b.w
        for b in wb:
            waits += b.rs
            waits += b.w
        ctr = s.dma_ctr if (dma and s.dma_ctr is not None) else s.ctr
        self_ok = any(b.scalar for b in rb)
        tok = s.raw(fn, waits=waits, inc=ctr, self_ok=self_ok)
        for b in rb:
            b.rs.append(tok)
        for b in wb:
            b.w = [tok]
            b.rs = []
        return tok

    def barrier(self):
        toks = [(c, c.n) for c in self.ctrs.values() if c.n > 0]
        for b in self.bufs.values():
            b.w = _merge_toks(b.w + toks)
            b.rs = []
        bb = self.buf("__barrier__")
        bb.w = _merge_toks(toks)

    def bwait(self):
        return list(self.buf("__barrier__").w)

    def begin_branch(self):
        self._snap_ctr = {k: c.n for k, c in self.ctrs.items()}
        self._snap_buf = {k: (list(b.w), list(b.rs)) for k, b in self.bufs.items()}
        self._snap_wait = {k: dict(s.waited) for k, s in self.st.items()}
        self._sideA = {k: [] for k in self.st}
        self._sideB = {k: [] for k in self.st}
        for k, s in self.st.items():
            s.sink = self._sideA[k]

    def mid_branch(self):
        self._a_ctr = {k: c.n for k, c in self.ctrs.items()}
        self._a_buf = {k: (list(b.w), list(b.rs)) for k, b in self.bufs.items()}
        for k, c in self.ctrs.items():
            c.n = self._snap_ctr[k]
        for k, b in self.bufs.items():
            w, r = self._snap_buf.get(k, ([], []))
            b.w, b.rs = list(w), list(r)
        for k, s in self.st.items():
            s.waited = dict(self._snap_wait[k])
            s.sink = self._sideB[k]

    def end_branch(self):
        for k in self.st:
            fn = self.nopfns[k]
            for side in (self._sideA[k], self._sideB[k]):
                if not side:
                    side.append(((), fn, None, 0))
        b_ctr = {k: c.n for k, c in self.ctrs.items()}
        b_buf = {k: (list(b.w), list(b.rs)) for k, b in self.bufs.items()}
        for k, c in self.ctrs.items():
            hi = max(self._a_ctr[k], b_ctr[k])
            if hi - self._a_ctr[k]:
                self._sideA[k].append(("pad", c, hi - self._a_ctr[k]))
            if hi - b_ctr[k]:
                self._sideB[k].append(("pad", c, hi - b_ctr[k]))
            c.n = hi
        for nm in set(self._a_buf) | set(b_buf):
            wa, ra = self._a_buf.get(nm, ([], []))
            wb, rb = b_buf.get(nm, ([], []))
            bb = self.buf(nm)
            bb.w = _merge_toks(wa + wb)
            bb.rs = _merge_toks(ra + rb)
        for k, s in self.st.items():
            s.waited = dict(self._snap_wait[k])
            s.sink = s.items
            s.items.append(("br", self._sideA[k], self._sideB[k]))


def _ensure_regs(K, eng, st):
    if st["preg"] is None:
        preg = eng.alloc_register(f"parity_{st['nm']}")
        roff = eng.alloc_register(f"roff_{st['nm']}")
        for c, v in st["pidw"]:
            eng.wait_ge(c.h, v)
        eng.reg_load(preg, K.pidsb[0:1, 0:1])
        eng.reg_mod(preg, preg, 2)
        st["preg"], st["roff"] = preg, roff


def _emit_items(K, eng, items, st):
    for it in items:
        if it and it[0] == "pad":
            eng.sem_inc(it[1].h, it[2])
        elif it and it[0] == "br":
            _ensure_regs(K, eng, st)
            with eng.If_eq(st["preg"], 0):
                _emit_items(K, eng, it[1], st)
            with eng.Else():
                _emit_items(K, eng, it[2], st)
        else:
            ws, fn, ctr, by = it
            for c, v in ws:
                eng.wait_ge(c.h, v)
            if getattr(fn, "_needs_regs", False):
                _ensure_regs(K, eng, st)
                inst = fn(eng, st["preg"], st["roff"])
            else:
                inst = fn(eng)
            if ctr is not None:
                inst.then_inc(ctr.h, by)


def _emit_engine(K, eng, nm):
    P = K.P
    items = P.st[nm].items
    st = {"preg": None, "roff": None, "nm": nm,
          "pidw": getattr(P, "pidw_new", list(P.bufs["pidsb"].w))}
    _emit_items(K, eng, items, st)


def make_K(debug=()):
    nc = bass.Bass("TRN2", target_bir_lowering=False)
    K = types.SimpleNamespace(nc=nc, dbg=set(debug))
    K.P = Prog(nc)

    K.x_d = nc.dram_tensor("x", [N, HID], F32, kind="ExternalInput").ap()
    K.xown_d = nc.dram_tensor("xown", [1024, HID], F32, kind="ExternalInput").ap()
    K.wq_d = nc.dram_tensor("wq", [HID, 2048], F32, kind="ExternalInput").ap()
    K.wk_d = nc.dram_tensor("wk", [HID, 1024], F32, kind="ExternalInput").ap()
    K.wv_d = nc.dram_tensor("wv", [HID, 1024], F32, kind="ExternalInput").ap()
    K.wo_d = nc.dram_tensor("wo", [2048, HID], F32, kind="ExternalInput").ap()
    K.wg_d = nc.dram_tensor("wg", [D, 1], F32, kind="ExternalInput").ap()
    K.bg_d = nc.dram_tensor("bg", [8, 1], F32, kind="ExternalInput").ap()
    K.gw_d = nc.dram_tensor("gatew", [HID, 8], F32, kind="ExternalInput").ap()
    K.gb_d = nc.dram_tensor("gateb", [1, 8], F32, kind="ExternalInput").ap()
    K.ew_d = nc.dram_tensor("expw", [8, HID, 2048], F32, kind="ExternalInput").ap()
    K.eb_d = nc.dram_tensor("expb", [8, 2048], F32, kind="ExternalInput").ap()
    K.fmask2_d = nc.dram_tensor("fmask2", [128, 4096], BF16, kind="ExternalInput").ap()
    K.pmask_d = nc.dram_tensor("pmask", [128, 256], U8, kind="ExternalInput").ap()
    K.out_d = nc.dram_tensor("out", [1024, HID], F32, kind="ExternalOutput").ap()
    def scratch(name, shape):
        kind = "ExternalOutput" if name in K.dbg else "Internal"
        return nc.dram_tensor(name, shape, BF16, kind=kind).ap()

    K.xfm_h = nc.dram_tensor("xfm_h", [NT, 128, HID], BF16,
                             kind="ExternalOutput").ap()
    K.kfm_h = scratch("kfm_h", [128, HKV * N])
    K.qfm_h = scratch("qfm_h", [128, H * 1024])

    def at(name, shape, dtype, off):
        assert off % 32 == 0, (name, off)
        per = int(np.prod(shape[1:])) * mybir.dt.size(dtype)
        assert off + per <= SBUF_LIMIT, (name, off, per)
        return nc.alloc_sbuf_tensor_at(name, shape, dtype, offset=SBUF_BASE + off).ap()

    o = [0]

    def nx(sz):
        r = o[0]
        o[0] += (sz + 31) // 32 * 32
        return r

    K.id_b = at("id_b", [128, 128], BF16, nx(256))
    K.id_f = at("id_f", [128, 128], F32, nx(512))
    K.ones1_b = at("ones1_b", [1, 128], BF16, nx(256))
    K.ones1_f = at("ones1_f", [1, 128], F32, nx(512))
    K.ones128_b = at("ones128_b", [128, 1], BF16, nx(32))
    K.mstr = at("mstr", [128, 128], F32, nx(512))
    K.mustr = at("mustr", [128, 128], F32, nx(512))
    K.muincs = at("muincs", [128, 128], F32, nx(512))
    K.cumT = at("cumT", [128, 128], F32, nx(512))
    K.lall = at("lall", [128, 128], F32, nx(512))
    K.slall = at("slall", [128, 128], F32, nx(512))
    K.lcb = at("lcb", [128, 128], F32, nx(512))
    K.ccb = at("ccb", [128, 128], F32, nx(512))
    K.sstate = at("sstate", [128, 8 * 128], BF16, nx(2048))
    K.pmasksb = at("pmasksb", [128, 256], U8, nx(256))
    K.cum = at("cum", [8, N], F32, nx(8192))
    K.pidsb = at("pidsb", [1, 1], U32, nx(32))
    K.bgsb = at("bgsb", [8, 1], F32, nx(32))
    K.wgsb = at("wgsb", [128, 1], BF16, nx(32))
    K.ln_a = at("ln_a", [128, 1], F32, nx(32))
    K.ln_b = at("ln_b", [128, 1], F32, nx(32))
    K.ln_c = at("ln_c", [128, 1], F32, nx(32))
    K.ln_d = at("ln_d", [128, 1], F32, nx(32))
    K.ln_e = at("ln_e", [128, 1], F32, nx(32))
    K.dn4 = at("dn4", [128, 8], F32, nx(64))
    K.dn4r = at("dn4r", [128, 8], F32, nx(64))
    K.gt1 = at("gt1", [128, 8], F32, nx(64))
    K.gt2 = at("gt2", [128, 8], F32, nx(64))
    K.gm1 = at("gm1", [128, 1], F32, nx(32))
    K.gm2 = at("gm2", [128, 1], F32, nx(32))
    K.cpool = at("cpool", [128, 2], F32, nx(64))
    K.nopsc = at("nopsc", [1, 8], F32, nx(32))
    nc.const_aps.register(F32, EPS, K.cpool[:, 0:1])
    nc.const_aps.register(F32, 1.0, K.cpool[:, 1:2])

    PEND = (o[0] + 1023) // 1024 * 1024
    R1 = PEND
    R2 = R1 + 32768
    R3 = R2 + 32768
    R4 = R3 + 32768
    R5 = R4 + 32768
    R6 = R5 + 32768
    assert R6 + 22528 <= SBUF_LIMIT, (PEND, R6)
    K.R6_avail = SBUF_LIMIT - R6

    K.ks_tm = at("ks_tm", [128, NT * 1024], BF16, R1)
    K.qfm_bk = at("qfm_bk", [128, H * 1024], BF16, R1)
    K.x1a = at("x1a", [128, 4 * 2048], F32, R1)
    K.v_tm = at("v_tm", [128, NT * 1024], BF16, R2)
    K.xh2_fm = at("xh2_fm", [128, 16 * 1024], BF16, R2)
    K.woslab = [at("woslab0", [128, 16 * 512], BF16, R2),
                at("woslab1", [128, 16 * 512], BF16, R2 + 16384)]
    K.xcb = at("xcb", [128, 2048], BF16, R3)
    K.expst = at("expst", [128, 512], BF16, R3 + 4096)
    K.kfm_st = at("kfm_st", [128, 1024], BF16, R3 + 5120)
    K.t2b = at("t2b", [8, N], F32, R3 + 7168)
    K.gsb = at("gsb", [8, N], F32, R3 + 15360)
    K.fla = at("fla", [128, 8 * 2048], BF16, R3)
    K.weslab = [at("weslab0", [128, 16 * 512], BF16, R3),
                at("weslab1", [128, 16 * 512], BF16, R3 + 16384)]
    K.wkv = at("wkv", [128, 16 * 2048], BF16, R4)
    K.qs_fm = at("qs_fm", [128, H * 1024], BF16, R4)
    K.base_fm = at("base_fm", [128, H * 1024], BF16, R4)
    K.xpiece = at("xpiece", [128, 16 * 512], BF16, R5)
    K.wqslab = [at("wqslab0", [128, 16 * 512], BF16, R5 + 16384),
                at("wqslab1", [128, 16 * 512], BF16, R6)]
    K.kfm_bk = at("kfm_bk", [128, HKV * N], BF16, R5)
    K.x1b = at("x1b", [128, 4 * 2048], F32, R5)
    ds = [R5]

    def dn(sz):
        r = ds[0]
        ds[0] += (sz + 31) // 32 * 32
        return r

    K.dsets = []
    for si in range(2):
        d = types.SimpleNamespace()
        for nm2, dt2, cols in [("ksfm_c", BF16, 128), ("a_sb", BF16, 128),
                               ("at_sb", BF16, 128), ("ap0", BF16, 128),
                               ("ap1", BF16, 128), ("apt0", BF16, 128),
                               ("apt1", BF16, 128), ("z0", BF16, 128),
                               ("z1", BF16, 128), ("lk_sb", BF16, 128),
                               ("kp_sb", BF16, 128), ("nmt_sb", BF16, 128),
                               ("uc_sb", BF16, 128), ("oa_sb", BF16, 128),
                               ("bt_sb", BF16, 128), ("dtmp", BF16, 128),
                               ("e1", F32, 128), ("e2", F32, 128),
                               ("e1m", F32, 128), ("e2m", F32, 128),
                               ("ebt", F32, 128), ("dn1", F32, 128),
                               ("kdcol", F32, 1), ("kpcol", F32, 1)]:
            sz = cols * (2 if dt2 is BF16 else 4)
            setattr(d, nm2, at(f"{nm2}_{si}", [128, cols], dt2, dn(sz)))
            setattr(d, nm2 + "_n", f"{nm2}_{si}")
        K.dsets.append(d)
    K.cumrow = at("cumrow", [1, 1024], F32, dn(4096))
    assert ds[0] <= R5 + 32768

    K.xt = [at("xt0", [128, 2048], F32, R6),
            at("xt1", [128, 2048], F32, R6 + 8192)]
    K.sqsc = at("sqsc", [128, 2048], F32, R6 + 16384)
    K.kraw_st = at("kraw_st", [128, 1024], F32, R6 + 24576)
    K.gstage = at("gstage", [1, 1024], F32, R6 + 24576)
    K.t1b = at("t1b", [8, N], F32, R6)
    K.rmask = at("rmask", [8, N], F32, R6 + 8192)
    K.pt_all = at("pt_all", [128, 16 * 512], BF16, R6)
    K.rb_sb = at("rb_sb", [128, 512], BF16, R6 + 16384)
    K.rl_sb = at("rl_sb", [1, 512], BF16, R6 + 17408)
    K.fmask2sb = at("fmask2sb", [128, 4096], BF16, R6 + 18432)
    K.pt_tmp = at("pt_tmp", [128, 512], BF16, R6 + 26624)
    K.ln2sc = at("ln2sc", [128, 2048], F32, R6)
    K.berow = [at("berow0", [1, 2048], BF16, R6 + 8192),
               at("berow1", [1, 2048], BF16, R6 + 12288)]
    K.wcols = at("wcols", [128, 64], F32, R6 + 16384)
    K.gatewsb = at("gatewsb", [128, 128], BF16, R6 + 16672)
    K.gatebsb = at("gatebsb", [1, 8], BF16, R6 + 16960)
    K.ln2st = at("ln2st", [128, 2048], BF16, R6 + 20480)
    K.ffnu = at("ffnu", [128, 512], BF16, R6 + 24576)
    K.qse = at("qse", [128, 512], BF16, R6 + 17408)
    K.qse2 = at("qse2", [128, 512], BF16, R6 + 18432)
    K.rde = at("rde", [1, 512], BF16, R6 + 19456)

    K.P.nopfns = {
        "t": lambda e: e.matmul(K.pb[0][0:1, 0:1], K.ones1_b[0:1, 0:1],
                                K.ones1_b[0:1, 0:1], start=True, stop=True),
        "v": lambda e: e.memset(K.nopsc[0:1, 0:1], 0.0),
        "s": lambda e: e.memzero(K.nopsc[0:1, 1:2]),
        "gp": lambda e: e.memset(K.nopsc[0:1, 2:3], 0.0),
        "sy": lambda e: e.nop(),
    }
    K.pb = [nc.alloc_psum_tensor(f"pb{i}", [128, 512], F32).ap() for i in range(7)]
    K.pb7b = nc.alloc_psum_tensor("pb7b", [128, 1024], BF16).ap()
    K.pbn = [f"pb{i}" for i in range(7)] + ["pb7b"]
    return K


def ph_consts(K):
    P = K.P

    def g(fn, **kw):
        P.op("gp", fn, **kw)

    def mask_init(ap, nm, fill_val, cmp, base, pattern, chm):
        g(lambda e: e.memset(ap[:, :], fill_val), writes=(nm,))
        g(lambda e: e.affine_select(out=ap[:, :], in_=ap[:, :], compare_op=cmp,
                                    fill=0.0, base=base, pattern=pattern,
                                    channel_multiplier=chm), writes=(nm,))

    g(lambda e: e.memset(K.id_b[:, :], 0.0), writes=("id_b",))
    g(lambda e: e.affine_select(out=K.id_b[:, :], in_=K.id_b[:, :], compare_op=OP.not_equal,
                                fill=1.0, base=0, pattern=[[-1, 128]], channel_multiplier=1),
      writes=("id_b",))
    g(lambda e: e.memset(K.id_f[:, :], 0.0), writes=("id_f",))
    g(lambda e: e.affine_select(out=K.id_f[:, :], in_=K.id_f[:, :], compare_op=OP.not_equal,
                                fill=1.0, base=0, pattern=[[-1, 128]], channel_multiplier=1),
      writes=("id_f",))
    g(lambda e: e.memset(K.ones1_b[:, :], 1.0), writes=("ones1_b",))
    g(lambda e: e.memset(K.ones1_f[:, :], 1.0), writes=("ones1_f",))
    g(lambda e: e.memset(K.ones128_b[:, :], 1.0), writes=("ones128_b",))
    mask_init(K.mstr, "mstr", 1.0, OP.is_gt, 0, [[-1, 128]], 1)
    mask_init(K.mustr, "mustr", 1.0, OP.is_gt, 0, [[1, 128]], -1)
    mask_init(K.muincs, "muincs", SCALE, OP.is_ge, 0, [[1, 128]], -1)
    g(lambda e: e.memset(K.sstate[:, :], 0.0),
      writes=tuple(f"sstate{j}" for j in range(8)))
    g(lambda e: e.memset(K.cpool[:, 0:1], EPS), writes=("cpool",))
    g(lambda e: e.memset(K.cpool[:, 1:2], 1.0), writes=("cpool",))

    P.pid_tok_old = [P.op("sy", lambda e: e.dma_start(
        out=K.pidsb[:, :], in_=K.nc.partition_id_tensor.ap()[0:1, 0:1]),
        writes=("pidsb",))]
    P.op("sy", lambda e: e.dma_start(out=K.bgsb[:, :], in_=K.bg_d[:, :]), writes=("bgsb",))
    P.op("gp", lambda e: e.dma_start(out=K.wgsb[:, :], in_=K.wg_d[:, :]),
         writes=("wgsb",), dma=True)
    P.op("gp", lambda e: e.dma_start(
        out=K.wkv.rearrange("p (c n) -> p c n", n=2048)[:, :, 0:1024],
        in_=K.wk_d.rearrange("(c p) n -> p c n", p=128)), writes=("wkv",), dma=True)
    P.op("gp", lambda e: e.dma_start(
        out=K.wkv.rearrange("p (c n) -> p c n", n=2048)[:, :, 1024:2048],
        in_=K.wv_d.rearrange("(c p) n -> p c n", p=128)), writes=("wkv",), dma=True)


def ph_b1(K):
    P = K.P
    pb = K.pb
    for t in range(NT):
        if DEBUG_TILE_BARRIER:
            P.barrier()
        xb = K.xt[t % 2]
        xbuf = f"xt{t % 2}"
        P.op("sy", lambda e, t=t, xb=xb: e.dma_start(out=xb[:, :],
                                                     in_=K.x_d[t * 128:(t + 1) * 128, :]),
             writes=(xbuf,))
        P.op("v", lambda e, xb=xb: e.reduce_sum(K.ln_a[:, :], xb[:, :], axis=AX.X),
             reads=(xbuf,), writes=("ln_a",))
        P.op("s", lambda e: e.activation(K.ln_b[:, :], K.ln_a[:, :], AF.Copy, scale=1.0 / HID),
             reads=("ln_a",), writes=("ln_b",))
        P.op("v", lambda e, xb=xb: e.tensor_scalar(xb[:, :], xb[:, :], K.ln_b[:, :], None,
                                                   OP.subtract),
             reads=("ln_b",), writes=(xbuf,))
        P.op("s", lambda e, xb=xb: e.activation(K.sqsc[:, :], xb[:, :], AF.Square),
             reads=(xbuf,), writes=("sqsc",))
        P.op("v", lambda e: e.reduce_sum(K.ln_c[:, :], K.sqsc[:, :], axis=AX.X),
             reads=("sqsc",), writes=("ln_c",))
        P.op("s", lambda e: e.activation(K.ln_d[:, :], K.ln_c[:, :], AF.Sqrt,
                                         scale=1.0 / HID, bias=EPS),
             reads=("ln_c", "cpool"), writes=("ln_d",))
        P.op("v", lambda e: e.reciprocal(K.ln_e[:, :], K.ln_d[:, :]),
             reads=("ln_d",), writes=("ln_e",))
        P.op("v", lambda e, xb=xb: e.tensor_scalar(xb[:, :], xb[:, :], K.ln_e[:, :],
                                                   None, OP.mult),
             reads=("ln_e",), writes=(xbuf,))
        for grp in range(4):
            bank = grp % 2
            for kk in range(4):
                fc = grp * 4 + kk
                P.op("t", lambda e, xb=xb, fc=fc, bank=bank, kk=kk: e.transpose(
                    pb[bank][:, kk * 128:(kk + 1) * 128],
                    xb[:, fc * 128:(fc + 1) * 128], K.id_f[:, :]),
                    reads=(xbuf, "id_f"), writes=(K.pbn[bank],))
            P.op("s", lambda e, grp=grp, bank=bank: e.copy(
                K.xcb[:, grp * 512:(grp + 1) * 512], pb[bank][:, :]),
                reads=(K.pbn[bank],), writes=("xcb",))
        P.op("sy", lambda e, t=t: e.dma_start(out=K.xfm_h[t], in_=K.xcb[:, :]),
             reads=("xcb",))
        for bi, co in ((2, 0), (3, 512), (4, 1024), (5, 1536)):
            for fc in range(16):
                st, sp = fc == 0, fc == 15
                P.op("t", lambda e, fc=fc, st=st, sp=sp, bi=bi, co=co: e.matmul(
                    pb[bi][:, :], K.xcb[:, fc * 128:(fc + 1) * 128],
                    K.wkv[:, fc * 2048 + co:fc * 2048 + co + 512], start=st, stop=sp),
                    reads=("xcb", "wkv"), writes=(K.pbn[bi],))
        for half in range(2):
            kb, vb = K.pbn[2 + half], K.pbn[4 + half]
            P.op("s", lambda e, half=half: e.copy(
                K.kraw_st[:, half * 512:(half + 1) * 512], pb[2 + half][:, :]),
                reads=(kb,), writes=("gkovl",))
            P.op("s", lambda e, half=half: e.activation(K.expst[:, :], pb[2 + half][:, :], AF.Exp),
                reads=(kb,), writes=("expst",))
            P.op("v", lambda e, half=half: e.reduce_sum(
                K.dn4[:, half * 4:(half + 1) * 4],
                K.expst.rearrange("p (s d) -> p s d", d=128), axis=AX.X),
                reads=("expst",), writes=("dn4",))
            P.op("v", lambda e, half=half: e.reciprocal(
                K.dn4r[:, half * 4:(half + 1) * 4], K.dn4[:, half * 4:(half + 1) * 4]),
                reads=("dn4",), writes=("dn4r",))
            for s4 in range(4):
                j = half * 4 + s4
                P.op("v", lambda e, t=t, j=j, s4=s4: e.tensor_scalar(
                    K.ks_tm[:, t * 1024 + j * 128:t * 1024 + (j + 1) * 128],
                    K.expst[:, s4 * 128:(s4 + 1) * 128], K.dn4r[:, j:j + 1], None, OP.mult),
                    reads=("expst", "dn4r"), writes=("ks_tm",))
            P.op("s", lambda e, t=t, half=half: e.copy(
                K.v_tm[:, t * 1024 + half * 512:t * 1024 + (half + 1) * 512],
                pb[4 + half][:, :]),
                reads=(vb,), writes=("v_tm",))
        for grp in range(2):
            for kk in range(4):
                j = grp * 4 + kk
                P.op("t", lambda e, j=j, kk=kk: e.transpose(
                    pb[6][:, kk * 128:(kk + 1) * 128],
                    K.kraw_st[:, j * 128:(j + 1) * 128], K.id_f[:, :]),
                    reads=("gkovl", "id_f"), writes=(K.pbn[6],))
            P.op("s", lambda e, grp=grp: e.copy(
                K.kfm_st[:, grp * 512:(grp + 1) * 512], pb[6][:, :]),
                reads=(K.pbn[6],), writes=("kfm_st",))
        P.op("sy", lambda e, t=t: e.dma_start(
            out=K.kfm_h.rearrange("p (j n) -> p j n", n=N)[:, :, t * 128:(t + 1) * 128],
            in_=K.kfm_st.rearrange("p (j n) -> p j n", n=128)),
            reads=("kfm_st",))
        for j in range(8):
            bi = 2 if j < 4 else 3
            P.op("t", lambda e, j=j, bi=bi: e.matmul(
                K.pb[bi][0:1, (j % 4) * 128:((j % 4) + 1) * 128], K.wgsb[:, :],
                K.kfm_st[:, j * 128:(j + 1) * 128], start=True, stop=True),
                reads=("kfm_st", "wgsb"), writes=(K.pbn[bi],))
        P.op("s", lambda e: e.copy(K.gstage[:, 0:512], pb[2][0:1, 0:512]),
             reads=(K.pbn[2], "kfm_st"), writes=("gkovl",))
        P.op("s", lambda e: e.copy(K.gstage[:, 512:1024], pb[3][0:1, 0:512]),
             reads=(K.pbn[3],), writes=("gkovl",))
        P.op("sy", lambda e, t=t: e.dma_start(
            out=K.gsb[0:8, t * 128:(t + 1) * 128],
            in_=K.gstage[0:1, 0:1024]),
            reads=("gkovl",), writes=("gsb",))


def ph_prologue(K):
    P = K.P
    if "v_early" in K.dbg:
        d = K.nc.dram_tensor("dbg_v_early", [128, NT * 1024], BF16,
                             kind="ExternalOutput").ap()
        P.op("sy", lambda e, d=d: e.dma_start(out=d, in_=K.v_tm[:, :]), reads=("v_tm",))
    pb = K.pb
    P.barrier()
    P.op("gp", lambda e: e.memset(K.rmask[:, :], 1.0), writes=("rmask",))
    P.op("gp", lambda e: e.affine_select(out=K.rmask[:, :], in_=K.rmask[:, :],
                                         compare_op=OP.is_gt, fill=0.0, base=0,
                                         pattern=[[0, 16], [1, 128]], channel_multiplier=0),
         writes=("rmask",))
    if "rmask_early" in K.dbg:
        dearly = K.nc.dram_tensor("dbg_rmask_early", [8, N], F32,
                                  kind="ExternalOutput").ap()
        P.op("sy", lambda e, d=dearly: e.dma_start(out=d, in_=K.rmask[:, :]),
             reads=("rmask",))
    P.op("v", lambda e: e.tensor_scalar(K.gsb[:, :], K.gsb[:, :], K.bgsb[:, :], None, OP.add),
         reads=("bgsb",), writes=("gsb",))
    P.op("s", lambda e: e.activation(K.t1b[:, :], K.gsb[:, :], AF.Abs),
         reads=("gsb",), writes=("t1b",))
    P.op("s", lambda e: e.activation(K.t1b[:, :], K.t1b[:, :], AF.Exp, scale=-1.0),
         reads=("t1b",), writes=("t1b",))
    P.op("s", lambda e: e.activation(K.t2b[:, :], K.t1b[:, :], AF.Ln, bias=1.0),
         reads=("t1b", "cpool"), writes=("t2b",))
    P.op("v", lambda e: e.tensor_scalar(K.t1b[:, :], K.gsb[:, :], 0.0, None, OP.min),
         reads=("gsb",), writes=("t1b",))
    P.op("v", lambda e: e.tensor_sub(K.t1b[:, :], K.t1b[:, :], K.t2b[:, :]),
         reads=("t1b", "t2b"), writes=("t1b",))
    P.op("s", lambda e: e.activation(K.gsb[:, :], K.t1b[:, :], AF.Copy, scale=1.0 / 16.0),
         reads=("t1b",), writes=("gsb",))
    P.op("v", lambda e: e.tensor_tensor_scan(K.cum[:, :], K.rmask[:, :], K.gsb[:, :], 0.0,
                                             OP.mult, OP.add),
         reads=("rmask", "gsb"), writes=("cum",))
    for c in range(16):
        P.op("t", lambda e, c=c: e.transpose(pb[0][0:128, 0:8],
                                             K.cum[0:8, c * 128:(c + 1) * 128],
                                             K.id_f[0:8, 0:8]),
             reads=("cum", "id_f"), writes=(K.pbn[0],))
        P.op("s", lambda e, c=c: e.copy(K.cumT[:, c * 8:(c + 1) * 8], pb[0][0:128, 0:8]),
             reads=(K.pbn[0],), writes=("cumT",))
    P.op("s", lambda e: e.activation(K.lall[:, :], K.cumT[:, :], AF.Exp),
         reads=("cumT",), writes=("lall",))
    P.op("v", lambda e: e.tensor_scalar(K.slall[:, :], K.lall[:, :], SCALE, None, OP.mult),
         reads=("lall",), writes=("slall",))
    P.op("sy", lambda e: e.dma_start(out=K.cumrow[0:1, 0:128], in_=K.cumT[127:128, :]),
         reads=("cumT",), writes=("cumrow",))
    P.op("t", lambda e: e.matmul(pb[1][:, 0:128], K.ones1_f[:, :], K.cumrow[0:1, 0:128],
                                 start=True, stop=True),
         reads=("ones1_f", "cumrow"), writes=(K.pbn[1],))
    P.op("s", lambda e: e.copy(K.ccb[:, :], pb[1][:, 0:128]), reads=(K.pbn[1],), writes=("ccb",))
    P.op("s", lambda e: e.activation(K.lcb[:, :], pb[1][:, 0:128], AF.Exp),
         reads=(K.pbn[1],), writes=("lcb",))


def dbg_dump(K, names):
    P = K.P
    for nm in names:
        if not hasattr(K, nm):
            continue
        ap = getattr(K, nm)
        d = K.nc.dram_tensor("dbg_" + nm, list(ap.shape), ap.dtype,
                             kind="ExternalOutput").ap()
        P.op("sy", lambda e, d=d, ap=ap: e.dma_start(out=d, in_=ap), reads=(nm,))


def _collect_waited(items, used):
    for it in items:
        if it and it[0] == "pad":
            continue
        if it and it[0] == "br":
            _collect_waited(it[1], used)
            _collect_waited(it[2], used)
            continue
        ws, fn, ctr, by = it
        for c, v in ws:
            used.setdefault(id(c), set()).add(v)


def _number_incs(items, used, newv, cnt, keep_all=False):
    out = []
    for it in items:
        if it and it[0] == "pad":
            _, c, amt = it
            st = cnt.setdefault(id(c), [0, 0])
            u = used.get(id(c), ())
            kept = sum(1 for v in range(st[0] + 1, st[0] + amt + 1) if v in u)
            for v in range(st[0] + 1, st[0] + amt + 1):
                st_new = st[1] + sum(1 for q in range(st[0] + 1, v + 1) if q in u)
                newv.setdefault(id(c), {})[v] = st_new
            st[0] += amt
            st[1] += kept
            if kept:
                out.append(("pad", c, kept))
            continue
        if it and it[0] == "br":
            snap = {k: [vv[0], vv[1]] for k, vv in cnt.items()}
            a = _number_incs(it[1], used, newv, cnt, keep_all)
            enda = {k: [vv[0], vv[1]] for k, vv in cnt.items()}
            for k in list(cnt.keys()):
                cnt[k] = [snap.get(k, [0, 0])[0], snap.get(k, [0, 0])[1]]
            b = _number_incs(it[2], used, newv, cnt, keep_all)
            for k in set(cnt) | set(enda):
                ca = enda.get(k, [0, 0])
                cb = cnt.get(k, [0, 0])
                assert ca == cb or k not in enda or k not in cnt, (ca, cb)
                cnt[k] = [max(ca[0], cb[0]), max(ca[1], cb[1])]
            out.append(("br", a, b))
            continue
        ws, fn, ctr, by = it
        keep = False
        if ctr is not None:
            st = cnt.setdefault(id(ctr), [0, 0])
            st[0] += by
            if keep_all or st[0] in used.get(id(ctr), ()):
                st[1] += by
                keep = True
            newv.setdefault(id(ctr), {})[st[0]] = st[1]
        out.append((ws, fn, ctr if keep else None, by))
    return out


def _rewrite_waits(items, newv):
    out = []
    for it in items:
        if it and it[0] == "pad":
            out.append(it)
        elif it and it[0] == "br":
            out.append(("br", _rewrite_waits(it[1], newv),
                        _rewrite_waits(it[2], newv)))
        else:
            ws, fn, ctr, by = it
            out.append(([(c, newv[id(c)][v]) for c, v in ws], fn, ctr, by))
    return out


def _prune_incs(P, force_used=()):
    used = {}
    for st in P.st.values():
        _collect_waited(st.items, used)
    for c, v in force_used:
        used.setdefault(id(c), set()).add(v)
    newv = {}
    cnt = {}
    for snm in P.st:
        P.st[snm].items = _number_incs(P.st[snm].items, used, newv, cnt,
                                       keep_all=snm in ("sy", "gp"))
    for snm in P.st:
        P.st[snm].items = _rewrite_waits(P.st[snm].items, newv)
    return newv


def finish(K):
    P = K.P
    fin = (P.ctrs["sy"], P.ctrs["sy"].n)
    pidw_old = list(P.pid_tok_old)
    newv = _prune_incs(P, force_used=[fin] + pidw_old)
    K._final_wait = (P.ctrs["sy"].h, newv[id(P.ctrs["sy"])][fin[1]])
    P.pidw_new = [(c, newv[id(c)][v]) for c, v in pidw_old]
    with K.nc.Block() as block:

        @block.tensor
        def _(eng):
            _emit_engine(K, eng, "t")

        @block.vector
        def _(eng):
            _emit_engine(K, eng, "v")

        @block.scalar
        def _(eng):
            _emit_engine(K, eng, "s")

        @block.gpsimd
        def _(eng):
            _emit_engine(K, eng, "gp")

        @block.sync
        def _(eng):
            _emit_engine(K, eng, "sy")
            eng.wait_ge(*K._final_wait)
    return K.nc


def shard_inputs(inputs):
    x = np.ascontiguousarray(inputs["L0_attn_hidden_states"], dtype=np.float32)
    maps = []
    import ml_dtypes
    tril = np.triu(np.ones((128, 128), np.float32))
    for core in range(8):
        b, h = core // 2, core % 2
        rows = np.concatenate([np.arange((2 * u + h) * 128, (2 * u + h) * 128 + 128)
                               for u in range(8)])
        own = np.ascontiguousarray(x[b, rows])
        fm = np.zeros((128, 8, 4, 128), np.float32)
        for r in range(8):
            for v in range(4):
                tv = 2 * v + h
                if r < tv:
                    fm[:, r, v, :] = 1.0
                elif r == tv:
                    fm[:, r, v, :] = tril
        pm = np.zeros((128, 256), np.uint8)
        pm[:, 0:128] = 1 if h == 0 else 0
        pm[:, 128:256] = 0 if h == 0 else 1
        maps.append({
            "fmask2": np.ascontiguousarray(
                fm.reshape(128, 4096).astype(ml_dtypes.bfloat16)),
            "pmask": pm,
            "x": np.ascontiguousarray(x[b]),
            "xown": own,
            "wq": np.ascontiguousarray(inputs["L0_attn_Wq"], np.float32),
            "wk": np.ascontiguousarray(inputs["L0_attn_Wk"], np.float32),
            "wv": np.ascontiguousarray(inputs["L0_attn_Wv"], np.float32),
            "wo": np.ascontiguousarray(inputs["L0_attn_Wo"], np.float32),
            "wg": np.ascontiguousarray(inputs["L0_attn_Wg"], np.float32),
            "bg": np.full((8, 1), np.float32(np.asarray(inputs["L0_attn_bg"]).reshape(-1)[0])),
            "gatew": np.ascontiguousarray(inputs["L0_ffn_gate_w"], np.float32),
            "gateb": np.ascontiguousarray(inputs["L0_ffn_gate_b"], np.float32).reshape(1, 8),
            "expw": np.ascontiguousarray(inputs["L0_ffn_expert_w"], np.float32),
            "expb": np.ascontiguousarray(inputs["L0_ffn_expert_b"], np.float32),
        })
    return maps


def assemble_out(results):
    out = np.empty((B, N, HID), np.float32)
    for core in range(8):
        b, h = core // 2, core % 2
        r = results[core]["out"]
        for u in range(8):
            out[b, (2 * u + h) * 128:(2 * u + h) * 128 + 128] = r[u * 128:(u + 1) * 128]
    return out


def ph_b2(K):
    P = K.P
    pb = K.pb
    P.barrier()
    for p in range(2):
        if STATIC_PIECE:
            P.op("gp", lambda e, p=p: e.dma_start(
                out=K.xpiece.rearrange("q (c t n) -> q c t n", t=4, n=128),
                in_=bass.AP(K.xfm_h.tensor, 8 * p * 262144,
                            [[2048, 128], [128, 16], [524288, 4], [1, 128]])),
                writes=("xpiece",), dma=True)
        else:
            def piece_load(e, preg, roff, p=p):
                e.reg_mov(roff, preg)
                e.reg_mul(roff, roff, 262144)
                e.reg_add(roff, roff, 8 * p * 262144)
                return e.dma_start(
                    out=K.xpiece.rearrange("q (c t n) -> q c t n", t=4, n=128),
                    in_=bass.AP(K.xfm_h.tensor, roff,
                                [[2048, 128], [128, 16], [524288, 4], [1, 128]]))
            piece_load._needs_regs = True
            P.op("gp", piece_load, writes=("xpiece",), dma=True)
        for fg in range(4):
            slab = K.wqslab[fg % 2]
            sbuf = f"wqslab{fg % 2}"
            P.op("gp", lambda e, slab=slab, fg=fg: e.dma_start(
                out=slab.rearrange("q (c n) -> q c n", n=512),
                in_=K.wq_d.rearrange("(c q) n -> q c n", q=128)[:, :, fg * 512:(fg + 1) * 512]),
                writes=(sbuf,), dma=True)
            for hp in range(4):
                h = fg * 4 + hp
                bank = hp % 2
                for fc in range(16):
                    P.op("t", lambda e, slab=slab, hp=hp, fc=fc, bank=bank: e.matmul(
                        pb[bank][:, :], slab[:, fc * 512 + hp * 128:fc * 512 + (hp + 1) * 128],
                        K.xpiece[:, fc * 512:(fc + 1) * 512],
                        start=fc == 0, stop=fc == 15),
                        reads=(sbuf, "xpiece"), writes=(K.pbn[bank],))
                P.op("s", lambda e, bank=bank: e.copy(K.qse[:, :], pb[bank][:, :]),
                     reads=(K.pbn[bank],), writes=("qse",))
                P.op("sy", lambda e, h=h, p=p: e.dma_start(
                    out=K.qfm_h[:, h * 1024 + p * 512:h * 1024 + (p + 1) * 512],
                    in_=K.qse[:, :]), reads=("qse",))
                P.op("s", lambda e, bank=bank: e.activation(K.qse2[:, :], pb[bank][:, :],
                                                            AF.Exp),
                     reads=(K.pbn[bank],), writes=("qse2",))
                P.op("t", lambda e: e.matmul(pb[2][0:1, :], K.ones128_b[:, :], K.qse2[:, :],
                                             start=True, stop=True),
                     reads=("ones128_b", "qse2"), writes=(K.pbn[2],))
                def _recip(e):
                    with K.nc.allow_low_precision(reason="softmax denom bf16"):
                        return e.reciprocal(K.rde[0:1, :], pb[2][0:1, :])
                P.op("v", _recip, reads=(K.pbn[2],), writes=("rde",))
                P.op("t", lambda e: e.matmul(pb[3][:, :], K.ones1_b[:, :], K.rde[0:1, :],
                                             start=True, stop=True),
                     reads=("ones1_b", "rde"), writes=(K.pbn[3],))
                P.op("v", lambda e, h=h, p=p: e.tensor_tensor(
                    K.qs_fm[:, h * 1024 + p * 512:h * 1024 + (p + 1) * 512],
                    K.qse2[:, :], pb[3][:, :], OP.mult),
                    reads=("qse2", K.pbn[3]), writes=("qs_fm",))


def ph_delta(K):
    P = K.P
    pb = K.pb
    P.barrier()
    P.op("sy", lambda e: e.dma_start(out=K.pmasksb[:, :], in_=K.pmask_d[:, :]),
         writes=("pmasksb",))
    ksl = lambda c, j: K.ks_tm[:, c * 1024 + j * 128:c * 1024 + (j + 1) * 128]

    def inst(c, j):
        si = j % 2
        d = K.dsets[si]
        bA, bB, bC = (0, 2, 4) if si == 0 else (1, 3, 5)
        nA, nB, nC = K.pbn[bA], K.pbn[bB], K.pbn[bC]
        pA, pB, pC = pb[bA], pb[bB], pb[bC]
        tr = K.pb7b[:, si * 512:si * 512 + 128]
        col = slice(c * 8 + j, c * 8 + j + 1)
        cumTc = K.cumT[:, col]
        P.op("t", lambda e, c=c, j=j, tr=tr: e.transpose(
            tr, ksl(c, j), K.id_b[:, :]),
            reads=("ks_tm", "id_b"), writes=(f"pb7b{si}",))
        yield
        P.op("s", lambda e, tr=tr, d=d: e.copy(d.ksfm_c[:, :], tr),
             reads=(f"pb7b{si}",), writes=(d.ksfm_c_n,))
        yield
        P.op("t", lambda e, j=j, pC=pC: e.matmul(
            pC[:, 0:128], K.ones1_f[:, :], K.cumrow[0:1, j * 128:(j + 1) * 128],
            start=True, stop=True),
            reads=("ones1_f", "cumrow"), writes=(nC,))
        yield
        P.op("v", lambda e, cumTc=cumTc, d=d, pC=pC: e.tensor_scalar(
            d.dn1[:, :], pC[:, 0:128], cumTc, 0.0, OP.subtract, OP.max),
            reads=(nC, "cumT"), writes=(d.dn1_n,))
        yield
        P.op("s", lambda e, d=d: e.activation(d.e1[:, :], d.dn1[:, :], AF.Exp,
                                              scale=-1.0),
             reads=(d.dn1_n,), writes=(d.e1_n,))
        yield
        P.op("v", lambda e, cumTc=cumTc, d=d, pC=pC: e.tensor_scalar(
            d.dn1[:, :], pC[:, 0:128], cumTc, 0.0, OP.subtract, OP.min),
            reads=(nC, "cumT", d.e1_n), writes=(d.dn1_n,))
        yield
        P.op("s", lambda e, d=d: e.activation(d.e2[:, :], d.dn1[:, :], AF.Exp),
             reads=(d.dn1_n,), writes=(d.e2_n,))
        yield
        P.op("v", lambda e, d=d: e.tensor_tensor(d.e1m[:, :], d.e1[:, :],
                                                 K.mstr[:, :], OP.mult),
             reads=(d.e1_n, "mstr"), writes=(d.e1m_n,))
        yield
        P.op("v", lambda e, d=d: e.tensor_tensor(d.e2m[:, :], d.e2[:, :],
                                                 K.mustr[:, :], OP.mult),
             reads=(d.e2_n, "mustr"), writes=(d.e2m_n,))
        yield
        P.op("v", lambda e, d=d: e.tensor_tensor(d.ebt[:, :], d.e2[:, :],
                                                 K.muincs[:, :], OP.mult),
             reads=(d.e2_n, "muincs"), writes=(d.ebt_n,))
        yield
        P.op("t", lambda e, d=d, pA=pA: e.matmul(
            pA[:, 0:128], d.ksfm_c[:, :], d.ksfm_c[:, :], start=True, stop=True),
            reads=(d.ksfm_c_n,), writes=(nA,))
        yield
        P.op("v", lambda e, d=d, pA=pA: e.tensor_tensor(
            d.a_sb[:, :], pA[:, 0:128], d.e1m[:, :], OP.mult),
            reads=(nA, d.e1m_n), writes=(d.a_sb_n,))
        yield
        P.op("v", lambda e, d=d, pA=pA: e.tensor_tensor(
            d.at_sb[:, :], pA[:, 0:128], d.e2m[:, :], OP.mult),
            reads=(nA, d.e2m_n), writes=(d.at_sb_n,))
        yield
        P.op("v", lambda e, d=d: e.scalar_tensor_tensor(
            d.z0[:, :], d.at_sb[:, :], -1.0, K.id_b[:, :], OP.mult, OP.add),
            reads=(d.at_sb_n, "id_b"), writes=(d.z0_n,))
        yield
        aps = [(d.ap0, d.ap0_n), (d.ap1, d.ap1_n)]
        apts = [(d.apt0, d.apt0_n), (d.apt1, d.apt1_n)]
        zs = [(d.z0, d.z0_n), (d.z1, d.z1_n)]
        cur, curn = d.a_sb, d.a_sb_n
        curt, curtn = d.at_sb, d.at_sb_n
        for lvl in range(6):
            zi, zo = zs[lvl % 2], zs[(lvl + 1) % 2]
            P.op("t", lambda e, cur=cur, curt=curt, pA=pA: e.matmul(
                pA[:, 0:128], curt[:, :], cur[:, :], start=True, stop=True),
                reads=(curn, curtn), writes=(nA,))
            yield
            if lvl < 5:
                P.op("t", lambda e, cur=cur, curt=curt, pB=pB: e.matmul(
                    pB[:, 0:128], cur[:, :], curt[:, :], start=True, stop=True),
                    reads=(curn, curtn), writes=(nB,))
                yield
            nap, napn = aps[lvl % 2]
            napt, naptn = apts[lvl % 2]
            P.op("s", lambda e, nap=nap, pA=pA: e.copy(nap[:, :], pA[:, 0:128]),
                 reads=(nA,), writes=(napn,))
            yield
            if lvl < 5:
                P.op("s", lambda e, napt=napt, pB=pB: e.copy(napt[:, :], pB[:, 0:128]),
                     reads=(nB,), writes=(naptn,))
                yield
            P.op("t", lambda e, nap=nap, zi=zi, pC=pC: e.matmul(
                pC[:, 0:128], nap[:, :], zi[0][:, :], start=True, stop=False),
                reads=(napn, zi[1]), writes=(nC,))
            yield
            P.op("t", lambda e, zi=zi, pC=pC: e.matmul(
                pC[:, 0:128], K.id_b[:, :], zi[0][:, :], start=False, stop=True),
                reads=("id_b", zi[1]), writes=(nC,))
            yield
            P.op("s", lambda e, zo=zo, pC=pC: e.copy(zo[0][:, :], pC[:, 0:128]),
                 reads=(nC,), writes=(zo[1],))
            yield
            cur, curn = nap, napn
            curt, curtn = napt, naptn
        zf, zfn = zs[0]
        lallc = K.lall[:, col]
        P.op("v", lambda e, c=c, j=j, lallc=lallc, d=d: e.tensor_scalar(
            d.lk_sb[:, :], ksl(c, j), lallc, None, OP.mult),
            reads=("ks_tm", "lall"), writes=(d.lk_sb_n,))
        yield
        P.op("v", lambda e, cumTc=cumTc, c=c, j=j, d=d: e.tensor_scalar(
            d.kdcol[:, :], cumTc, K.ccb[:, c * 8 + j:c * 8 + j + 1], -1.0,
            OP.subtract, OP.mult),
            reads=("cumT", "ccb"), writes=(d.kdcol_n,))
        yield
        P.op("s", lambda e, d=d: e.activation(d.kpcol[:, :], d.kdcol[:, :], AF.Exp),
             reads=(d.kdcol_n,), writes=(d.kpcol_n,))
        yield
        P.op("v", lambda e, c=c, j=j, d=d: e.tensor_scalar(
            d.kp_sb[:, :], ksl(c, j), d.kpcol[:, :], None, OP.mult),
            reads=("ks_tm", d.kpcol_n), writes=(d.kp_sb_n,))
        yield
        P.op("t", lambda e, zf=zf, d=d, pB=pB: e.matmul(
            pB[:, 0:128], d.lk_sb[:, :], zf[:, :], start=True, stop=True),
            reads=(d.lk_sb_n, zfn), writes=(nB,))
        yield
        P.op("s", lambda e, d=d, pB=pB: e.activation(
            d.nmt_sb[:, :], pB[:, 0:128], AF.Copy, scale=-1.0),
            reads=(nB,), writes=(d.nmt_sb_n,))
        yield
        sstate_j = K.sstate[:, j * 128:(j + 1) * 128]
        P.op("t", lambda e, zf=zf, c=c, j=j, pC=pC: e.matmul(
            pC[:, 0:128], zf[:, :],
            K.v_tm[:, c * 1024 + j * 128:c * 1024 + (j + 1) * 128],
            start=True, stop=False),
            reads=(zfn, "v_tm"), writes=(nC,))
        yield
        P.op("t", lambda e, sstate_j=sstate_j, d=d, pC=pC: e.matmul(
            pC[:, 0:128], d.nmt_sb[:, :], sstate_j, start=False, stop=True),
            reads=(d.nmt_sb_n, f"sstate{j}"), writes=(nC,))
        yield
        P.op("v", lambda e, d=d, pC=pC: e.tensor_copy(d.uc_sb[:, :], pC[:, 0:128]),
             reads=(nC,), writes=(d.uc_sb_n,))
        yield
        u = c // 2
        slallc = K.slall[:, col]
        pmsl = K.pmasksb[:, (c % 2) * 128:(c % 2) * 128 + 128]
        for hh in (2 * j, 2 * j + 1):
            qsl = K.qs_fm[:, hh * 1024 + u * 128:hh * 1024 + (u + 1) * 128]
            P.op("t", lambda e, qsl=qsl, d=d, pA=pA: e.matmul(
                pA[:, 0:128], d.ksfm_c[:, :], qsl, start=True, stop=True),
                reads=(d.ksfm_c_n, "qs_fm"), writes=(nA,))
            yield
            P.op("v", lambda e, d=d, pA=pA: e.tensor_tensor(
                d.bt_sb[:, :], pA[:, 0:128], d.ebt[:, :], OP.mult),
                reads=(nA, d.ebt_n), writes=(d.bt_sb_n,))
            yield
            P.op("t", lambda e, qsl=qsl, sstate_j=sstate_j, pB=pB: e.matmul(
                pB[:, 0:128], qsl, sstate_j, start=True, stop=True),
                reads=("qs_fm", f"sstate{j}"), writes=(nB,))
            yield
            P.op("s", lambda e, slallc=slallc, d=d, pB=pB: e.activation(
                d.oa_sb[:, :], pB[:, 0:128], AF.Copy, scale=slallc),
                reads=(nB, "slall"), writes=(d.oa_sb_n,))
            yield
            P.op("t", lambda e, d=d, pC=pC: e.matmul(
                pC[:, 0:128], d.bt_sb[:, :], d.uc_sb[:, :], start=True, stop=True),
                reads=(d.bt_sb_n, d.uc_sb_n), writes=(nC,))
            yield
            P.op("v", lambda e, d=d, pC=pC: e.tensor_tensor(
                d.dtmp[:, :], d.oa_sb[:, :], pC[:, 0:128], OP.add),
                reads=(d.oa_sb_n, nC), writes=(d.dtmp_n,))
            yield
            P.op("v", lambda e, hh=hh, u=u, pmsl=pmsl, d=d: e.copy_predicated(
                K.fla[:, u * 2048 + hh * 128:u * 2048 + (hh + 1) * 128],
                pmsl, d.dtmp[:, :]),
                reads=(d.dtmp_n, "pmasksb"), writes=("fla",))
            yield
        lcbc = K.lcb[:, col]
        P.op("t", lambda e, d=d, pA=pA: e.matmul(
            pA[:, 0:128], d.kp_sb[:, :], d.uc_sb[:, :], start=True, stop=True),
            reads=(d.kp_sb_n, d.uc_sb_n), writes=(nA,))
        yield
        P.op("v", lambda e, sstate_j=sstate_j, lcbc=lcbc, pA=pA: e.scalar_tensor_tensor(
            sstate_j, sstate_j, lcbc, pA[:, 0:128], OP.mult, OP.add),
            reads=("lcb", nA), writes=(f"sstate{j}",))
        yield

    for c in range(16):
        P.op("sy", lambda e, c=c: e.dma_start(
            out=K.cumrow.rearrange("o (j n) -> o j n", n=128),
            in_=K.cum[0:8, c * 128:(c + 1) * 128]),
            reads=("cum",), writes=("cumrow",))
        for jp in range(4):
            gens = [inst(c, 2 * jp), inst(c, 2 * jp + 1)]
            alive = [True, True]
            while any(alive):
                for gi, g in enumerate(gens):
                    if alive[gi]:
                        try:
                            next(g)
                        except StopIteration:
                            alive[gi] = False


def ph_flash(K):
    P = K.P
    pb = K.pb
    P.barrier()
    P.op("sy", lambda e: e.dma_start(out=K.kfm_bk[:, :], in_=K.kfm_h[:, :]),
         writes=("kfm_bk",))
    P.op("sy", lambda e: e.dma_start(out=K.qfm_bk[:, :], in_=K.qfm_h[:, :]),
         writes=("qfm_bk",))
    P.op("sy", lambda e: e.dma_start(out=K.fmask2sb[:, :], in_=K.fmask2_d[:, :]),
         writes=("fmask2sb",))
    for h in range(H):
        j = h // 2
        for p in range(2):
            kext = 8 * p + 8
            qsl = K.qfm_bk[:, h * 1024 + p * 512:h * 1024 + (p + 1) * 512]
            for k in range(kext):
                bank = k % 2
                P.op("t", lambda e, j=j, k=k, bank=bank, qsl=qsl: e.matmul(
                    pb[bank][:, :], K.kfm_bk[:, j * 2048 + k * 128:j * 2048 + (k + 1) * 128],
                    qsl, start=True, stop=True),
                    reads=("kfm_bk", "qfm_bk"), writes=(K.pbn[bank],))
                r = k - 8 * p
                if r >= 0:
                    P.op("s", lambda e, bank=bank: e.activation(
                        K.pt_tmp[:, :], pb[bank][:, :], AF.Exp, scale=SCALE),
                        reads=(K.pbn[bank],), writes=("pt_tmp",))
                    P.op("v", lambda e, k=k, r=r: e.tensor_tensor(
                        K.pt_all[:, k * 512:(k + 1) * 512], K.pt_tmp[:, :],
                        K.fmask2sb[:, r * 512:(r + 1) * 512], OP.mult),
                        reads=("fmask2sb", "pt_tmp"), writes=("pt_all",))
                else:
                    P.op("s", lambda e, k=k, bank=bank: e.activation(
                        K.pt_all[:, k * 512:(k + 1) * 512], pb[bank][:, :], AF.Exp,
                        scale=SCALE),
                        reads=(K.pbn[bank],), writes=("pt_all",))
            for k in range(kext):
                P.op("t", lambda e, j=j, k=k, kext=kext: e.matmul(
                    pb[2][:, :], K.v_tm[:, k * 1024 + j * 128:k * 1024 + (j + 1) * 128],
                    K.pt_all[:, k * 512:(k + 1) * 512],
                    start=k == 0, stop=k == kext - 1),
                    reads=("v_tm", "pt_all"), writes=(K.pbn[2],))
            for k in range(kext):
                P.op("t", lambda e, k=k, kext=kext: e.matmul(
                    pb[3][0:1, :], K.ones128_b[:, :],
                    K.pt_all[:, k * 512:(k + 1) * 512],
                    start=k == 0, stop=k == kext - 1),
                    reads=("ones128_b", "pt_all"), writes=(K.pbn[3],))
            def _recipl(e):
                with K.nc.allow_low_precision(reason="softmax denom bf16"):
                    return e.reciprocal(K.rl_sb[0:1, :], pb[3][0:1, :])
            P.op("v", _recipl, reads=(K.pbn[3],), writes=("rl_sb",))
            P.op("t", lambda e: e.matmul(pb[4][:, :], K.ones1_b[:, :], K.rl_sb[0:1, :],
                                         start=True, stop=True),
                 reads=("ones1_b", "rl_sb"), writes=(K.pbn[4],))
            P.op("s", lambda e: e.copy(K.rb_sb[:, :], pb[4][:, :]),
                 reads=(K.pbn[4],), writes=("rb_sb",))
            P.op("v", lambda e, h=h, p=p: e.tensor_tensor(
                K.base_fm[:, h * 1024 + p * 512:h * 1024 + (p + 1) * 512],
                pb[2][:, :], K.rb_sb[:, :], OP.mult),
                reads=(K.pbn[2], "rb_sb"), writes=("base_fm",))


def x1sl(K, tto, c0=0, c1=2048):
    buf = K.x1a if tto < 4 else K.x1b
    nm = "x1a" if tto < 4 else "x1b"
    return buf[:, (tto % 4) * 2048 + c0:(tto % 4) * 2048 + c1], nm


def ph_wo(K):
    P = K.P
    pb = K.pb
    P.barrier()
    for tto in range(8):
        xs, nm = x1sl(K, tto)
        P.op("sy", lambda e, tto=tto, xs=xs: e.dma_start(
            out=xs, in_=K.xown_d[tto * 128:(tto + 1) * 128, :]), writes=(nm,))
    for fg in range(4):
        slab = K.woslab[fg % 2]
        snm = f"woslab{fg % 2}"
        P.op("gp", lambda e, slab=slab, fg=fg: e.dma_start(
            out=slab.rearrange("q (c n) -> q c n", n=512),
            in_=K.wo_d.rearrange("(c q) n -> q c n", q=128)[:, :, fg * 512:(fg + 1) * 512]),
            writes=(snm,), dma=True)
        for tto in range(8):
            bank = tto % 2
            for fc in range(16):
                P.op("t", lambda e, slab=slab, fc=fc, tto=tto, bank=bank: e.matmul(
                    pb[bank][:, :], K.base_fm[:, fc * 1024 + tto * 128:fc * 1024 + (tto + 1) * 128],
                    slab[:, fc * 512:(fc + 1) * 512], start=fc == 0, stop=fc == 15),
                    reads=("base_fm", snm), writes=(K.pbn[bank],))
            xs, nm = x1sl(K, tto, fg * 512, (fg + 1) * 512)
            P.op("v", lambda e, xs=xs, bank=bank: e.scalar_tensor_tensor(
                xs, pb[bank][:, :], 0.99, xs, OP.mult, OP.add),
                reads=(K.pbn[bank],), writes=(nm,))
            P.op("v", lambda e, xs=xs, tto=tto, fg=fg: e.scalar_tensor_tensor(
                xs, K.fla[:, tto * 2048 + fg * 512:tto * 2048 + (fg + 1) * 512],
                0.01, xs, OP.mult, OP.add),
                reads=("fla",), writes=(nm,))


def ph_ffn(K):
    P = K.P
    pb = K.pb
    P.barrier()
    P.op("gp", lambda e: e.dma_start(
        out=K.gatewsb.rearrange("q (c n) -> q c n", n=8),
        in_=K.gw_d.rearrange("(c q) n -> q c n", q=128)), writes=("gatewsb",), dma=True)
    P.op("gp", lambda e: e.dma_start(out=K.gatebsb[:, :], in_=K.gb_d[:, :]),
         writes=("gatebsb",), dma=True)
    for tto in range(8):
        xs, nm = x1sl(K, tto)
        P.op("v", lambda e, xs=xs: e.reduce_sum(K.ln_a[:, :], xs, axis=AX.X),
             reads=(nm,), writes=("ln_a",))
        P.op("s", lambda e: e.activation(K.ln_b[:, :], K.ln_a[:, :], AF.Copy,
                                         scale=1.0 / HID),
             reads=("ln_a",), writes=("ln_b",))
        P.op("v", lambda e, xs=xs: e.tensor_scalar(K.ln2sc[:, :], xs, K.ln_b[:, :],
                                                   None, OP.subtract),
             reads=(nm, "ln_b"), writes=("ln2sc",))
        P.op("s", lambda e: e.activation(K.ln2st[:, :], K.ln2sc[:, :], AF.Square),
             reads=("ln2sc",), writes=("ln2st",))
        P.op("v", lambda e: e.reduce_sum(K.ln_c[:, :], K.ln2st[:, :], axis=AX.X),
             reads=("ln2st",), writes=("ln_c",))
        P.op("s", lambda e: e.activation(K.ln_d[:, :], K.ln_c[:, :], AF.Sqrt,
                                         scale=1.0 / HID, bias=EPS),
             reads=("ln_c", "cpool"), writes=("ln_d",))
        P.op("v", lambda e: e.reciprocal(K.ln_e[:, :], K.ln_d[:, :]),
             reads=("ln_d",), writes=("ln_e",))
        P.op("v", lambda e: e.tensor_scalar(K.ln2sc[:, :], K.ln2sc[:, :], K.ln_e[:, :],
                                            None, OP.mult),
             reads=("ln_e",), writes=("ln2sc",))
        for grp in range(4):
            bank = grp % 2
            for kk in range(4):
                fc = grp * 4 + kk
                P.op("t", lambda e, fc=fc, bank=bank, kk=kk: e.transpose(
                    pb[bank][:, kk * 128:(kk + 1) * 128],
                    K.ln2sc[:, fc * 128:(fc + 1) * 128], K.id_f[:, :]),
                    reads=("ln2sc", "id_f"), writes=(K.pbn[bank],))
            for kk in range(4):
                fc = grp * 4 + kk
                P.op("s", lambda e, fc=fc, kk=kk, bank=bank, tto=tto: e.copy(
                    K.xh2_fm[:, fc * 1024 + tto * 128:fc * 1024 + (tto + 1) * 128],
                    pb[bank][:, kk * 128:(kk + 1) * 128]),
                    reads=(K.pbn[bank],), writes=("xh2_fm",))
        P.op("t", lambda e: e.matmul(pb[5][:, 0:8], K.ones1_b[:, :],
                                     K.gatebsb[0:1, :], start=True, stop=False),
             reads=("ones1_b", "gatebsb"), writes=(K.pbn[5],))
        for fc in range(16):
            P.op("t", lambda e, fc=fc, tto=tto: e.matmul(
                pb[5][:, 0:8], K.xh2_fm[:, fc * 1024 + tto * 128:fc * 1024 + (tto + 1) * 128],
                K.gatewsb[:, fc * 8:(fc + 1) * 8], start=False, stop=fc == 15),
                reads=("xh2_fm", "gatewsb"), writes=(K.pbn[5],))
        P.op("v", lambda e: e.reduce_max(K.gm1[:, :], pb[5][:, 0:8], axis=AX.X),
             reads=(K.pbn[5],), writes=("gm1",))
        P.op("v", lambda e: e.tensor_scalar(K.gt1[:, :], pb[5][:, 0:8], K.gm1[:, :],
                                            None, OP.is_ge),
             reads=(K.pbn[5], "gm1"), writes=("gt1",))
        P.op("v", lambda e: e.scalar_tensor_tensor(K.gt2[:, :], K.gt1[:, :], -1e30,
                                                   pb[5][:, 0:8], OP.mult, OP.add),
             reads=("gt1", K.pbn[5]), writes=("gt2",))
        P.op("v", lambda e: e.reduce_max(K.gm2[:, :], K.gt2[:, :], axis=AX.X),
             reads=("gt2",), writes=("gm2",))
        P.op("v", lambda e, tto=tto: e.tensor_scalar(
            K.wcols[:, tto * 8:(tto + 1) * 8], pb[5][:, 0:8], K.gm2[:, :], 0.5,
            OP.is_ge, OP.mult),
            reads=(K.pbn[5], "gm2"), writes=("wcols",))
    for e_i in range(8):
        brow = K.berow[e_i % 2]
        bnm = f"berow{e_i % 2}"
        P.op("gp", lambda e, brow=brow, e_i=e_i: e.dma_start(
            out=brow[0:1, :], in_=K.eb_d[e_i:e_i + 1, :]), writes=(bnm,), dma=True)
        for fg in range(4):
            si = (e_i * 4 + fg) % 2
            slab = K.weslab[si]
            snm = f"weslab{si}"
            P.op("gp", lambda e, slab=slab, e_i=e_i, fg=fg: e.dma_start(
                out=slab.rearrange("q (c n) -> q c n", n=512),
                in_=K.ew_d.rearrange("e (c q) n -> e q c n", q=128)[
                    e_i, :, :, fg * 512:(fg + 1) * 512]),
                writes=(snm,), dma=True)
            for tto in range(8):
                bank = 2 + tto % 2
                P.op("t", lambda e, brow=brow, fg=fg, bank=bank: e.matmul(
                    pb[bank][:, :], K.ones1_b[:, :],
                    brow[0:1, fg * 512:(fg + 1) * 512], start=True, stop=False),
                    reads=("ones1_b", bnm), writes=(K.pbn[bank],))
                for fc in range(16):
                    P.op("t", lambda e, slab=slab, fc=fc, tto=tto, bank=bank: e.matmul(
                        pb[bank][:, :],
                        K.xh2_fm[:, fc * 1024 + tto * 128:fc * 1024 + (tto + 1) * 128],
                        slab[:, fc * 512:(fc + 1) * 512], start=False, stop=fc == 15),
                        reads=("xh2_fm", snm), writes=(K.pbn[bank],))
                P.op("s", lambda e, bank=bank, tto=tto, e_i=e_i: e.activation(
                    K.ffnu[:, :], pb[bank][:, :], AF.Relu,
                    scale=K.wcols[:, tto * 8 + e_i:tto * 8 + e_i + 1]),
                    reads=(K.pbn[bank], "wcols"), writes=("ffnu",))
                xs, nm = x1sl(K, tto, fg * 512, (fg + 1) * 512)
                P.op("v", lambda e, xs=xs: e.tensor_tensor(xs, xs, K.ffnu[:, :], OP.add),
                     reads=("ffnu",), writes=(nm,))
    for tto in range(8):
        xs, nm = x1sl(K, tto)
        P.op("sy", lambda e, tto=tto, xs=xs: e.dma_start(
            out=K.out_d[tto * 128:(tto + 1) * 128, :], in_=xs), reads=(nm,))


def build_full(debug=(), upto="all"):
    K = make_K(debug)
    ph_consts(K)
    ph_b1(K)
    ph_prologue(K)
    if upto != "b1":
        ph_b2(K)
        if upto != "b2":
            ph_delta(K)
            if upto != "delta":
                ph_flash(K)
                if upto != "flash":
                    ph_wo(K)
                    ph_ffn(K)
    if K.dbg:
        dbg_dump(K, [n for n in K.dbg if not n.endswith("_h")])
    return finish(K)


def kernel(**inputs):
    nc = build_full()
    maps = shard_inputs(inputs)
    res = run_bass_kernel_spmd(nc, maps, core_ids=list(range(8)))
    return assemble_out(res.results)


# version 05
# speedup vs baseline: 1.2239x; 1.0405x over previous
import math
import types
import numpy as np
import concourse.bass as bass
import concourse.mybir as mybir
from concourse.bass_utils import run_bass_kernel_spmd

F32 = mybir.dt.float32
BF16 = mybir.dt.bfloat16
U32 = mybir.dt.uint32
U8 = mybir.dt.uint8
AF = mybir.ActivationFunctionType
OP = mybir.AluOpType
AX = mybir.AxisListType

B, N, HID = 4, 2048, 2048
H, HKV, D = 16, 8, 128
NT = 16
SCALE = 1.0 / math.sqrt(D)
EPS = 1e-5

DEBUG_TILE_BARRIER = False
FULL_SERIAL = False
SER_MARKED = True
STATIC_PIECE = False
DELTA_CHUNKS = 16
DELTA_J = 8
DELTA_UPTO = 8

SBUF_BASE = 16512
SBUF_LIMIT = 212832


class Ctr:
    __slots__ = ("h", "n", "by")

    def __init__(self, h, by=1):
        self.h, self.n, self.by = h, 0, by


SCALARISH = {"ln_a", "ln_b", "ln_c", "ln_d", "ln_e", "dn4", "dn4r", "gt1",
             "gt2", "gm1", "gm2", "kdcol", "kpcol", "rl_sb", "rde", "wcols",
             "bgsb"}


class Buf:
    __slots__ = ("name", "w", "rs", "scalar")

    def __init__(self, name):
        self.name, self.w, self.rs = name, [], []
        self.scalar = name in SCALARISH


class Stream:
    def __init__(self, name, ctr, self_waits=False):
        self.name, self.ctr = name, ctr
        self.self_waits = self_waits
        self.items = []
        self.waited = {}
        self.sink = self.items

    dma_ctr = None

    def raw(self, fn, waits=(), inc=None, self_ok=False):
        ws = []
        for t in waits:
            if t is None:
                continue
            c, v = t
            if c is self.ctr and not (self.self_waits and self_ok):
                continue
            if self.waited.get(id(c), 0) >= v:
                continue
            self.waited[id(c)] = v
            ws.append((c, v))
        tok = None
        if inc is not None:
            inc.n += inc.by
            tok = (inc, inc.n)
        self.sink.append((ws, fn, inc, inc.by if inc else 0))
        return tok


def _merge_toks(toks):
    best = {}
    for c, v in toks:
        if best.get(id(c), (None, -1))[1] < v:
            best[id(c)] = (c, v)
    return list(best.values())


class Prog:
    def __init__(self, nc):
        self.nc = nc
        self.ctrs = {}
        for nm in ("t", "v", "s", "sy", "gp", "gq"):
            by = 16 if nm in ("sy", "gq") else 1
            self.ctrs[nm] = Ctr(nc.alloc_semaphore(name=f"c_{nm}"), by)
        self.st = {nm: Stream(nm, self.ctrs[nm], self_waits=nm in ("v", "s"))
                   for nm in ("t", "v", "s", "sy", "gp")}
        self.st["gp"].dma_ctr = self.ctrs["gq"]
        self.nopfns = {}
        self.bufs = {}
        self.pid_ap = None

    def buf(self, name):
        b = self.bufs.get(name)
        if b is None:
            b = self.bufs[name] = Buf(name)
        return b

    def op(self, snm, fn, reads=(), writes=(), extra=(), ser=False, dma=False):
        if (FULL_SERIAL is True or (isinstance(FULL_SERIAL, str) and snm in FULL_SERIAL)
                or (ser and SER_MARKED)):
            self.barrier()
        s = self.st[snm]
        waits = list(extra) + list(self.buf("__barrier__").w)
        rb = [self.buf(x) for x in reads]
        wb = [self.buf(x) for x in writes]
        for b in rb:
            waits += b.w
        for b in wb:
            waits += b.rs
            waits += b.w
        ctr = s.dma_ctr if (dma and s.dma_ctr is not None) else s.ctr
        self_ok = any(b.scalar for b in rb)
        tok = s.raw(fn, waits=waits, inc=ctr, self_ok=self_ok)
        for b in rb:
            b.rs.append(tok)
        for b in wb:
            b.w = [tok]
            b.rs = []
        return tok

    def barrier(self):
        toks = [(c, c.n) for c in self.ctrs.values() if c.n > 0]
        for b in self.bufs.values():
            b.w = _merge_toks(b.w + toks)
            b.rs = []
        bb = self.buf("__barrier__")
        bb.w = _merge_toks(toks)

    def bwait(self):
        return list(self.buf("__barrier__").w)

    def begin_branch(self):
        self._snap_ctr = {k: c.n for k, c in self.ctrs.items()}
        self._snap_buf = {k: (list(b.w), list(b.rs)) for k, b in self.bufs.items()}
        self._snap_wait = {k: dict(s.waited) for k, s in self.st.items()}
        self._sideA = {k: [] for k in self.st}
        self._sideB = {k: [] for k in self.st}
        for k, s in self.st.items():
            s.sink = self._sideA[k]

    def mid_branch(self):
        self._a_ctr = {k: c.n for k, c in self.ctrs.items()}
        self._a_buf = {k: (list(b.w), list(b.rs)) for k, b in self.bufs.items()}
        for k, c in self.ctrs.items():
            c.n = self._snap_ctr[k]
        for k, b in self.bufs.items():
            w, r = self._snap_buf.get(k, ([], []))
            b.w, b.rs = list(w), list(r)
        for k, s in self.st.items():
            s.waited = dict(self._snap_wait[k])
            s.sink = self._sideB[k]

    def end_branch(self):
        for k in self.st:
            fn = self.nopfns[k]
            for side in (self._sideA[k], self._sideB[k]):
                if not side:
                    side.append(((), fn, None, 0))
        b_ctr = {k: c.n for k, c in self.ctrs.items()}
        b_buf = {k: (list(b.w), list(b.rs)) for k, b in self.bufs.items()}
        for k, c in self.ctrs.items():
            hi = max(self._a_ctr[k], b_ctr[k])
            if hi - self._a_ctr[k]:
                self._sideA[k].append(("pad", c, hi - self._a_ctr[k]))
            if hi - b_ctr[k]:
                self._sideB[k].append(("pad", c, hi - b_ctr[k]))
            c.n = hi
        for nm in set(self._a_buf) | set(b_buf):
            wa, ra = self._a_buf.get(nm, ([], []))
            wb, rb = b_buf.get(nm, ([], []))
            bb = self.buf(nm)
            bb.w = _merge_toks(wa + wb)
            bb.rs = _merge_toks(ra + rb)
        for k, s in self.st.items():
            s.waited = dict(self._snap_wait[k])
            s.sink = s.items
            s.items.append(("br", self._sideA[k], self._sideB[k]))


def _ensure_regs(K, eng, st):
    if st["preg"] is None:
        preg = eng.alloc_register(f"parity_{st['nm']}")
        roff = eng.alloc_register(f"roff_{st['nm']}")
        for c, v in st["pidw"]:
            eng.wait_ge(c.h, v)
        eng.reg_load(preg, K.pidsb[0:1, 0:1])
        eng.reg_mod(preg, preg, 2)
        st["preg"], st["roff"] = preg, roff


def _emit_items(K, eng, items, st):
    for it in items:
        if it and it[0] == "pad":
            eng.sem_inc(it[1].h, it[2])
        elif it and it[0] == "br":
            _ensure_regs(K, eng, st)
            with eng.If_eq(st["preg"], 0):
                _emit_items(K, eng, it[1], st)
            with eng.Else():
                _emit_items(K, eng, it[2], st)
        else:
            ws, fn, ctr, by = it
            for c, v in ws:
                eng.wait_ge(c.h, v)
            if getattr(fn, "_needs_regs", False):
                _ensure_regs(K, eng, st)
                inst = fn(eng, st["preg"], st["roff"])
            else:
                inst = fn(eng)
            if ctr is not None:
                inst.then_inc(ctr.h, by)


def _emit_engine(K, eng, nm):
    P = K.P
    items = P.st[nm].items
    st = {"preg": None, "roff": None, "nm": nm,
          "pidw": getattr(P, "pidw_new", list(P.bufs["pidsb"].w))}
    _emit_items(K, eng, items, st)


def make_K(debug=()):
    nc = bass.Bass("TRN2", target_bir_lowering=False)
    K = types.SimpleNamespace(nc=nc, dbg=set(debug))
    K.P = Prog(nc)

    K.x_d = nc.dram_tensor("x", [N, HID], F32, kind="ExternalInput").ap()
    K.xown_d = nc.dram_tensor("xown", [1024, HID], F32, kind="ExternalInput").ap()
    K.wq_d = nc.dram_tensor("wq", [HID, 2048], F32, kind="ExternalInput").ap()
    K.wk_d = nc.dram_tensor("wk", [HID, 1024], F32, kind="ExternalInput").ap()
    K.wv_d = nc.dram_tensor("wv", [HID, 1024], F32, kind="ExternalInput").ap()
    K.wo_d = nc.dram_tensor("wo", [2048, HID], F32, kind="ExternalInput").ap()
    K.wg_d = nc.dram_tensor("wg", [D, 1], F32, kind="ExternalInput").ap()
    K.bg_d = nc.dram_tensor("bg", [8, 1], F32, kind="ExternalInput").ap()
    K.gw_d = nc.dram_tensor("gatew", [HID, 8], F32, kind="ExternalInput").ap()
    K.gb_d = nc.dram_tensor("gateb", [1, 8], F32, kind="ExternalInput").ap()
    K.ew_d = nc.dram_tensor("expw", [8, HID, 2048], F32, kind="ExternalInput").ap()
    K.eb_d = nc.dram_tensor("expb", [8, 2048], F32, kind="ExternalInput").ap()
    K.fmask2_d = nc.dram_tensor("fmask2", [128, 4096], BF16, kind="ExternalInput").ap()
    K.pmask_d = nc.dram_tensor("pmask", [128, 256], U8, kind="ExternalInput").ap()
    K.out_d = nc.dram_tensor("out", [1024, HID], F32, kind="ExternalOutput").ap()
    def scratch(name, shape):
        kind = "ExternalOutput" if name in K.dbg else "Internal"
        return nc.dram_tensor(name, shape, BF16, kind=kind).ap()

    K.xfm_h = nc.dram_tensor("xfm_h", [NT, 128, HID], BF16,
                             kind="ExternalOutput").ap()
    K.kfm_h = scratch("kfm_h", [128, HKV * N])
    K.qfm_h = scratch("qfm_h", [128, H * 1024])

    def at(name, shape, dtype, off):
        assert off % 32 == 0, (name, off)
        per = int(np.prod(shape[1:])) * mybir.dt.size(dtype)
        assert off + per <= SBUF_LIMIT, (name, off, per)
        return nc.alloc_sbuf_tensor_at(name, shape, dtype, offset=SBUF_BASE + off).ap()

    o = [0]

    def nx(sz):
        r = o[0]
        o[0] += (sz + 31) // 32 * 32
        return r

    K.id_b = at("id_b", [128, 128], BF16, nx(256))
    K.id_f = at("id_f", [128, 128], F32, nx(512))
    K.ones1_b = at("ones1_b", [1, 128], BF16, nx(256))
    K.ones1_f = at("ones1_f", [1, 128], F32, nx(512))
    K.ones128_b = at("ones128_b", [128, 1], BF16, nx(32))
    K.mstr = at("mstr", [128, 128], F32, nx(512))
    K.mustr = at("mustr", [128, 128], F32, nx(512))
    K.muincs = at("muincs", [128, 128], F32, nx(512))
    K.cumT = at("cumT", [128, 128], F32, nx(512))
    K.lall = at("lall", [128, 128], F32, nx(512))
    K.slall = at("slall", [128, 128], F32, nx(512))
    K.lcb = at("lcb", [128, 128], F32, nx(512))
    K.ccb = at("ccb", [128, 128], F32, nx(512))
    K.sstate = at("sstate", [128, 8 * 128], BF16, nx(2048))
    K.pmasksb = at("pmasksb", [128, 256], U8, nx(256))
    cum_off = nx(8192)
    K.cum = at("cum", [8, N], F32, cum_off)
    K.pidsb = at("pidsb", [1, 1], U32, nx(32))
    K.bgsb = at("bgsb", [8, 1], F32, nx(32))
    K.wgsb = at("wgsb", [128, 1], BF16, nx(32))
    K.ln_a = at("ln_a", [128, 1], F32, nx(32))
    K.ln_b = at("ln_b", [128, 1], F32, nx(32))
    K.ln_c = at("ln_c", [128, 1], F32, nx(32))
    K.ln_d = at("ln_d", [128, 1], F32, nx(32))
    K.ln_e = at("ln_e", [128, 1], F32, nx(32))
    K.dn4 = at("dn4", [128, 8], F32, nx(64))
    K.dn4r = at("dn4r", [128, 8], F32, nx(64))
    K.gt1 = at("gt1", [128, 8], F32, nx(64))
    K.gt2 = at("gt2", [128, 8], F32, nx(64))
    K.gm1 = at("gm1", [128, 1], F32, nx(32))
    K.gm2 = at("gm2", [128, 1], F32, nx(32))
    K.cpool = at("cpool", [128, 2], F32, nx(64))
    K.nopsc = at("nopsc", [1, 8], F32, nx(32))
    nc.const_aps.register(F32, EPS, K.cpool[:, 0:1])
    nc.const_aps.register(F32, 1.0, K.cpool[:, 1:2])

    PEND = (o[0] + 1023) // 1024 * 1024
    R1 = PEND
    R2 = R1 + 32768
    R3 = R2 + 32768
    R4 = R3 + 32768
    R5 = R4 + 32768
    R6 = R5 + 32768
    assert R6 + 22528 <= SBUF_LIMIT, (PEND, R6)
    K.R6_avail = SBUF_LIMIT - R6

    K.ks_tm = at("ks_tm", [128, NT * 1024], BF16, R1)
    K.qfm_bk = at("qfm_bk", [128, H * 1024], BF16, R1)
    K.x1a = at("x1a", [128, 4 * 2048], F32, R1)
    K.v_tm = at("v_tm", [128, NT * 1024], BF16, R2)
    K.xh2_fm = at("xh2_fm", [128, 16 * 1024], BF16, R2)
    K.woslab = [at("woslab0", [128, 16 * 512], BF16, R2),
                at("woslab1", [128, 16 * 512], BF16, R2 + 16384)]
    K.xcb = at("xcb", [128, 2048], BF16, R3)
    K.expst = at("expst", [128, 512], BF16, R3 + 4096)
    K.kfm_st = at("kfm_st", [128, 1024], BF16, R3 + 5120)
    K.t2b = at("t2b", [8, N], F32, R3 + 7168)
    K.gsb = at("gsb", [8, N], F32, R3 + 15360)
    K.fla = at("fla", [128, 8 * 2048], BF16, R3)
    K.weslab = [at("weslab0", [128, 16 * 512], BF16, R3),
                at("weslab1", [128, 16 * 512], BF16, R3 + 16384)]
    K.wkv = at("wkv", [128, 16 * 2048], BF16, R4)
    K.qs_fm = at("qs_fm", [128, H * 1024], BF16, R4)
    K.base_fm = at("base_fm", [128, H * 1024], BF16, R4)
    K.xpiece = at("xpiece", [128, 16 * 512], BF16, R5)
    K.wqslab = [at("wqslab0", [128, 16 * 512], BF16, R5 + 16384),
                at("wqslab1", [128, 16 * 512], BF16, R6)]
    K.kfm_bk = at("kfm_bk", [128, HKV * N], BF16, R5)
    K.x1b = at("x1b", [128, 4 * 2048], F32, R5)
    ds = [R5]

    def dn(sz):
        r = ds[0]
        ds[0] += (sz + 31) // 32 * 32
        return r

    K.dsets = []
    for si in range(2):
        d = types.SimpleNamespace()
        for nm2, dt2, cols in [("ksfm_c", BF16, 128), ("a_sb", BF16, 128),
                               ("at_sb", BF16, 128), ("ap0", BF16, 128),
                               ("ap1", BF16, 128), ("apt0", BF16, 128),
                               ("apt1", BF16, 128), ("z0", BF16, 128),
                               ("z1", BF16, 128), ("lk_sb", BF16, 128),
                               ("kp_sb", BF16, 128), ("nmt_sb", BF16, 128),
                               ("uc_sb", BF16, 128), ("oa_sb", BF16, 128),
                               ("bt_sb", BF16, 128), ("dtmp", BF16, 128),
                               ("e1", F32, 128), ("e2", F32, 128),
                               ("e1m", F32, 128), ("e2m", F32, 128),
                               ("ebt", F32, 128), ("dn1", F32, 128),
                               ("kdcol", F32, 1), ("kpcol", F32, 1)]:
            sz = cols * (2 if dt2 is BF16 else 4)
            setattr(d, nm2, at(f"{nm2}_{si}", [128, cols], dt2, dn(sz)))
            setattr(d, nm2 + "_n", f"{nm2}_{si}")
        K.dsets.append(d)
    K.cumrow = at("cumrow", [1, 1024], F32, dn(4096))
    assert ds[0] <= R5 + 32768

    K.xt = [at("xt0", [128, 2048], F32, R6),
            at("xt1", [128, 2048], F32, R6 + 8192)]
    K.sqsc = at("sqsc", [128, 2048], F32, R6 + 16384)
    K.kraw_st = at("kraw_st", [128, 1024], F32, R6 + 24576)
    K.gstage = at("gstage", [1, 1024], F32, R6 + 24576)
    K.t1b = at("t1b", [8, N], F32, R6)
    K.rmask = at("rmask", [8, N], F32, R6 + 8192)
    K.pt_all = [at("pt_all0", [128, 16 * 512], BF16, R6),
                at("pt_all1", [128, 8 * 512], BF16, R6 + 16384)]
    K.rb_sb = [at("rb_sb0", [128, 512], BF16, R6 + 24576),
               at("rb_sb1", [128, 512], BF16, R6 + 25600)]
    K.rl_sb = [at("rl_sb0", [1, 512], BF16, R6 + 26624),
               at("rl_sb1", [1, 512], BF16, R6 + 27648)]
    K.fmask2sb = at("fmask2sb", [128, 4096], BF16, cum_off)
    K.pt_tmp = [at("pt_tmp0", [128, 512], BF16, R6 + 28672),
                at("pt_tmp1", [128, 512], BF16, R6 + 29696)]
    K.ln2sc = at("ln2sc", [128, 2048], F32, R6)
    K.berow = [at("berow0", [1, 2048], BF16, R6 + 8192),
               at("berow1", [1, 2048], BF16, R6 + 12288)]
    K.wcols = at("wcols", [128, 64], F32, R6 + 16384)
    K.gatewsb = at("gatewsb", [128, 128], BF16, R6 + 16672)
    K.gatebsb = at("gatebsb", [1, 8], BF16, R6 + 16960)
    K.ln2st = at("ln2st", [128, 2048], BF16, R6 + 20480)
    K.ffnu = at("ffnu", [128, 512], BF16, R6 + 24576)
    K.qse = at("qse", [128, 512], BF16, R6 + 17408)
    K.qse2 = at("qse2", [128, 512], BF16, R6 + 18432)
    K.rde = at("rde", [1, 512], BF16, R6 + 19456)

    K.P.nopfns = {
        "t": lambda e: e.matmul(K.pb[0][0:1, 0:1], K.ones1_b[0:1, 0:1],
                                K.ones1_b[0:1, 0:1], start=True, stop=True),
        "v": lambda e: e.memset(K.nopsc[0:1, 0:1], 0.0),
        "s": lambda e: e.memzero(K.nopsc[0:1, 1:2]),
        "gp": lambda e: e.memset(K.nopsc[0:1, 2:3], 0.0),
        "sy": lambda e: e.nop(),
    }
    K.pb = [nc.alloc_psum_tensor(f"pb{i}", [128, 512], F32).ap() for i in range(7)]
    K.pb7b = nc.alloc_psum_tensor("pb7b", [128, 1024], BF16).ap()
    K.pbn = [f"pb{i}" for i in range(7)] + ["pb7b"]
    return K


def ph_consts(K):
    P = K.P

    def g(fn, **kw):
        P.op("gp", fn, **kw)

    def mask_init(ap, nm, fill_val, cmp, base, pattern, chm):
        g(lambda e: e.memset(ap[:, :], fill_val), writes=(nm,))
        g(lambda e: e.affine_select(out=ap[:, :], in_=ap[:, :], compare_op=cmp,
                                    fill=0.0, base=base, pattern=pattern,
                                    channel_multiplier=chm), writes=(nm,))

    g(lambda e: e.memset(K.id_b[:, :], 0.0), writes=("id_b",))
    g(lambda e: e.affine_select(out=K.id_b[:, :], in_=K.id_b[:, :], compare_op=OP.not_equal,
                                fill=1.0, base=0, pattern=[[-1, 128]], channel_multiplier=1),
      writes=("id_b",))
    g(lambda e: e.memset(K.id_f[:, :], 0.0), writes=("id_f",))
    g(lambda e: e.affine_select(out=K.id_f[:, :], in_=K.id_f[:, :], compare_op=OP.not_equal,
                                fill=1.0, base=0, pattern=[[-1, 128]], channel_multiplier=1),
      writes=("id_f",))
    g(lambda e: e.memset(K.ones1_b[:, :], 1.0), writes=("ones1_b",))
    g(lambda e: e.memset(K.ones1_f[:, :], 1.0), writes=("ones1_f",))
    g(lambda e: e.memset(K.ones128_b[:, :], 1.0), writes=("ones128_b",))
    mask_init(K.mstr, "mstr", 1.0, OP.is_gt, 0, [[-1, 128]], 1)
    mask_init(K.mustr, "mustr", 1.0, OP.is_gt, 0, [[1, 128]], -1)
    mask_init(K.muincs, "muincs", SCALE, OP.is_ge, 0, [[1, 128]], -1)
    g(lambda e: e.memset(K.sstate[:, :], 0.0),
      writes=tuple(f"sstate{j}" for j in range(8)))
    g(lambda e: e.memset(K.cpool[:, 0:1], EPS), writes=("cpool",))
    g(lambda e: e.memset(K.cpool[:, 1:2], 1.0), writes=("cpool",))

    P.pid_tok_old = [P.op("sy", lambda e: e.dma_start(
        out=K.pidsb[:, :], in_=K.nc.partition_id_tensor.ap()[0:1, 0:1]),
        writes=("pidsb",))]
    P.op("sy", lambda e: e.dma_start(out=K.bgsb[:, :], in_=K.bg_d[:, :]), writes=("bgsb",))
    P.op("gp", lambda e: e.dma_start(out=K.wgsb[:, :], in_=K.wg_d[:, :]),
         writes=("wgsb",), dma=True)
    P.op("gp", lambda e: e.dma_start(
        out=K.wkv.rearrange("p (c n) -> p c n", n=2048)[:, :, 0:1024],
        in_=K.wk_d.rearrange("(c p) n -> p c n", p=128)), writes=("wkv",), dma=True)
    P.op("gp", lambda e: e.dma_start(
        out=K.wkv.rearrange("p (c n) -> p c n", n=2048)[:, :, 1024:2048],
        in_=K.wv_d.rearrange("(c p) n -> p c n", p=128)), writes=("wkv",), dma=True)


def ph_b1(K):
    P = K.P
    pb = K.pb
    for t in range(NT):
        if DEBUG_TILE_BARRIER:
            P.barrier()
        xb = K.xt[t % 2]
        xbuf = f"xt{t % 2}"
        P.op("sy", lambda e, t=t, xb=xb: e.dma_start(out=xb[:, :],
                                                     in_=K.x_d[t * 128:(t + 1) * 128, :]),
             writes=(xbuf,))
        P.op("v", lambda e, xb=xb: e.reduce_sum(K.ln_a[:, :], xb[:, :], axis=AX.X),
             reads=(xbuf,), writes=("ln_a",))
        P.op("s", lambda e: e.activation(K.ln_b[:, :], K.ln_a[:, :], AF.Copy, scale=1.0 / HID),
             reads=("ln_a",), writes=("ln_b",))
        P.op("v", lambda e, xb=xb: e.tensor_scalar(xb[:, :], xb[:, :], K.ln_b[:, :], None,
                                                   OP.subtract),
             reads=("ln_b",), writes=(xbuf,))
        P.op("s", lambda e, xb=xb: e.activation(K.sqsc[:, :], xb[:, :], AF.Square),
             reads=(xbuf,), writes=("sqsc",))
        P.op("v", lambda e: e.reduce_sum(K.ln_c[:, :], K.sqsc[:, :], axis=AX.X),
             reads=("sqsc",), writes=("ln_c",))
        P.op("s", lambda e: e.activation(K.ln_d[:, :], K.ln_c[:, :], AF.Sqrt,
                                         scale=1.0 / HID, bias=EPS),
             reads=("ln_c", "cpool"), writes=("ln_d",))
        P.op("v", lambda e: e.reciprocal(K.ln_e[:, :], K.ln_d[:, :]),
             reads=("ln_d",), writes=("ln_e",))
        P.op("v", lambda e, xb=xb: e.tensor_scalar(xb[:, :], xb[:, :], K.ln_e[:, :],
                                                   None, OP.mult),
             reads=("ln_e",), writes=(xbuf,))
        for grp in range(4):
            bank = grp % 2
            for kk in range(4):
                fc = grp * 4 + kk
                P.op("t", lambda e, xb=xb, fc=fc, bank=bank, kk=kk: e.transpose(
                    pb[bank][:, kk * 128:(kk + 1) * 128],
                    xb[:, fc * 128:(fc + 1) * 128], K.id_f[:, :]),
                    reads=(xbuf, "id_f"), writes=(K.pbn[bank],))
            P.op("s", lambda e, grp=grp, bank=bank: e.copy(
                K.xcb[:, grp * 512:(grp + 1) * 512], pb[bank][:, :]),
                reads=(K.pbn[bank],), writes=("xcb",))
        P.op("sy", lambda e, t=t: e.dma_start(out=K.xfm_h[t], in_=K.xcb[:, :]),
             reads=("xcb",))
        for bi, co in ((2, 0), (3, 512), (4, 1024), (5, 1536)):
            for fc in range(16):
                st, sp = fc == 0, fc == 15
                P.op("t", lambda e, fc=fc, st=st, sp=sp, bi=bi, co=co: e.matmul(
                    pb[bi][:, :], K.xcb[:, fc * 128:(fc + 1) * 128],
                    K.wkv[:, fc * 2048 + co:fc * 2048 + co + 512], start=st, stop=sp),
                    reads=("xcb", "wkv"), writes=(K.pbn[bi],))
        for half in range(2):
            kb, vb = K.pbn[2 + half], K.pbn[4 + half]
            P.op("s", lambda e, half=half: e.copy(
                K.kraw_st[:, half * 512:(half + 1) * 512], pb[2 + half][:, :]),
                reads=(kb,), writes=("gkovl",))
            P.op("s", lambda e, half=half: e.activation(K.expst[:, :], pb[2 + half][:, :], AF.Exp),
                reads=(kb,), writes=("expst",))
            P.op("v", lambda e, half=half: e.reduce_sum(
                K.dn4[:, half * 4:(half + 1) * 4],
                K.expst.rearrange("p (s d) -> p s d", d=128), axis=AX.X),
                reads=("expst",), writes=("dn4",))
            P.op("v", lambda e, half=half: e.reciprocal(
                K.dn4r[:, half * 4:(half + 1) * 4], K.dn4[:, half * 4:(half + 1) * 4]),
                reads=("dn4",), writes=("dn4r",))
            for s4 in range(4):
                j = half * 4 + s4
                P.op("v", lambda e, t=t, j=j, s4=s4: e.tensor_scalar(
                    K.ks_tm[:, t * 1024 + j * 128:t * 1024 + (j + 1) * 128],
                    K.expst[:, s4 * 128:(s4 + 1) * 128], K.dn4r[:, j:j + 1], None, OP.mult),
                    reads=("expst", "dn4r"), writes=("ks_tm",))
            P.op("s", lambda e, t=t, half=half: e.copy(
                K.v_tm[:, t * 1024 + half * 512:t * 1024 + (half + 1) * 512],
                pb[4 + half][:, :]),
                reads=(vb,), writes=("v_tm",))
        for grp in range(2):
            for kk in range(4):
                j = grp * 4 + kk
                P.op("t", lambda e, j=j, kk=kk: e.transpose(
                    pb[6][:, kk * 128:(kk + 1) * 128],
                    K.kraw_st[:, j * 128:(j + 1) * 128], K.id_f[:, :]),
                    reads=("gkovl", "id_f"), writes=(K.pbn[6],))
            P.op("s", lambda e, grp=grp: e.copy(
                K.kfm_st[:, grp * 512:(grp + 1) * 512], pb[6][:, :]),
                reads=(K.pbn[6],), writes=("kfm_st",))
        P.op("sy", lambda e, t=t: e.dma_start(
            out=K.kfm_h.rearrange("p (j n) -> p j n", n=N)[:, :, t * 128:(t + 1) * 128],
            in_=K.kfm_st.rearrange("p (j n) -> p j n", n=128)),
            reads=("kfm_st",))
        for j in range(8):
            bi = 2 if j < 4 else 3
            P.op("t", lambda e, j=j, bi=bi: e.matmul(
                K.pb[bi][0:1, (j % 4) * 128:((j % 4) + 1) * 128], K.wgsb[:, :],
                K.kfm_st[:, j * 128:(j + 1) * 128], start=True, stop=True),
                reads=("kfm_st", "wgsb"), writes=(K.pbn[bi],))
        P.op("s", lambda e: e.copy(K.gstage[:, 0:512], pb[2][0:1, 0:512]),
             reads=(K.pbn[2], "kfm_st"), writes=("gkovl",))
        P.op("s", lambda e: e.copy(K.gstage[:, 512:1024], pb[3][0:1, 0:512]),
             reads=(K.pbn[3],), writes=("gkovl",))
        P.op("sy", lambda e, t=t: e.dma_start(
            out=K.gsb[0:8, t * 128:(t + 1) * 128],
            in_=K.gstage[0:1, 0:1024]),
            reads=("gkovl",), writes=("gsb",))


def ph_prologue(K):
    P = K.P
    if "v_early" in K.dbg:
        d = K.nc.dram_tensor("dbg_v_early", [128, NT * 1024], BF16,
                             kind="ExternalOutput").ap()
        P.op("sy", lambda e, d=d: e.dma_start(out=d, in_=K.v_tm[:, :]), reads=("v_tm",))
    pb = K.pb
    P.barrier()
    P.op("gp", lambda e: e.memset(K.rmask[:, :], 1.0), writes=("rmask",))
    P.op("gp", lambda e: e.affine_select(out=K.rmask[:, :], in_=K.rmask[:, :],
                                         compare_op=OP.is_gt, fill=0.0, base=0,
                                         pattern=[[0, 16], [1, 128]], channel_multiplier=0),
         writes=("rmask",))
    if "rmask_early" in K.dbg:
        dearly = K.nc.dram_tensor("dbg_rmask_early", [8, N], F32,
                                  kind="ExternalOutput").ap()
        P.op("sy", lambda e, d=dearly: e.dma_start(out=d, in_=K.rmask[:, :]),
             reads=("rmask",))
    P.op("v", lambda e: e.tensor_scalar(K.gsb[:, :], K.gsb[:, :], K.bgsb[:, :], None, OP.add),
         reads=("bgsb",), writes=("gsb",))
    P.op("s", lambda e: e.activation(K.t1b[:, :], K.gsb[:, :], AF.Abs),
         reads=("gsb",), writes=("t1b",))
    P.op("s", lambda e: e.activation(K.t1b[:, :], K.t1b[:, :], AF.Exp, scale=-1.0),
         reads=("t1b",), writes=("t1b",))
    P.op("s", lambda e: e.activation(K.t2b[:, :], K.t1b[:, :], AF.Ln, bias=1.0),
         reads=("t1b", "cpool"), writes=("t2b",))
    P.op("v", lambda e: e.tensor_scalar(K.t1b[:, :], K.gsb[:, :], 0.0, None, OP.min),
         reads=("gsb",), writes=("t1b",))
    P.op("v", lambda e: e.tensor_sub(K.t1b[:, :], K.t1b[:, :], K.t2b[:, :]),
         reads=("t1b", "t2b"), writes=("t1b",))
    P.op("s", lambda e: e.activation(K.gsb[:, :], K.t1b[:, :], AF.Copy, scale=1.0 / 16.0),
         reads=("t1b",), writes=("gsb",))
    P.op("v", lambda e: e.tensor_tensor_scan(K.cum[:, :], K.rmask[:, :], K.gsb[:, :], 0.0,
                                             OP.mult, OP.add),
         reads=("rmask", "gsb"), writes=("cum",))
    for c in range(16):
        P.op("t", lambda e, c=c: e.transpose(pb[0][0:128, 0:8],
                                             K.cum[0:8, c * 128:(c + 1) * 128],
                                             K.id_f[0:8, 0:8]),
             reads=("cum", "id_f"), writes=(K.pbn[0],))
        P.op("s", lambda e, c=c: e.copy(K.cumT[:, c * 8:(c + 1) * 8], pb[0][0:128, 0:8]),
             reads=(K.pbn[0],), writes=("cumT",))
    P.op("s", lambda e: e.activation(K.lall[:, :], K.cumT[:, :], AF.Exp),
         reads=("cumT",), writes=("lall",))
    P.op("v", lambda e: e.tensor_scalar(K.slall[:, :], K.lall[:, :], SCALE, None, OP.mult),
         reads=("lall",), writes=("slall",))
    P.op("sy", lambda e: e.dma_start(out=K.cumrow[0:1, 0:128], in_=K.cumT[127:128, :]),
         reads=("cumT",), writes=("cumrow",))
    P.op("t", lambda e: e.matmul(pb[1][:, 0:128], K.ones1_f[:, :], K.cumrow[0:1, 0:128],
                                 start=True, stop=True),
         reads=("ones1_f", "cumrow"), writes=(K.pbn[1],))
    P.op("s", lambda e: e.copy(K.ccb[:, :], pb[1][:, 0:128]), reads=(K.pbn[1],), writes=("ccb",))
    P.op("s", lambda e: e.activation(K.lcb[:, :], pb[1][:, 0:128], AF.Exp),
         reads=(K.pbn[1],), writes=("lcb",))


def dbg_dump(K, names):
    P = K.P
    for nm in names:
        if not hasattr(K, nm):
            continue
        ap = getattr(K, nm)
        d = K.nc.dram_tensor("dbg_" + nm, list(ap.shape), ap.dtype,
                             kind="ExternalOutput").ap()
        P.op("sy", lambda e, d=d, ap=ap: e.dma_start(out=d, in_=ap), reads=(nm,))


def _collect_waited(items, used):
    for it in items:
        if it and it[0] == "pad":
            continue
        if it and it[0] == "br":
            _collect_waited(it[1], used)
            _collect_waited(it[2], used)
            continue
        ws, fn, ctr, by = it
        for c, v in ws:
            used.setdefault(id(c), set()).add(v)


def _number_incs(items, used, newv, cnt, keep_all=False):
    out = []
    for it in items:
        if it and it[0] == "pad":
            _, c, amt = it
            st = cnt.setdefault(id(c), [0, 0])
            u = used.get(id(c), ())
            kept = sum(1 for v in range(st[0] + 1, st[0] + amt + 1) if v in u)
            for v in range(st[0] + 1, st[0] + amt + 1):
                st_new = st[1] + sum(1 for q in range(st[0] + 1, v + 1) if q in u)
                newv.setdefault(id(c), {})[v] = st_new
            st[0] += amt
            st[1] += kept
            if kept:
                out.append(("pad", c, kept))
            continue
        if it and it[0] == "br":
            snap = {k: [vv[0], vv[1]] for k, vv in cnt.items()}
            a = _number_incs(it[1], used, newv, cnt, keep_all)
            enda = {k: [vv[0], vv[1]] for k, vv in cnt.items()}
            for k in list(cnt.keys()):
                cnt[k] = [snap.get(k, [0, 0])[0], snap.get(k, [0, 0])[1]]
            b = _number_incs(it[2], used, newv, cnt, keep_all)
            for k in set(cnt) | set(enda):
                ca = enda.get(k, [0, 0])
                cb = cnt.get(k, [0, 0])
                assert ca == cb or k not in enda or k not in cnt, (ca, cb)
                cnt[k] = [max(ca[0], cb[0]), max(ca[1], cb[1])]
            out.append(("br", a, b))
            continue
        ws, fn, ctr, by = it
        keep = False
        if ctr is not None:
            st = cnt.setdefault(id(ctr), [0, 0])
            st[0] += by
            if keep_all or st[0] in used.get(id(ctr), ()):
                st[1] += by
                keep = True
            newv.setdefault(id(ctr), {})[st[0]] = st[1]
        out.append((ws, fn, ctr if keep else None, by))
    return out


def _rewrite_waits(items, newv):
    out = []
    for it in items:
        if it and it[0] == "pad":
            out.append(it)
        elif it and it[0] == "br":
            out.append(("br", _rewrite_waits(it[1], newv),
                        _rewrite_waits(it[2], newv)))
        else:
            ws, fn, ctr, by = it
            out.append(([(c, newv[id(c)][v]) for c, v in ws], fn, ctr, by))
    return out


def _prune_incs(P, force_used=()):
    used = {}
    for st in P.st.values():
        _collect_waited(st.items, used)
    for c, v in force_used:
        used.setdefault(id(c), set()).add(v)
    newv = {}
    cnt = {}
    for snm in P.st:
        P.st[snm].items = _number_incs(P.st[snm].items, used, newv, cnt,
                                       keep_all=snm in ("sy", "gp"))
    for snm in P.st:
        P.st[snm].items = _rewrite_waits(P.st[snm].items, newv)
    return newv


def finish(K):
    P = K.P
    fin = (P.ctrs["sy"], P.ctrs["sy"].n)
    pidw_old = list(P.pid_tok_old)
    newv = _prune_incs(P, force_used=[fin] + pidw_old)
    K._final_wait = (P.ctrs["sy"].h, newv[id(P.ctrs["sy"])][fin[1]])
    P.pidw_new = [(c, newv[id(c)][v]) for c, v in pidw_old]
    with K.nc.Block() as block:

        @block.tensor
        def _(eng):
            _emit_engine(K, eng, "t")

        @block.vector
        def _(eng):
            _emit_engine(K, eng, "v")

        @block.scalar
        def _(eng):
            _emit_engine(K, eng, "s")

        @block.gpsimd
        def _(eng):
            _emit_engine(K, eng, "gp")

        @block.sync
        def _(eng):
            _emit_engine(K, eng, "sy")
            eng.wait_ge(*K._final_wait)
    return K.nc


def shard_inputs(inputs):
    x = np.ascontiguousarray(inputs["L0_attn_hidden_states"], dtype=np.float32)
    maps = []
    import ml_dtypes
    tril = np.triu(np.ones((128, 128), np.float32))
    for core in range(8):
        b, h = core // 2, core % 2
        rows = np.concatenate([np.arange((2 * u + h) * 128, (2 * u + h) * 128 + 128)
                               for u in range(8)])
        own = np.ascontiguousarray(x[b, rows])
        fm = np.zeros((128, 8, 4, 128), np.float32)
        for r in range(8):
            for v in range(4):
                tv = 2 * v + h
                if r < tv:
                    fm[:, r, v, :] = 1.0
                elif r == tv:
                    fm[:, r, v, :] = tril
        pm = np.zeros((128, 256), np.uint8)
        pm[:, 0:128] = 1 if h == 0 else 0
        pm[:, 128:256] = 0 if h == 0 else 1
        maps.append({
            "fmask2": np.ascontiguousarray(
                fm.reshape(128, 4096).astype(ml_dtypes.bfloat16)),
            "pmask": pm,
            "x": np.ascontiguousarray(x[b]),
            "xown": own,
            "wq": np.ascontiguousarray(inputs["L0_attn_Wq"], np.float32),
            "wk": np.ascontiguousarray(inputs["L0_attn_Wk"], np.float32),
            "wv": np.ascontiguousarray(inputs["L0_attn_Wv"], np.float32),
            "wo": np.ascontiguousarray(inputs["L0_attn_Wo"], np.float32),
            "wg": np.ascontiguousarray(inputs["L0_attn_Wg"], np.float32),
            "bg": np.full((8, 1), np.float32(np.asarray(inputs["L0_attn_bg"]).reshape(-1)[0])),
            "gatew": np.ascontiguousarray(inputs["L0_ffn_gate_w"], np.float32),
            "gateb": np.ascontiguousarray(inputs["L0_ffn_gate_b"], np.float32).reshape(1, 8),
            "expw": np.ascontiguousarray(inputs["L0_ffn_expert_w"], np.float32),
            "expb": np.ascontiguousarray(inputs["L0_ffn_expert_b"], np.float32),
        })
    return maps


def assemble_out(results):
    out = np.empty((B, N, HID), np.float32)
    for core in range(8):
        b, h = core // 2, core % 2
        r = results[core]["out"]
        for u in range(8):
            out[b, (2 * u + h) * 128:(2 * u + h) * 128 + 128] = r[u * 128:(u + 1) * 128]
    return out


def ph_b2(K):
    P = K.P
    pb = K.pb
    P.barrier()
    for p in range(2):
        if STATIC_PIECE:
            P.op("gp", lambda e, p=p: e.dma_start(
                out=K.xpiece.rearrange("q (c t n) -> q c t n", t=4, n=128),
                in_=bass.AP(K.xfm_h.tensor, 8 * p * 262144,
                            [[2048, 128], [128, 16], [524288, 4], [1, 128]])),
                writes=("xpiece",), dma=True)
        else:
            def piece_load(e, preg, roff, p=p):
                e.reg_mov(roff, preg)
                e.reg_mul(roff, roff, 262144)
                e.reg_add(roff, roff, 8 * p * 262144)
                return e.dma_start(
                    out=K.xpiece.rearrange("q (c t n) -> q c t n", t=4, n=128),
                    in_=bass.AP(K.xfm_h.tensor, roff,
                                [[2048, 128], [128, 16], [524288, 4], [1, 128]]))
            piece_load._needs_regs = True
            P.op("gp", piece_load, writes=("xpiece",), dma=True)
        for fg in range(4):
            slab = K.wqslab[fg % 2]
            sbuf = f"wqslab{fg % 2}"
            P.op("gp", lambda e, slab=slab, fg=fg: e.dma_start(
                out=slab.rearrange("q (c n) -> q c n", n=512),
                in_=K.wq_d.rearrange("(c q) n -> q c n", q=128)[:, :, fg * 512:(fg + 1) * 512]),
                writes=(sbuf,), dma=True)
            for hp in range(4):
                h = fg * 4 + hp
                bank = hp % 2
                for fc in range(16):
                    P.op("t", lambda e, slab=slab, hp=hp, fc=fc, bank=bank: e.matmul(
                        pb[bank][:, :], slab[:, fc * 512 + hp * 128:fc * 512 + (hp + 1) * 128],
                        K.xpiece[:, fc * 512:(fc + 1) * 512],
                        start=fc == 0, stop=fc == 15),
                        reads=(sbuf, "xpiece"), writes=(K.pbn[bank],))
                P.op("s", lambda e, bank=bank: e.copy(K.qse[:, :], pb[bank][:, :]),
                     reads=(K.pbn[bank],), writes=("qse",))
                P.op("sy", lambda e, h=h, p=p: e.dma_start(
                    out=K.qfm_h[:, h * 1024 + p * 512:h * 1024 + (p + 1) * 512],
                    in_=K.qse[:, :]), reads=("qse",))
                P.op("s", lambda e, bank=bank: e.activation(K.qse2[:, :], pb[bank][:, :],
                                                            AF.Exp),
                     reads=(K.pbn[bank],), writes=("qse2",))
                P.op("t", lambda e: e.matmul(pb[2][0:1, :], K.ones128_b[:, :], K.qse2[:, :],
                                             start=True, stop=True),
                     reads=("ones128_b", "qse2"), writes=(K.pbn[2],))
                def _recip(e):
                    with K.nc.allow_low_precision(reason="softmax denom bf16"):
                        return e.reciprocal(K.rde[0:1, :], pb[2][0:1, :])
                P.op("v", _recip, reads=(K.pbn[2],), writes=("rde",))
                P.op("t", lambda e: e.matmul(pb[3][:, :], K.ones1_b[:, :], K.rde[0:1, :],
                                             start=True, stop=True),
                     reads=("ones1_b", "rde"), writes=(K.pbn[3],))
                P.op("v", lambda e, h=h, p=p: e.tensor_tensor(
                    K.qs_fm[:, h * 1024 + p * 512:h * 1024 + (p + 1) * 512],
                    K.qse2[:, :], pb[3][:, :], OP.mult),
                    reads=("qse2", K.pbn[3]), writes=("qs_fm",))


def ph_delta(K):
    P = K.P
    pb = K.pb
    P.barrier()
    P.op("sy", lambda e: e.dma_start(out=K.pmasksb[:, :], in_=K.pmask_d[:, :]),
         writes=("pmasksb",))
    ksl = lambda c, j: K.ks_tm[:, c * 1024 + j * 128:c * 1024 + (j + 1) * 128]

    def inst(c, j):
        si = j % 2
        d = K.dsets[si]
        bA, bB, bC = (0, 2, 4) if si == 0 else (1, 3, 5)
        nA, nB, nC = K.pbn[bA], K.pbn[bB], K.pbn[bC]
        pA, pB, pC = pb[bA], pb[bB], pb[bC]
        tr = K.pb7b[:, si * 512:si * 512 + 128]
        col = slice(c * 8 + j, c * 8 + j + 1)
        cumTc = K.cumT[:, col]
        P.op("t", lambda e, c=c, j=j, tr=tr: e.transpose(
            tr, ksl(c, j), K.id_b[:, :]),
            reads=("ks_tm", "id_b"), writes=(f"pb7b{si}",))
        yield
        P.op("s", lambda e, tr=tr, d=d: e.copy(d.ksfm_c[:, :], tr),
             reads=(f"pb7b{si}",), writes=(d.ksfm_c_n,))
        yield
        P.op("t", lambda e, j=j, pC=pC: e.matmul(
            pC[:, 0:128], K.ones1_f[:, :], K.cumrow[0:1, j * 128:(j + 1) * 128],
            start=True, stop=True),
            reads=("ones1_f", "cumrow"), writes=(nC,))
        yield
        P.op("v", lambda e, cumTc=cumTc, d=d, pC=pC: e.tensor_scalar(
            d.dn1[:, :], pC[:, 0:128], cumTc, 0.0, OP.subtract, OP.max),
            reads=(nC, "cumT"), writes=(d.dn1_n,))
        yield
        P.op("s", lambda e, d=d: e.activation(d.e1[:, :], d.dn1[:, :], AF.Exp,
                                              scale=-1.0),
             reads=(d.dn1_n,), writes=(d.e1_n,))
        yield
        P.op("v", lambda e, cumTc=cumTc, d=d, pC=pC: e.tensor_scalar(
            d.dn1[:, :], pC[:, 0:128], cumTc, 0.0, OP.subtract, OP.min),
            reads=(nC, "cumT", d.e1_n), writes=(d.dn1_n,))
        yield
        P.op("s", lambda e, d=d: e.activation(d.e2[:, :], d.dn1[:, :], AF.Exp),
             reads=(d.dn1_n,), writes=(d.e2_n,))
        yield
        P.op("v", lambda e, d=d: e.tensor_tensor(d.e1m[:, :], d.e1[:, :],
                                                 K.mstr[:, :], OP.mult),
             reads=(d.e1_n, "mstr"), writes=(d.e1m_n,))
        yield
        P.op("v", lambda e, d=d: e.tensor_tensor(d.e2m[:, :], d.e2[:, :],
                                                 K.mustr[:, :], OP.mult),
             reads=(d.e2_n, "mustr"), writes=(d.e2m_n,))
        yield
        P.op("v", lambda e, d=d: e.tensor_tensor(d.ebt[:, :], d.e2[:, :],
                                                 K.muincs[:, :], OP.mult),
             reads=(d.e2_n, "muincs"), writes=(d.ebt_n,))
        yield
        P.op("t", lambda e, d=d, pA=pA: e.matmul(
            pA[:, 0:128], d.ksfm_c[:, :], d.ksfm_c[:, :], start=True, stop=True),
            reads=(d.ksfm_c_n,), writes=(nA,))
        yield
        P.op("v", lambda e, d=d, pA=pA: e.tensor_tensor(
            d.a_sb[:, :], pA[:, 0:128], d.e1m[:, :], OP.mult),
            reads=(nA, d.e1m_n), writes=(d.a_sb_n,))
        yield
        P.op("v", lambda e, d=d, pA=pA: e.tensor_tensor(
            d.at_sb[:, :], pA[:, 0:128], d.e2m[:, :], OP.mult),
            reads=(nA, d.e2m_n), writes=(d.at_sb_n,))
        yield
        P.op("v", lambda e, d=d: e.scalar_tensor_tensor(
            d.z0[:, :], d.at_sb[:, :], -1.0, K.id_b[:, :], OP.mult, OP.add),
            reads=(d.at_sb_n, "id_b"), writes=(d.z0_n,))
        yield
        aps = [(d.ap0, d.ap0_n), (d.ap1, d.ap1_n)]
        apts = [(d.apt0, d.apt0_n), (d.apt1, d.apt1_n)]
        zs = [(d.z0, d.z0_n), (d.z1, d.z1_n)]
        cur, curn = d.a_sb, d.a_sb_n
        curt, curtn = d.at_sb, d.at_sb_n
        for lvl in range(6):
            zi, zo = zs[lvl % 2], zs[(lvl + 1) % 2]
            P.op("t", lambda e, cur=cur, curt=curt, pA=pA: e.matmul(
                pA[:, 0:128], curt[:, :], cur[:, :], start=True, stop=True),
                reads=(curn, curtn), writes=(nA,))
            yield
            if lvl < 5:
                P.op("t", lambda e, cur=cur, curt=curt, pB=pB: e.matmul(
                    pB[:, 0:128], cur[:, :], curt[:, :], start=True, stop=True),
                    reads=(curn, curtn), writes=(nB,))
                yield
            nap, napn = aps[lvl % 2]
            napt, naptn = apts[lvl % 2]
            P.op("s", lambda e, nap=nap, pA=pA: e.copy(nap[:, :], pA[:, 0:128]),
                 reads=(nA,), writes=(napn,))
            yield
            if lvl < 5:
                P.op("s", lambda e, napt=napt, pB=pB: e.copy(napt[:, :], pB[:, 0:128]),
                     reads=(nB,), writes=(naptn,))
                yield
            P.op("t", lambda e, nap=nap, zi=zi, pC=pC: e.matmul(
                pC[:, 0:128], nap[:, :], zi[0][:, :], start=True, stop=False),
                reads=(napn, zi[1]), writes=(nC,))
            yield
            P.op("t", lambda e, zi=zi, pC=pC: e.matmul(
                pC[:, 0:128], K.id_b[:, :], zi[0][:, :], start=False, stop=True),
                reads=("id_b", zi[1]), writes=(nC,))
            yield
            P.op("s", lambda e, zo=zo, pC=pC: e.copy(zo[0][:, :], pC[:, 0:128]),
                 reads=(nC,), writes=(zo[1],))
            yield
            cur, curn = nap, napn
            curt, curtn = napt, naptn
        zf, zfn = zs[0]
        lallc = K.lall[:, col]
        P.op("v", lambda e, c=c, j=j, lallc=lallc, d=d: e.tensor_scalar(
            d.lk_sb[:, :], ksl(c, j), lallc, None, OP.mult),
            reads=("ks_tm", "lall"), writes=(d.lk_sb_n,))
        yield
        P.op("v", lambda e, cumTc=cumTc, c=c, j=j, d=d: e.tensor_scalar(
            d.kdcol[:, :], cumTc, K.ccb[:, c * 8 + j:c * 8 + j + 1], -1.0,
            OP.subtract, OP.mult),
            reads=("cumT", "ccb"), writes=(d.kdcol_n,))
        yield
        P.op("s", lambda e, d=d: e.activation(d.kpcol[:, :], d.kdcol[:, :], AF.Exp),
             reads=(d.kdcol_n,), writes=(d.kpcol_n,))
        yield
        P.op("v", lambda e, c=c, j=j, d=d: e.tensor_scalar(
            d.kp_sb[:, :], ksl(c, j), d.kpcol[:, :], None, OP.mult),
            reads=("ks_tm", d.kpcol_n), writes=(d.kp_sb_n,))
        yield
        P.op("t", lambda e, zf=zf, d=d, pB=pB: e.matmul(
            pB[:, 0:128], d.lk_sb[:, :], zf[:, :], start=True, stop=True),
            reads=(d.lk_sb_n, zfn), writes=(nB,))
        yield
        P.op("s", lambda e, d=d, pB=pB: e.activation(
            d.nmt_sb[:, :], pB[:, 0:128], AF.Copy, scale=-1.0),
            reads=(nB,), writes=(d.nmt_sb_n,))
        yield
        sstate_j = K.sstate[:, j * 128:(j + 1) * 128]
        P.op("t", lambda e, zf=zf, c=c, j=j, pC=pC: e.matmul(
            pC[:, 0:128], zf[:, :],
            K.v_tm[:, c * 1024 + j * 128:c * 1024 + (j + 1) * 128],
            start=True, stop=False),
            reads=(zfn, "v_tm"), writes=(nC,))
        yield
        P.op("t", lambda e, sstate_j=sstate_j, d=d, pC=pC: e.matmul(
            pC[:, 0:128], d.nmt_sb[:, :], sstate_j, start=False, stop=True),
            reads=(d.nmt_sb_n, f"sstate{j}"), writes=(nC,))
        yield
        P.op("v", lambda e, d=d, pC=pC: e.tensor_copy(d.uc_sb[:, :], pC[:, 0:128]),
             reads=(nC,), writes=(d.uc_sb_n,))
        yield
        u = c // 2
        slallc = K.slall[:, col]
        pmsl = K.pmasksb[:, (c % 2) * 128:(c % 2) * 128 + 128]
        for hh in (2 * j, 2 * j + 1):
            qsl = K.qs_fm[:, hh * 1024 + u * 128:hh * 1024 + (u + 1) * 128]
            P.op("t", lambda e, qsl=qsl, d=d, pA=pA: e.matmul(
                pA[:, 0:128], d.ksfm_c[:, :], qsl, start=True, stop=True),
                reads=(d.ksfm_c_n, "qs_fm"), writes=(nA,))
            yield
            P.op("v", lambda e, d=d, pA=pA: e.tensor_tensor(
                d.bt_sb[:, :], pA[:, 0:128], d.ebt[:, :], OP.mult),
                reads=(nA, d.ebt_n), writes=(d.bt_sb_n,))
            yield
            P.op("t", lambda e, qsl=qsl, sstate_j=sstate_j, pB=pB: e.matmul(
                pB[:, 0:128], qsl, sstate_j, start=True, stop=True),
                reads=("qs_fm", f"sstate{j}"), writes=(nB,))
            yield
            P.op("s", lambda e, slallc=slallc, d=d, pB=pB: e.activation(
                d.oa_sb[:, :], pB[:, 0:128], AF.Copy, scale=slallc),
                reads=(nB, "slall"), writes=(d.oa_sb_n,))
            yield
            P.op("t", lambda e, d=d, pC=pC: e.matmul(
                pC[:, 0:128], d.bt_sb[:, :], d.uc_sb[:, :], start=True, stop=True),
                reads=(d.bt_sb_n, d.uc_sb_n), writes=(nC,))
            yield
            P.op("v", lambda e, d=d, pC=pC: e.tensor_tensor(
                d.dtmp[:, :], d.oa_sb[:, :], pC[:, 0:128], OP.add),
                reads=(d.oa_sb_n, nC), writes=(d.dtmp_n,))
            yield
            P.op("v", lambda e, hh=hh, u=u, pmsl=pmsl, d=d: e.copy_predicated(
                K.fla[:, u * 2048 + hh * 128:u * 2048 + (hh + 1) * 128],
                pmsl, d.dtmp[:, :]),
                reads=(d.dtmp_n, "pmasksb"), writes=("fla",))
            yield
        lcbc = K.lcb[:, col]
        P.op("t", lambda e, d=d, pA=pA: e.matmul(
            pA[:, 0:128], d.kp_sb[:, :], d.uc_sb[:, :], start=True, stop=True),
            reads=(d.kp_sb_n, d.uc_sb_n), writes=(nA,))
        yield
        P.op("v", lambda e, sstate_j=sstate_j, lcbc=lcbc, pA=pA: e.scalar_tensor_tensor(
            sstate_j, sstate_j, lcbc, pA[:, 0:128], OP.mult, OP.add),
            reads=("lcb", nA), writes=(f"sstate{j}",))
        yield

    for c in range(16):
        P.op("sy", lambda e, c=c: e.dma_start(
            out=K.cumrow.rearrange("o (j n) -> o j n", n=128),
            in_=K.cum[0:8, c * 128:(c + 1) * 128]),
            reads=("cum",), writes=("cumrow",))
        for jp in range(4):
            gens = [inst(c, 2 * jp), inst(c, 2 * jp + 1)]
            alive = [True, True]
            while any(alive):
                for gi, g in enumerate(gens):
                    if alive[gi]:
                        try:
                            next(g)
                        except StopIteration:
                            alive[gi] = False


def ph_flash(K):
    P = K.P
    pb = K.pb
    P.barrier()
    P.op("sy", lambda e: e.dma_start(out=K.kfm_bk[:, :], in_=K.kfm_h[:, :]),
         writes=("kfm_bk",))
    P.op("sy", lambda e: e.dma_start(out=K.qfm_bk[:, :], in_=K.qfm_h[:, :]),
         writes=("qfm_bk",))
    P.op("sy", lambda e: e.dma_start(out=K.fmask2sb[:, :], in_=K.fmask2_d[:, :]),
         writes=("fmask2sb",))

    def inst(h, p, si):
        j = h // 2
        kext = 8 * p + 8
        qsl = K.qfm_bk[:, h * 1024 + p * 512:h * 1024 + (p + 1) * 512]
        pt = K.pt_all[si]
        ptn = f"pt_all{si}"
        b0 = si
        bpv = 2 + si
        bl = 4 + si
        ptmp = K.pt_tmp[si]
        ptmpn = f"pt_tmp{si}"
        for k in range(kext):
            P.op("t", lambda e, j=j, k=k, qsl=qsl: e.matmul(
                pb[b0][:, :], K.kfm_bk[:, j * 2048 + k * 128:j * 2048 + (k + 1) * 128],
                qsl, start=True, stop=True),
                reads=("kfm_bk", "qfm_bk"), writes=(K.pbn[b0],))
            yield
            r = k - 8 * p
            if r >= 0:
                P.op("s", lambda e, ptmp=ptmp: e.activation(
                    ptmp[:, :], pb[b0][:, :], AF.Exp, scale=SCALE),
                    reads=(K.pbn[b0],), writes=(ptmpn,))
                yield
                P.op("v", lambda e, k=k, r=r, pt=pt, ptmp=ptmp: e.tensor_tensor(
                    pt[:, k * 512:(k + 1) * 512], ptmp[:, :],
                    K.fmask2sb[:, r * 512:(r + 1) * 512], OP.mult),
                    reads=("fmask2sb", ptmpn), writes=(ptn,))
                yield
            else:
                P.op("s", lambda e, k=k, pt=pt: e.activation(
                    pt[:, k * 512:(k + 1) * 512], pb[b0][:, :], AF.Exp, scale=SCALE),
                    reads=(K.pbn[b0],), writes=(ptn,))
                yield
        for k in range(kext):
            P.op("t", lambda e, j=j, k=k, kext=kext, pt=pt: e.matmul(
                pb[bpv][:, :], K.v_tm[:, k * 1024 + j * 128:k * 1024 + (j + 1) * 128],
                pt[:, k * 512:(k + 1) * 512], start=k == 0, stop=k == kext - 1),
                reads=("v_tm", ptn), writes=(K.pbn[bpv],))
            yield
        for k in range(kext):
            P.op("t", lambda e, k=k, kext=kext, pt=pt: e.matmul(
                pb[bl][0:1, :], K.ones128_b[:, :], pt[:, k * 512:(k + 1) * 512],
                start=k == 0, stop=k == kext - 1),
                reads=("ones128_b", ptn), writes=(K.pbn[bl],))
            yield

        def _recipl(e, si=si):
            with K.nc.allow_low_precision(reason="softmax denom bf16"):
                return e.reciprocal(K.rl_sb[si][0:1, :], pb[bl][0:1, :])
        P.op("v", _recipl, reads=(K.pbn[bl],), writes=(f"rl_sb{si}",))
        yield
        P.op("t", lambda e, si=si: e.matmul(pb[6][:, :], K.ones1_b[:, :],
                                            K.rl_sb[si][0:1, :], start=True, stop=True),
             reads=("ones1_b", f"rl_sb{si}"), writes=(K.pbn[6],))
        yield
        P.op("s", lambda e, si=si: e.copy(K.rb_sb[si][:, :], pb[6][:, :]),
             reads=(K.pbn[6],), writes=(f"rb_sb{si}",))
        yield
        P.op("v", lambda e, h=h, p=p, si=si: e.tensor_tensor(
            K.base_fm[:, h * 1024 + p * 512:h * 1024 + (p + 1) * 512],
            pb[bpv][:, :], K.rb_sb[si][:, :], OP.mult),
            reads=(K.pbn[bpv], f"rb_sb{si}"), writes=("base_fm",))
        yield

    for h in range(H):
        pair = [(h, 1), (h, 0)]
        gens = [inst(hh, pp, si) for si, (hh, pp) in enumerate(pair)]
        alive = [True] * len(gens)
        while any(alive):
            for gi, g in enumerate(gens):
                if alive[gi]:
                    try:
                        next(g)
                    except StopIteration:
                        alive[gi] = False


def x1sl(K, tto, c0=0, c1=2048):
    buf = K.x1a if tto < 4 else K.x1b
    nm = "x1a" if tto < 4 else "x1b"
    return buf[:, (tto % 4) * 2048 + c0:(tto % 4) * 2048 + c1], nm


def ph_wo(K):
    P = K.P
    pb = K.pb
    P.barrier()
    for tto in range(8):
        xs, nm = x1sl(K, tto)
        P.op("sy", lambda e, tto=tto, xs=xs: e.dma_start(
            out=xs, in_=K.xown_d[tto * 128:(tto + 1) * 128, :]), writes=(nm,))
    for fg in range(4):
        slab = K.woslab[fg % 2]
        snm = f"woslab{fg % 2}"
        P.op("gp", lambda e, slab=slab, fg=fg: e.dma_start(
            out=slab.rearrange("q (c n) -> q c n", n=512),
            in_=K.wo_d.rearrange("(c q) n -> q c n", q=128)[:, :, fg * 512:(fg + 1) * 512]),
            writes=(snm,), dma=True)
        for tto in range(8):
            bank = tto % 2
            for fc in range(16):
                P.op("t", lambda e, slab=slab, fc=fc, tto=tto, bank=bank: e.matmul(
                    pb[bank][:, :], K.base_fm[:, fc * 1024 + tto * 128:fc * 1024 + (tto + 1) * 128],
                    slab[:, fc * 512:(fc + 1) * 512], start=fc == 0, stop=fc == 15),
                    reads=("base_fm", snm), writes=(K.pbn[bank],))
            xs, nm = x1sl(K, tto, fg * 512, (fg + 1) * 512)
            P.op("v", lambda e, xs=xs, bank=bank: e.scalar_tensor_tensor(
                xs, pb[bank][:, :], 0.99, xs, OP.mult, OP.add),
                reads=(K.pbn[bank],), writes=(nm,))
            P.op("v", lambda e, xs=xs, tto=tto, fg=fg: e.scalar_tensor_tensor(
                xs, K.fla[:, tto * 2048 + fg * 512:tto * 2048 + (fg + 1) * 512],
                0.01, xs, OP.mult, OP.add),
                reads=("fla",), writes=(nm,))


def ph_ffn(K):
    P = K.P
    pb = K.pb
    P.barrier()
    P.op("gp", lambda e: e.dma_start(
        out=K.gatewsb.rearrange("q (c n) -> q c n", n=8),
        in_=K.gw_d.rearrange("(c q) n -> q c n", q=128)), writes=("gatewsb",), dma=True)
    P.op("gp", lambda e: e.dma_start(out=K.gatebsb[:, :], in_=K.gb_d[:, :]),
         writes=("gatebsb",), dma=True)
    for tto in range(8):
        xs, nm = x1sl(K, tto)
        P.op("v", lambda e, xs=xs: e.reduce_sum(K.ln_a[:, :], xs, axis=AX.X),
             reads=(nm,), writes=("ln_a",))
        P.op("s", lambda e: e.activation(K.ln_b[:, :], K.ln_a[:, :], AF.Copy,
                                         scale=1.0 / HID),
             reads=("ln_a",), writes=("ln_b",))
        P.op("v", lambda e, xs=xs: e.tensor_scalar(K.ln2sc[:, :], xs, K.ln_b[:, :],
                                                   None, OP.subtract),
             reads=(nm, "ln_b"), writes=("ln2sc",))
        P.op("s", lambda e: e.activation(K.ln2st[:, :], K.ln2sc[:, :], AF.Square),
             reads=("ln2sc",), writes=("ln2st",))
        P.op("v", lambda e: e.reduce_sum(K.ln_c[:, :], K.ln2st[:, :], axis=AX.X),
             reads=("ln2st",), writes=("ln_c",))
        P.op("s", lambda e: e.activation(K.ln_d[:, :], K.ln_c[:, :], AF.Sqrt,
                                         scale=1.0 / HID, bias=EPS),
             reads=("ln_c", "cpool"), writes=("ln_d",))
        P.op("v", lambda e: e.reciprocal(K.ln_e[:, :], K.ln_d[:, :]),
             reads=("ln_d",), writes=("ln_e",))
        P.op("v", lambda e: e.tensor_scalar(K.ln2sc[:, :], K.ln2sc[:, :], K.ln_e[:, :],
                                            None, OP.mult),
             reads=("ln_e",), writes=("ln2sc",))
        for grp in range(4):
            bank = grp % 2
            for kk in range(4):
                fc = grp * 4 + kk
                P.op("t", lambda e, fc=fc, bank=bank, kk=kk: e.transpose(
                    pb[bank][:, kk * 128:(kk + 1) * 128],
                    K.ln2sc[:, fc * 128:(fc + 1) * 128], K.id_f[:, :]),
                    reads=("ln2sc", "id_f"), writes=(K.pbn[bank],))
            for kk in range(4):
                fc = grp * 4 + kk
                P.op("s", lambda e, fc=fc, kk=kk, bank=bank, tto=tto: e.copy(
                    K.xh2_fm[:, fc * 1024 + tto * 128:fc * 1024 + (tto + 1) * 128],
                    pb[bank][:, kk * 128:(kk + 1) * 128]),
                    reads=(K.pbn[bank],), writes=("xh2_fm",))
        P.op("t", lambda e: e.matmul(pb[5][:, 0:8], K.ones1_b[:, :],
                                     K.gatebsb[0:1, :], start=True, stop=False),
             reads=("ones1_b", "gatebsb"), writes=(K.pbn[5],))
        for fc in range(16):
            P.op("t", lambda e, fc=fc, tto=tto: e.matmul(
                pb[5][:, 0:8], K.xh2_fm[:, fc * 1024 + tto * 128:fc * 1024 + (tto + 1) * 128],
                K.gatewsb[:, fc * 8:(fc + 1) * 8], start=False, stop=fc == 15),
                reads=("xh2_fm", "gatewsb"), writes=(K.pbn[5],))
        P.op("v", lambda e: e.reduce_max(K.gm1[:, :], pb[5][:, 0:8], axis=AX.X),
             reads=(K.pbn[5],), writes=("gm1",))
        P.op("v", lambda e: e.tensor_scalar(K.gt1[:, :], pb[5][:, 0:8], K.gm1[:, :],
                                            None, OP.is_ge),
             reads=(K.pbn[5], "gm1"), writes=("gt1",))
        P.op("v", lambda e: e.scalar_tensor_tensor(K.gt2[:, :], K.gt1[:, :], -1e30,
                                                   pb[5][:, 0:8], OP.mult, OP.add),
             reads=("gt1", K.pbn[5]), writes=("gt2",))
        P.op("v", lambda e: e.reduce_max(K.gm2[:, :], K.gt2[:, :], axis=AX.X),
             reads=("gt2",), writes=("gm2",))
        P.op("v", lambda e, tto=tto: e.tensor_scalar(
            K.wcols[:, tto * 8:(tto + 1) * 8], pb[5][:, 0:8], K.gm2[:, :], 0.5,
            OP.is_ge, OP.mult),
            reads=(K.pbn[5], "gm2"), writes=("wcols",))
    for e_i in range(8):
        brow = K.berow[e_i % 2]
        bnm = f"berow{e_i % 2}"
        P.op("gp", lambda e, brow=brow, e_i=e_i: e.dma_start(
            out=brow[0:1, :], in_=K.eb_d[e_i:e_i + 1, :]), writes=(bnm,), dma=True)
        for fg in range(4):
            si = (e_i * 4 + fg) % 2
            slab = K.weslab[si]
            snm = f"weslab{si}"
            P.op("gp", lambda e, slab=slab, e_i=e_i, fg=fg: e.dma_start(
                out=slab.rearrange("q (c n) -> q c n", n=512),
                in_=K.ew_d.rearrange("e (c q) n -> e q c n", q=128)[
                    e_i, :, :, fg * 512:(fg + 1) * 512]),
                writes=(snm,), dma=True)
            for tto in range(8):
                bank = 2 + tto % 2
                P.op("t", lambda e, brow=brow, fg=fg, bank=bank: e.matmul(
                    pb[bank][:, :], K.ones1_b[:, :],
                    brow[0:1, fg * 512:(fg + 1) * 512], start=True, stop=False),
                    reads=("ones1_b", bnm), writes=(K.pbn[bank],))
                for fc in range(16):
                    P.op("t", lambda e, slab=slab, fc=fc, tto=tto, bank=bank: e.matmul(
                        pb[bank][:, :],
                        K.xh2_fm[:, fc * 1024 + tto * 128:fc * 1024 + (tto + 1) * 128],
                        slab[:, fc * 512:(fc + 1) * 512], start=False, stop=fc == 15),
                        reads=("xh2_fm", snm), writes=(K.pbn[bank],))
                P.op("s", lambda e, bank=bank, tto=tto, e_i=e_i: e.activation(
                    K.ffnu[:, :], pb[bank][:, :], AF.Relu,
                    scale=K.wcols[:, tto * 8 + e_i:tto * 8 + e_i + 1]),
                    reads=(K.pbn[bank], "wcols"), writes=("ffnu",))
                xs, nm = x1sl(K, tto, fg * 512, (fg + 1) * 512)
                P.op("v", lambda e, xs=xs: e.tensor_tensor(xs, xs, K.ffnu[:, :], OP.add),
                     reads=("ffnu",), writes=(nm,))
    for tto in range(8):
        xs, nm = x1sl(K, tto)
        P.op("sy", lambda e, tto=tto, xs=xs: e.dma_start(
            out=K.out_d[tto * 128:(tto + 1) * 128, :], in_=xs), reads=(nm,))


def build_full(debug=(), upto="all"):
    K = make_K(debug)
    ph_consts(K)
    ph_b1(K)
    ph_prologue(K)
    if upto != "b1":
        ph_b2(K)
        if upto != "b2":
            ph_delta(K)
            if upto != "delta":
                ph_flash(K)
                if upto != "flash":
                    ph_wo(K)
                    ph_ffn(K)
    if K.dbg:
        dbg_dump(K, [n for n in K.dbg if not n.endswith("_h")])
    return finish(K)


def kernel(**inputs):
    nc = build_full()
    maps = shard_inputs(inputs)
    res = run_bass_kernel_spmd(nc, maps, core_ids=list(range(8)))
    return assemble_out(res.results)
